# version 04 (final)
# speedup vs baseline: 1.2814x; 1.0368x over previous
import math
from contextlib import ExitStack

import numpy as np
import concourse.bass as bass
import concourse.mybir as mybir
from concourse.bass_utils import run_bass_kernel_spmd

F32 = mybir.dt.float32
BF16 = mybir.dt.bfloat16
I32 = mybir.dt.int32
U32 = mybir.dt.uint32
AF = mybir.ActivationFunctionType
ALU = mybir.AluOpType

NCORE = 8
TPC = 2048
DM = 1024
TT = 512
INV127 = float(np.float32(1.0 / 127.0))
ENGS = ("tensor", "scalar", "vector", "gpsimd", "sync")


def fm(x):
    return np.ascontiguousarray(np.asarray(x).T)


def pp(v):
    v = np.ascontiguousarray(np.asarray(v))
    return np.ascontiguousarray(v.reshape(-1, 128).T)


def shard(a, c, t=TPC):
    return np.ascontiguousarray(a[:, c * t:(c + 1) * t])


class Res:
    __slots__ = ("name", "w", "rs")

    def __init__(self, name=""):
        self.name = name
        self.w = None
        self.rs = []


class Sched:
    def __init__(self, nc):
        self.nc = nc
        self.ops = []

    def op(self, eng, fn, reads=(), writes=(), dma=False, fence=False):
        deps = set()
        for r in reads:
            if r.w is not None:
                deps.add(r.w)
        for r in writes:
            if r.w is not None:
                deps.add(r.w)
            deps.update(r.rs)
        i = len(self.ops)
        self.ops.append([eng, fn, deps, dma, False, 0, fence])
        for r in reads:
            r.rs.append(i)
        for r in writes:
            r.w = i
            r.rs = []
        return i

    def dma(self, eng, fn, reads=(), writes=()):
        return self.op(eng, fn, reads, writes, dma=True)

    def barrier_all(self):
        r = Res("bar")
        return r

    def emit(self, lp):
        ops = self.ops
        for i, o in enumerate(ops):
            if o[3]:
                o[4] = True
            for d in o[2]:
                p = ops[d]
                if p[0] != o[0] or (p[3] and not o[3]) or p[6]:
                    p[4] = True
        sems = {}
        for e in ENGS:
            sems[(e, False)] = lp.sem("S_" + e)
            sems[(e, True)] = lp.sem("S_" + e + "_d")
        seq = {k: 0 for k in sems}
        for o in ops:
            if o[4]:
                k = (o[0], o[3])
                seq[k] += 16 if o[3] else 1
                o[5] = seq[k]
        bodies = {e: [] for e in ENGS}
        for i, o in enumerate(ops):
            bodies[o[0]].append(i)

        def run_body(eng):
            def body(h):
                last_wait = {}
                prev_dma_seq = 0
                for i in bodies[eng]:
                    o = ops[i]
                    needs = {}
                    if o[3]:
                        if prev_dma_seq:
                            needs[(eng, True)] = prev_dma_seq
                        prev_dma_seq = o[5]
                    for d in o[2]:
                        p = ops[d]
                        if p[0] == eng and not p[3] and not p[6]:
                            continue
                        if p[0] == eng and p[3] and o[3]:
                            continue
                        k = (p[0], p[3])
                        if p[5] == 0:
                            raise RuntimeError(f"dep op {d} has no inc")
                        needs[k] = max(needs.get(k, 0), p[5])
                    for k, v in needs.items():
                        if last_wait.get(k, -1) >= v:
                            continue
                        h.wait_ge(sems[k], v)
                        last_wait[k] = v
                    ins = o[1](h)
                    if o[4]:
                        if ins is None:
                            raise RuntimeError(f"op {i} on {eng} returned no inst")
                        ins.then_inc(sems[(eng, o[3])], 16 if o[3] else 1)
                return None
            return body

        with self.nc.Block() as blk:
            for eng in ENGS:
                if bodies[eng]:
                    getattr(blk, eng)(run_body(eng))


class LP:

    def __init__(self):
        self.nc = bass.Bass("TRN2", target_bir_lowering=False)
        self.es = ExitStack()
        self.sems = {}
        self.g = Sched(self.nc)

    def sem(self, name):
        if name not in self.sems:
            self.sems[name] = self.es.enter_context(self.nc.semaphore(name))
        return self.sems[name]

    def sbuf(self, name, shape, dtype):
        t = self.es.enter_context(self.nc.sbuf_tensor(name, list(shape), dtype))
        return t, Res(name)

    def psum(self, name, shape, dtype=F32):
        t = self.es.enter_context(self.nc.psum_tensor(name, list(shape), dtype))
        return t, Res(name)

    def dram_in(self, name, shape, dtype):
        return self.nc.dram_tensor(name, list(shape), dtype, kind="ExternalInput").ap()

    def dram_out(self, name, shape, dtype):
        return self.nc.dram_tensor(name, list(shape), dtype, kind="ExternalOutput").ap(), Res(name)

    def constp(self, value):
        if not hasattr(self, "_consts"):
            self._consts = {}
        if value not in self._consts:
            t, r = self.sbuf(f"cst{len(self._consts)}", [128, 1], F32)
            def fn(h, t=t, value=value):
                return h.memset(t[:, :], float(value))
            self.g.op("vector", fn, writes=(r,))
            self._consts[value] = (t, r)
        return self._consts[value]

    def finish(self):
        self.g.emit(self)
        self.es.close()
        return self.nc


def ld_fm(lp, dst_sb, dst_res, src_ap, nchunk, eng="gpsimd", slow=False):
    def fn(h):
        return h.dma_start(
            out=dst_sb[:, :].rearrange("p (kc t) -> p kc t", kc=nchunk),
            in_=src_ap.rearrange("(kc p) t -> p kc t", p=128),
            allow_slow_non_contiguous=slow)
    return lp.g.dma(eng, fn, writes=(dst_res,))


def ld_plain(lp, dst_sb, dst_res, src_ap, eng="sync", slow=False):
    def fn(h):
        return h.dma_start(out=dst_sb, in_=src_ap,
                           allow_slow_non_contiguous=slow)
    return lp.g.dma(eng, fn, writes=(dst_res,))


def st_fm(lp, dst_ap, dst_res, src_sb, nchunk, eng="gpsimd", reads=()):
    def fn(h):
        return h.dma_start(
            out=dst_ap.rearrange("(kc p) t -> p kc t", p=128),
            in_=src_sb[:, :].rearrange("p (kc t) -> p kc t", kc=nchunk))
    return lp.g.dma(eng, fn, reads=tuple(reads), writes=(dst_res,))


class PsumPool:

    def __init__(self, lp, name, nbank):
        self.tiles = []
        for i in range(nbank):
            t, r = lp.psum(f"{name}{i}", [128, TT])
            self.tiles.append((t, r))
        self.i = 0

    def get(self):
        t, r = self.tiles[self.i % len(self.tiles)]
        self.i += 1
        return t, r


def emit_ln(lp, pool, x_sb, x_res, xn_sb, xn_res, on1, on1_r, onb, onb_r,
            sq_sb, sq_res, st_sb, st_res, nchunk, xcol, Tx, ncol, Txn,
            eps=1e-5, gam=None, gam_r=None, bet=None, bet_r=None):
    D = 128 * nchunk
    g = lp.g
    sqr0, sqr1 = sq_res
    psx, psx_r = pool.get()
    for kc in range(nchunk):
        sl = slice(kc * Tx + xcol, kc * Tx + xcol + TT)
        sqr = sqr0 if kc % 2 == 0 else sqr1
        sq_ap = sq_sb[:, (kc % 2) * TT:(kc % 2) * TT + TT]

        def sqfn(h, sl=sl, sq_ap=sq_ap):
            return h.activation(sq_ap, x_sb[:, sl], AF.Square)
        g.op("scalar", sqfn, reads=(x_res,), writes=(sqr,))

        def mmfn(h, sl=sl, kc=kc, sq_ap=sq_ap, psx=psx):
            h.matmul(psx[0:1, 0:TT], on1[:, :], x_sb[:, sl],
                     start=(kc == 0), stop=(kc == nchunk - 1))
            return h.matmul(psx[32:33, 0:TT], on1[:, :], sq_ap,
                            start=(kc == 0), stop=(kc == nchunk - 1))
        g.op("tensor", mmfn, reads=(x_res, on1_r, sqr), writes=(psx_r,))

    m_ = st_sb[0:1, 0:TT]; e2 = st_sb[0:1, TT:2 * TT]
    r_ = st_sb[0:1, 2 * TT:3 * TT]; nmr = st_sb[0:1, 3 * TT:4 * TT]

    def ev1(h, psx=psx):
        h.activation(m_, psx[0:1, 0:TT], AF.Copy, scale=1.0 / D)
        h.activation(e2, psx[32:33, 0:TT], AF.Copy, scale=1.0 / D)
        return h.activation(r_, m_, AF.Square)
    g.op("scalar", ev1, reads=(psx_r,), writes=(st_res,))

    def v1(h):
        return h.tensor_tensor(e2, e2, r_, op=ALU.subtract)
    g.op("vector", v1, reads=(st_res,), writes=(st_res,))

    epst, epsr = lp.constp(eps)

    def s2(h):
        return h.activation(e2, e2, AF.Sqrt, bias=epst[0:1, :])
    g.op("scalar", s2, reads=(st_res, epsr), writes=(st_res,))

    def v2a(h):
        return h.reciprocal(r_, e2)
    g.op("vector", v2a, reads=(st_res,), writes=(st_res,), fence=True)

    def v2(h):
        h.tensor_tensor(nmr, m_, r_, op=ALU.mult)
        return h.tensor_scalar_mul(nmr, nmr, -1.0)
    g.op("vector", v2, reads=(st_res,), writes=(st_res,))

    psa, psa_r = pool.get()
    psb, psb_r = pool.get()

    def bc(h, psa=psa, psb=psb):
        h.matmul(psa[:, 0:TT], onb[:, :], r_, start=True, stop=True)
        return h.matmul(psb[:, 0:TT], onb[:, :], nmr, start=True, stop=True)
    g.op("tensor", bc, reads=(st_res, onb_r), writes=(psa_r, psb_r))

    for kc in range(nchunk):
        slx = slice(kc * Tx + xcol, kc * Tx + xcol + TT)
        sln = slice(kc * Txn + ncol, kc * Txn + ncol + TT)
        sqr = sqr0 if kc % 2 == 0 else sqr1
        sq_ap = sq_sb[:, (kc % 2) * TT:(kc % 2) * TT + TT]

        def cmb(h, slx=slx, sln=sln, sq_ap=sq_ap, kc=kc, psa=psa, psb=psb):
            h.tensor_tensor(sq_ap, x_sb[:, slx], psa[:, 0:TT], op=ALU.mult)
            if gam is None:
                return h.tensor_tensor(xn_sb[:, sln], sq_ap,
                                       psb[:, 0:TT], op=ALU.add)
            h.tensor_tensor(sq_ap, sq_ap, psb[:, 0:TT], op=ALU.add)
            return h.tensor_scalar(xn_sb[:, sln], sq_ap,
                                   gam[:, kc:kc + 1], bet[:, kc:kc + 1],
                                   op0=ALU.mult, op1=ALU.add)
        rds = [x_res, psa_r, psb_r] + ([gam_r, bet_r] if gam is not None else [])
        g.op("vector", cmb, reads=tuple(rds), writes=(xn_res, sqr))


def lin_fm(lp, x_sb, x_res, w_sb, w_res, out_sb, out_res, pool,
           KC, F, T, Tx=None, x_col0=0, w_all_res=None,
           bias=None, bias_res=None, act=AF.Copy, scale=1.0,
           evac="scalar", out_col0=0, out_stride=None, extra_ev=None,
           acc_res=None):
    if Tx is None:
        Tx = T
    if out_stride is None:
        out_stride = T
    FC = (F + 127) // 128
    nts = T // TT
    for fc in range(FC):
        fw = min(128, F - fc * 128)
        for tt in range(nts):
            ps, psr = pool.get()

            def mmfn(h, fc=fc, tt=tt, fw=fw, ps=ps):
                ins = None
                for kc in range(KC):
                    ins = h.matmul(
                        ps[0:fw, 0:TT],
                        w_sb[:, kc * F + fc * 128: kc * F + fc * 128 + fw],
                        x_sb[:, kc * Tx + x_col0 + tt * TT: kc * Tx + x_col0 + tt * TT + TT],
                        start=(kc == 0), stop=(kc == KC - 1))
                return ins

            lp.g.op("tensor", mmfn, reads=(x_res, w_res), writes=(psr,))

            oc = out_col0 + fc * out_stride + tt * TT
            if extra_ev is not None:
                extra_ev(fc, tt, fw, ps, psr, oc)
            else:
                def evfn(h, fc=fc, fw=fw, ps=ps, oc=oc):
                    b = bias[:, fc:fc + 1] if bias is not None else None
                    kw = {}
                    if b is not None:
                        kw["bias"] = b
                    return h.activation(out_sb[0:fw, oc:oc + TT],
                                        ps[0:fw, 0:TT], act, scale=scale, **kw)
                rds = [psr] + ([bias_res] if bias_res is not None else [])
                wrs = [out_res] if acc_res is None else [out_res, acc_res]
                lp.g.op(evac, evfn, reads=tuple(rds), writes=tuple(wrs))




def build_A1():
    lp = LP()
    g = lp.g
    x = lp.dram_in("x", [DM, TPC], F32)
    wq = lp.dram_in("wq", [DM, DM], F32)
    wk = lp.dram_in("wk", [DM, DM], F32)
    wv = lp.dram_in("wv", [DM, DM], F32)
    bq = lp.dram_in("bq", [128, 8], F32)
    bk = lp.dram_in("bk", [128, 8], F32)
    proj = lp.dram_in("proj", [64, 128], F32)
    bvr = lp.dram_in("bvr", [1, 1024], F32)
    wg = lp.dram_in("wg", [DM, 3072], I32)
    wu = lp.dram_in("wu", [DM, 3072], I32)
    wd = lp.dram_in("wd", [3072, DM], I32)
    q_o, q_or = lp.dram_out("q_o", [DM, TPC], BF16)
    ffn0_o, ffn0_or = lp.dram_out("ffn0_o", [DM, TPC], F32)
    pkv_o, pkv_or = lp.dram_out("pkv_o", [128, 2048], F32)

    xs, xs_r = lp.sbuf("xs", [128, 8 * TT], F32)
    xnb, xnb_r = lp.sbuf("xnb", [128, 8 * TPC], BF16)
    wqb, wqb_r = lp.sbuf("wqb", [128, 8 * DM], BF16)
    wkb, wkb_r = lp.sbuf("wkb", [128, 8 * DM], BF16)
    wvb, wvb_r = lp.sbuf("wvb", [128, 8 * DM], BF16)
    bqp, bqp_r = lp.sbuf("bqp", [128, 8], F32)
    bkp, bkp_r = lp.sbuf("bkp", [128, 8], F32)
    prjf, prjf_r = lp.sbuf("prjf", [64, 128], F32)
    prjb, prjb_r = lp.sbuf("prjb", [128, 256], BF16)
    bvb, bvb_r = lp.sbuf("bvb", [128, 1024], F32)
    bvrs, bvrs_r = lp.sbuf("bvrs", [1, 1024], F32)
    ksb, ksb_r = lp.sbuf("ksb", [128, 8 * TT], BF16)
    ksn, ksn_r = lp.sbuf("ksn", [128, 2048], BF16)
    kcs, kcs_r = lp.sbuf("kcs", [128, 2048], BF16)
    vtk, vtk_r = lp.sbuf("vtk", [128, 1024], BF16)
    it0, it0_r = lp.sbuf("it0", [128, TT], I32)
    it1, it1_r = it0, it0_r
    ds0, ds0_r = lp.sbuf("ds0", [128, TT], F32)
    ds1, ds1_r = ds0, ds0_r
    qst, _ = lp.sbuf("qst", [128, 2 * TT], BF16)
    qst_r = [Res("qst0"), Res("qst1")]
    kvt, kvt_r = lp.sbuf("kvt", [128, 2048], F32)
    sqs, _ = lp.sbuf("sqs", [128, 2 * TT], F32)
    sq_r = (Res("sq0"), Res("sq1"))
    sts, sts_r = lp.sbuf("sts", [1, 4 * TT], F32)
    on1, on1_r = lp.sbuf("on1", [128, 1], F32)
    onb, onb_r = lp.sbuf("onb", [1, 128], F32)
    wrs, wrs_r = lp.sbuf("wrs", [128, 64], F32)
    brs, brs_r = lp.sbuf("brs", [1, 8], F32)
    brb, brb_r = lp.sbuf("brb", [128, 8], F32)
    lgr, lgr_r = lp.sbuf("lgr", [128, 128], F32)
    e8t, e8t_r = lp.sbuf("e8t", [128, 128], F32)
    mx8, mx8_r = lp.sbuf("mx8", [128, 16], F32)
    nm8, nm8_r = lp.sbuf("nm8", [128, 16], F32)
    sm8, sm8_r = lp.sbuf("sm8", [128, 16], F32)
    rc8, rc8_r = lp.sbuf("rc8", [128, 16], F32)
    wmt, wmt_r = lp.sbuf("wmt", [128, 128], F32)
    wgu, _ = lp.sbuf("wgu", [128, 4 * 1024], BF16)
    wgu_r = [Res(f"wgu{i}") for i in range(4)]
    cvi, _ = lp.sbuf("cvi", [128, 2048], I32)
    cvi_r = [Res("cvi0"), Res("cvi1")]
    hsb, hsb_r = lp.sbuf("hsb", [128, 24 * TT], BF16)
    gsc, gsc_r = lp.sbuf("gsc", [128, TT], F32)
    ffs, _ = lp.sbuf("ffs", [128, 2 * TT], F32)
    ffs_r = [Res("ffs0"), Res("ffs1")]

    class Pool:
        def __init__(self):
            self.tiles = [lp.psum(f"pp{i}", [128, TT]) for i in range(4)]
            self.i = 0
        def get(self):
            t = self.tiles[self.i % 4]
            self.i += 1
            return t
    pool = Pool()
    ps_kv, ps_kv_r = lp.psum("ps_kv", [128, 2048])

    def c0(h):
        h.memset(on1[:, :], 1.0)
        return h.memset(onb[:, :], 1.0)
    g.op("vector", c0, writes=(on1_r, onb_r))
    ld_plain(lp, bqp[:, :], bqp_r, bq[:, :])
    ld_plain(lp, bkp[:, :], bkp_r, bk[:, :])
    ld_plain(lp, bvrs[:, :], bvrs_r, bvr[:, :])
    ld_plain(lp, prjf[:, :], prjf_r, proj[:, :])
    ld_fm(lp, wqb, wqb_r, wq, 8)
    ld_fm(lp, wkb, wkb_r, wk, 8)
    ld_fm(lp, wvb, wvb_r, wv, 8)

    def mkprjb(h):
        h.memset(prjb[64:128, 0:128], 0.0)
        h.memset(prjb[0:64, 128:256], 0.0)
        h.tensor_scalar_mul(prjb[0:64, 0:128], prjf[:, :], 1.0 / (16.0 * math.pi))
        return h.tensor_scalar_mul(prjb[64:128, 128:256], prjf[:, :],
                                   1.0 / (16.0 * math.pi))
    g.op("vector", mkprjb, reads=(prjf_r,), writes=(prjb_r,))

    for half in range(2):
        psb0, psb0_r = pool.get()

        def bvmm(h, half=half, psb0=psb0):
            return h.matmul(psb0[:, 0:TT], onb[:, :],
                            bvrs[:, half * TT:(half + 1) * TT],
                            start=True, stop=True)
        g.op("tensor", bvmm, reads=(bvrs_r, onb_r), writes=(psb0_r,))

        def bvcp(h, half=half, psb0=psb0):
            return h.tensor_copy(bvb[:, half * TT:(half + 1) * TT], psb0[:, 0:TT])
        g.op("vector", bvcp, reads=(psb0_r,), writes=(bvb_r,))

    SQ128 = 1.0 / math.sqrt(128.0)
    TWO_PI = 2.0 * math.pi
    pih_t, pih_r = lp.constp(math.pi / 2.0)
    xr = x.rearrange("(kc p) t -> p kc t", p=128)
    q_r3 = q_o.rearrange("(fc p) t -> p fc t", p=128)
    f_r3 = ffn0_o.rearrange("(fc p) t -> p fc t", p=128)

    for t in range(TPC // TT):
        def ldx(h, t=t):
            return h.dma_start(out=xs[:, :].rearrange("p (kc t) -> p kc t", kc=8),
                               in_=xr[:, :, t * TT:(t + 1) * TT])
        g.dma("sync", ldx, writes=(xs_r,))

        emit_ln(lp, pool, xs, xs_r, xnb, xnb_r, on1, on1_r, onb, onb_r,
                sqs, sq_r, sts, sts_r, 8, 0, TT, t * TT, TPC)

        for fc in range(8):
            ps, psr = pool.get()

            def mmq(h, fc=fc, t=t, ps=ps):
                ins = None
                for kc in range(8):
                    ins = h.matmul(ps[:, 0:TT],
                                   wqb[:, kc * DM + fc * 128: kc * DM + (fc + 1) * 128],
                                   xnb[:, kc * TPC + t * TT: kc * TPC + t * TT + TT],
                                   start=(kc == 0), stop=(kc == 7))
                return ins
            g.op("tensor", mmq, reads=(xnb_r, wqb_r), writes=(psr,))
            slot = fc % 2

            def evq(h, fc=fc, ps=ps, slot=slot):
                return h.activation(qst[:, slot * TT:(slot + 1) * TT], ps[:, 0:TT],
                                    AF.Identity, bias=bqp[:, fc:fc + 1])
            g.op("scalar", evq, reads=(psr, bqp_r), writes=(qst_r[slot],))

            def stq(h, fc=fc, t=t, slot=slot):
                return h.dma_start(out=q_r3[:, fc, t * TT:(t + 1) * TT],
                                   in_=qst[:, slot * TT:(slot + 1) * TT])
            g.dma("gpsimd", stq, reads=(qst_r[slot],), writes=(q_or,))

        for dst, dst_r, wsb, wsb_r, bp, bp_r in (
                (ksb, ksb_r, wkb, wkb_r, bkp, bkp_r),):
            for fc in range(8):
                ps, psr = pool.get()

                def mmk(h, fc=fc, t=t, ps=ps, wsb=wsb):
                    ins = None
                    for kc in range(8):
                        ins = h.matmul(ps[:, 0:TT],
                                       wsb[:, kc * DM + fc * 128: kc * DM + (fc + 1) * 128],
                                       xnb[:, kc * TPC + t * TT: kc * TPC + t * TT + TT],
                                       start=(kc == 0), stop=(kc == 7))
                    return ins
                g.op("tensor", mmk, reads=(xnb_r, wsb_r), writes=(psr,))

                def evk(h, fc=fc, ps=ps, dst=dst, bp=bp):
                    return h.activation(dst[:, fc * TT:(fc + 1) * TT], ps[:, 0:TT],
                                        AF.Identity, bias=bp[:, fc:fc + 1])
                g.op("scalar", evk, reads=(psr, bp_r), writes=(dst_r,))

        for s in range(4):
            gs = t * 4 + s
            ptinfo = []
            for half in range(4):
                ps, psr = pool.get()

                def pm(h, half=half, s=s, ps=ps):
                    ins = None
                    for j in range(2):
                        cp = 2 * half + j
                        ins = h.matmul(ps[:, j * 256:(j + 1) * 256],
                                       ksb[:, cp * TT + s * 128: cp * TT + s * 128 + 128],
                                       prjb[:, :], start=True, stop=True)
                    return ins
                g.op("tensor", pm, reads=(ksb_r, prjb_r), writes=(psr,))
                ptinfo.append((ps, psr, half))

            for ps, psr, half in ptinfo:
                def cv1(h, ps=ps):
                    return h.tensor_copy(it0[:, :], ps[:, :])
                g.op("vector", cv1, reads=(psr,), writes=(it0_r,))

                def sb1(h, ps=ps):
                    return h.tensor_tensor(ds0[:, :], ps[:, :], it0[:, :],
                                           op=ALU.subtract)
                g.op("vector", sb1, reads=(psr, it0_r), writes=(ds0_r,))

                def sn1(h, half=half):
                    return h.activation(ksn[:, half * 512:(half + 1) * 512],
                                        ds0[:, :], AF.Sin, scale=TWO_PI)
                g.op("scalar", sn1, reads=(ds0_r,), writes=(ksn_r,))

                def cv2(h, ps=ps):
                    return h.tensor_scalar_add(it1[:, :], ps[:, :], 0.25)
                g.op("vector", cv2, reads=(psr,), writes=(it1_r,))

                def sb2(h, ps=ps):
                    return h.tensor_tensor(ds1[:, :], ps[:, :], it1[:, :],
                                           op=ALU.subtract)
                g.op("vector", sb2, reads=(psr, it1_r), writes=(ds1_r,))

                def sn2(h, half=half):
                    return h.activation(kcs[:, half * 512:(half + 1) * 512],
                                        ds1[:, :], AF.Sin, scale=TWO_PI,
                                        bias=pih_t[:, :])
                g.op("scalar", sn2, reads=(ds1_r, pih_r), writes=(kcs_r,))

            for half in range(2):
                ps2, ps2r = pool.get()

                def vmm(h, half=half, s=s, t=t, ps2=ps2):
                    ins = None
                    for kc in range(8):
                        ins = h.matmul(
                            ps2[:, 0:TT],
                            xnb[:, kc * TPC + t * TT + s * 128:
                                kc * TPC + t * TT + s * 128 + 128],
                            wvb[:, kc * DM + half * TT: kc * DM + (half + 1) * TT],
                            start=(kc == 0), stop=(kc == 7))
                    return ins
                g.op("tensor", vmm, reads=(xnb_r, wvb_r), writes=(ps2r,))

                def vev(h, half=half, ps2=ps2):
                    return h.tensor_tensor(vtk[:, half * 512:(half + 1) * 512],
                                           ps2[:, 0:TT],
                                           bvb[:, half * 512:(half + 1) * 512],
                                           op=ALU.add)
                g.op("vector", vev, reads=(ps2r, bvb_r), writes=(vtk_r,))

            for quad in range(4):
                def kvmm(h, quad=quad, gs=gs):
                    ins = None
                    for hh in range(4):
                        hd = quad * 4 + hh
                        ck = hd * 128
                        cv = hd * 64
                        for trig, src in ((0, kcs), (1, ksn)):
                            si = hd * 2 + trig
                            ins = h.matmul(
                                ps_kv[:, si * 64: si * 64 + 64],
                                src[:, ck:ck + 128],
                                vtk[:, cv:cv + 64],
                                start=(gs == 0 and si % 8 == 0),
                                stop=(gs == 15))
                    return ins
                g.op("tensor", kvmm, reads=(kcs_r, ksn_r, vtk_r), writes=(ps_kv_r,))

    for i in range(4):
        def evkv(h, i=i):
            return h.activation(kvt[:, i * 512:(i + 1) * 512],
                                ps_kv[:, i * 512:(i + 1) * 512],
                                AF.Copy, scale=SQ128)
        g.op("scalar", evkv, reads=(ps_kv_r,), writes=(kvt_r,))

    def stkv(h):
        return h.dma_start(out=pkv_o[:, :], in_=kvt[:, :])
    g.dma("gpsimd", stkv, reads=(kvt_r,), writes=(pkv_or,))

    wgr = wg.rearrange("(kc p) f -> p kc f", p=128)
    wur = wu.rearrange("(kc p) f -> p kc f", p=128)
    wdr = wd.rearrange("(kc p) f -> p kc f", p=128)
    for kc24 in range(24):
        buf, buf_r = ((wqb, wqb_r), (wkb, wkb_r), (wvb, wvb_r))[kc24 // 8]
        slot24 = kc24 % 2

        def ldwd(h, kc24=kc24, slot24=slot24):
            return h.dma_start(out=cvi[:, slot24 * 1024:(slot24 + 1) * 1024],
                               in_=wdr[:, kc24, :])
        g.dma("gpsimd", ldwd, writes=(cvi_r[slot24],))

        def cvwd(h, kc24=kc24, buf=buf, slot24=slot24):
            return h.tensor_copy(buf[:, (kc24 % 8) * DM:(kc24 % 8 + 1) * DM],
                                 cvi[:, slot24 * 1024:(slot24 + 1) * 1024])
        g.op("vector", cvwd, reads=(cvi_r[slot24],), writes=(buf_r,))

    def wdsl(kc, dc):
        buf = (wqb, wkb, wvb)[kc // 8]
        return buf[:, (kc % 8) * DM + dc * 128: (kc % 8) * DM + (dc + 1) * 128]
    wd_rs = (wqb_r, wkb_r, wvb_r)

    for t in range(TPC // TT):
        for fc in range(24):
            sg = fc % 2
            su = 2 + fc % 2

            def ldg(h, t=t, fc=fc):
                return h.dma_start(
                    out=cvi[:, 0:1024].rearrange("p (kc f) -> p kc f", kc=8),
                    in_=wgr[:, :, fc * 128:(fc + 1) * 128])
            g.dma("gpsimd", ldg, writes=(cvi_r[0],))

            def cvg(h, sg=sg):
                return h.tensor_copy(wgu[:, sg * 1024:(sg + 1) * 1024],
                                     cvi[:, 0:1024])
            g.op("vector", cvg, reads=(cvi_r[0],), writes=(wgu_r[sg],))

            def ldu(h, t=t, fc=fc):
                return h.dma_start(
                    out=cvi[:, 1024:2048].rearrange("p (kc f) -> p kc f", kc=8),
                    in_=wur[:, :, fc * 128:(fc + 1) * 128])
            g.dma("gpsimd", ldu, writes=(cvi_r[1],))

            def cvu(h, su=su):
                return h.tensor_copy(wgu[:, su * 1024:(su + 1) * 1024],
                                     cvi[:, 1024:2048])
            g.op("vector", cvu, reads=(cvi_r[1],), writes=(wgu_r[su],))

            psg, psg_r = pool.get()
            psu, psu_r = pool.get()

            def mmg(h, t=t, fc=fc, sg=sg, psg=psg):
                ins = None
                for kc in range(8):
                    ins = h.matmul(psg[:, 0:TT],
                                   wgu[:, sg * 1024 + kc * 128: sg * 1024 + (kc + 1) * 128],
                                   xnb[:, kc * TPC + t * TT: kc * TPC + t * TT + TT],
                                   start=(kc == 0), stop=(kc == 7))
                return ins
            g.op("tensor", mmg, reads=(xnb_r, wgu_r[sg]), writes=(psg_r,))

            def mmu(h, t=t, fc=fc, su=su, psu=psu):
                ins = None
                for kc in range(8):
                    ins = h.matmul(psu[:, 0:TT],
                                   wgu[:, su * 1024 + kc * 128: su * 1024 + (kc + 1) * 128],
                                   xnb[:, kc * TPC + t * TT: kc * TPC + t * TT + TT],
                                   start=(kc == 0), stop=(kc == 7))
                return ins
            g.op("tensor", mmu, reads=(xnb_r, wgu_r[su]), writes=(psu_r,))

            def evg(h, psg=psg):
                return h.activation(gsc[:, :], psg[:, 0:TT], AF.Silu, scale=INV127)
            g.op("scalar", evg, reads=(psg_r,), writes=(gsc_r,))

            def mulgu(h, fc=fc, psu=psu):
                return h.tensor_tensor(hsb[:, fc * TT:(fc + 1) * TT],
                                       gsc[:, :], psu[:, 0:TT], op=ALU.mult)
            g.op("vector", mulgu, reads=(gsc_r, psu_r), writes=(hsb_r,))

        for dc in range(8):
            ps, psr = pool.get()

            def mmd(h, dc=dc, ps=ps):
                ins = None
                for kc in range(24):
                    ins = h.matmul(ps[:, 0:TT], wdsl(kc, dc),
                                   hsb[:, kc * TT:(kc + 1) * TT],
                                   start=(kc == 0), stop=(kc == 23))
                return ins
            g.op("tensor", mmd, reads=(hsb_r,) + wd_rs, writes=(psr,))
            slot = dc % 2

            def evd(h, ps=ps, slot=slot):
                return h.activation(ffs[:, slot * TT:(slot + 1) * TT], ps[:, 0:TT],
                                    AF.Copy, scale=INV127 * INV127)
            g.op("scalar", evd, reads=(psr,), writes=(ffs_r[slot],))

            def std(h, t=t, dc=dc, slot=slot):
                return h.dma_start(out=f_r3[:, dc, t * TT:(t + 1) * TT],
                                   in_=ffs[:, slot * TT:(slot + 1) * TT])
            g.dma("gpsimd", std, reads=(ffs_r[slot],), writes=(ffn0_or,))

    def fin(h):
        h.nop()
        return None
    g.op("vector", fin, reads=(q_or, ffn0_or, pkv_or))
    return lp


def host_A1(inputs, cores, prev=None):
    x_fm = fm(np.asarray(inputs["L0_attn_x"]).reshape(16384, DM))
    base = {
        "wq": np.ascontiguousarray(np.asarray(inputs["L0_attn_Wq"]).T),
        "wk": np.ascontiguousarray(np.asarray(inputs["L0_attn_Wk"]).T),
        "wv": np.ascontiguousarray(np.asarray(inputs["L0_attn_Wv"]).T),
        "bq": pp(inputs["L0_attn_bq"]),
        "bk": pp(inputs["L0_attn_bk"]),
        "bvr": np.ascontiguousarray(np.asarray(inputs["L0_attn_bv"],
                                               np.float32).reshape(1, 1024)),
        "proj": np.ascontiguousarray(np.asarray(inputs["L0_attn_proj_mat"], np.float32)),
        "wg": np.ascontiguousarray(np.asarray(inputs["L0_ffn_w_gate"]).T),
        "wu": np.ascontiguousarray(np.asarray(inputs["L0_ffn_w_up"]).T),
        "wd": np.ascontiguousarray(np.asarray(inputs["L0_ffn_w_down"]).T),
    }
    return [dict(base, x=shard(x_fm, c)) for c in cores]


def build_A2a():
    lp = LP()
    g = lp.g
    x = lp.dram_in("x", [DM, TPC], F32)
    qb = lp.dram_in("qb", [DM, TPC], BF16)
    ffn0 = lp.dram_in("ffn0", [DM, TPC], F32)
    pkv4 = lp.dram_in("pkv4", [512, 2048], F32)
    wo = lp.dram_in("wo", [DM, DM], F32)
    bo = lp.dram_in("bo", [128, 8], F32)
    proj = lp.dram_in("proj", [64, 128], F32)
    x1_o, x1_or = lp.dram_out("x1_o", [DM, TPC], F32)
    xn1_o, xn1_or = lp.dram_out("xn1_o", [DM, TPC], BF16)

    xs, xs_r = lp.sbuf("xs", [128, 8 * TT], F32)
    fs, fs_r = lp.sbuf("fs", [128, 8 * TT], F32)
    qs, qs_r = lp.sbuf("qs", [128, 8 * TT], BF16)
    kvs, kvs_r = lp.sbuf("kvs", [128, 2048], F32)
    kvl, kvl_r = lp.sbuf("kvl", [128, 2 * 2048], F32)
    kvb, kvb_r = lp.sbuf("kvb", [128, 2048], BF16)
    wob, wob_r = lp.sbuf("wob", [128, 8 * DM], BF16)
    bop, bop_r = lp.sbuf("bop", [128, 8], F32)
    prjs, prjs_r = lp.sbuf("prjs", [128, 128], BF16)
    prjf, prjf_r = lp.sbuf("prjf", [64, 128], F32)
    qcs, qcs_r = lp.sbuf("qcs", [128, TT], BF16)
    qsn, qsn_r = lp.sbuf("qsn", [128, TT], BF16)
    it0, it0_r = lp.sbuf("it0", [128, TT], I32)
    ds0, ds0_r = lp.sbuf("ds0", [128, TT], F32)
    osb, osb_r = lp.sbuf("osb", [128, 8 * TT], BF16)
    aos, _ = lp.sbuf("aos", [128, 2 * TT], F32)
    aos_r = [Res("aos0"), Res("aos1")]
    x1s, x1s_r = lp.sbuf("x1s", [128, 8 * TT], F32)
    xn1s, xn1s_r = lp.sbuf("xn1s", [128, 8 * TT], BF16)
    sqs, _ = lp.sbuf("sqs", [128, 2 * TT], F32)
    sq_r = (Res("sq0"), Res("sq1"))
    sts, sts_r = lp.sbuf("sts", [1, 4 * TT], F32)
    on1, on1_r = lp.sbuf("on1", [128, 1], F32)
    onb, onb_r = lp.sbuf("onb", [1, 128], F32)
    wrs, wrs_r = lp.sbuf("wrs", [128, 64], F32)
    brs, brs_r = lp.sbuf("brs", [1, 8], F32)
    brb, brb_r = lp.sbuf("brb", [128, 8], F32)
    lgr, lgr_r = lp.sbuf("lgr", [128, 128], F32)
    e8t, e8t_r = lp.sbuf("e8t", [128, 128], F32)
    mx8, mx8_r = lp.sbuf("mx8", [128, 16], F32)
    nm8, nm8_r = lp.sbuf("nm8", [128, 16], F32)
    sm8, sm8_r = lp.sbuf("sm8", [128, 16], F32)
    rc8, rc8_r = lp.sbuf("rc8", [128, 16], F32)
    wmt, wmt_r = lp.sbuf("wmt", [128, 128], F32)

    class Pool:
        def __init__(self):
            self.tiles = [lp.psum(f"pp{i}", [128, TT]) for i in range(6)]
            self.i = 0
        def get(self):
            t = self.tiles[self.i % 6]
            self.i += 1
            return t
    pool = Pool()

    def c0(h):
        h.memset(on1[:, :], 1.0)
        return h.memset(onb[:, :], 1.0)
    g.op("vector", c0, writes=(on1_r, onb_r))
    ld_plain(lp, bop[:, :], bop_r, bo[:, :])
    ld_plain(lp, prjf[:, :], prjf_r, proj[:, :])
    ld_fm(lp, wob, wob_r, wo, 8)

    def mkprj(h):
        h.tensor_scalar_mul(prjs[0:64, :], prjf[:, :], 1.0 / (16.0 * math.pi))
        return h.tensor_scalar_mul(prjs[64:128, :], prjf[:, :],
                                   1.0 / (16.0 * math.pi))
    g.op("vector", mkprj, reads=(prjf_r,), writes=(prjs_r,))

    for r4 in range(4):
        slot = r4 % 2

        def ldp(h, r4=r4, slot=slot):
            return h.dma_start(out=kvl[:, slot * 2048:(slot + 1) * 2048],
                               in_=pkv4[r4 * 128:(r4 + 1) * 128, :])
        g.dma("sync", ldp, writes=(kvs_r,) if False else (kvl_r,))
        if r4 == 0:
            def cp0(h):
                return h.tensor_copy(kvs[:, :], kvl[:, 0:2048])
            g.op("vector", cp0, reads=(kvl_r,), writes=(kvs_r,))
        else:
            def add0(h, slot=slot):
                return h.tensor_tensor(kvs[:, :], kvs[:, :],
                                       kvl[:, slot * 2048:(slot + 1) * 2048],
                                       op=ALU.add)
            g.op("vector", add0, reads=(kvl_r,), writes=(kvs_r,))

    def mkkvb(h):
        return h.tensor_copy(kvb[:, :], kvs[:, :])
    g.op("vector", mkkvb, reads=(kvs_r,), writes=(kvb_r,))

    SQ128 = 1.0 / math.sqrt(128.0)
    TWO_PI = 2.0 * math.pi
    pih_t, pih_r = lp.constp(math.pi / 2.0)
    xr = x.rearrange("(kc p) t -> p kc t", p=128)
    fr = ffn0.rearrange("(kc p) t -> p kc t", p=128)
    qr = qb.rearrange("(kc p) t -> p kc t", p=128)
    x1r = x1_o.rearrange("(kc p) t -> p kc t", p=128)
    xn1r = xn1_o.rearrange("(kc p) t -> p kc t", p=128)

    for t in range(TPC // TT):
        for buf, br, src in ((xs, xs_r, xr), (fs, fs_r, fr), (qs, qs_r, qr)):
            def ldt(h, buf=buf, src=src, t=t):
                return h.dma_start(
                    out=buf[:, :].rearrange("p (kc t) -> p kc t", kc=8),
                    in_=src[:, :, t * TT:(t + 1) * TT])
            g.dma("sync", ldt, writes=(br,))

        for hp in range(8):
            pso, pso_r = pool.get()
            for par in range(2):
                hd = 2 * hp + par
                psp, psp_r = pool.get()

                def pmm2(h, hp=hp, par=par, psp=psp):
                    rhs = qs[par * 64:par * 64 + 64, hp * TT:(hp + 1) * TT]
                    return h.matmul(psp[:, 0:TT],
                                    prjs[par * 64:par * 64 + 64, :], rhs,
                                    start=True, stop=True)
                g.op("tensor", pmm2, reads=(qs_r, prjs_r), writes=(psp_r,))

                def cv1(h, psp=psp):
                    return h.tensor_copy(it0[:, :], psp[:, 0:TT])
                g.op("vector", cv1, reads=(psp_r,), writes=(it0_r,))

                def sb1(h, psp=psp):
                    return h.tensor_tensor(ds0[:, :], psp[:, 0:TT], it0[:, :],
                                           op=ALU.subtract)
                g.op("vector", sb1, reads=(psp_r, it0_r), writes=(ds0_r,))

                def sn1(h):
                    return h.activation(qsn[:, :], ds0[:, :], AF.Sin, scale=TWO_PI)
                g.op("scalar", sn1, reads=(ds0_r,), writes=(qsn_r,))

                def cv2(h, psp=psp):
                    return h.tensor_scalar_add(it0[:, :], psp[:, 0:TT], 0.25)
                g.op("vector", cv2, reads=(psp_r,), writes=(it0_r,))

                def sb2(h, psp=psp):
                    return h.tensor_tensor(ds0[:, :], psp[:, 0:TT], it0[:, :],
                                           op=ALU.subtract)
                g.op("vector", sb2, reads=(psp_r, it0_r), writes=(ds0_r,))

                def sn2(h):
                    return h.activation(qcs[:, :], ds0[:, :], AF.Sin,
                                        scale=TWO_PI, bias=pih_t[:, :])
                g.op("scalar", sn2, reads=(ds0_r, pih_r), writes=(qcs_r,))

                def omm(h, hd=hd, par=par, pso=pso):
                    h.matmul(pso[par * 64:par * 64 + 64, 0:TT],
                             kvb[:, (hd * 2 + 0) * 64:(hd * 2 + 0) * 64 + 64],
                             qcs[:, :], start=True, stop=False)
                    return h.matmul(pso[par * 64:par * 64 + 64, 0:TT],
                                    kvb[:, (hd * 2 + 1) * 64:(hd * 2 + 1) * 64 + 64],
                                    qsn[:, :], start=False, stop=True)
                g.op("tensor", omm, reads=(kvb_r, qcs_r, qsn_r), writes=(pso_r,))

            def evo(h, hp=hp, pso=pso):
                return h.activation(osb[:, hp * TT:(hp + 1) * TT], pso[:, 0:TT],
                                    AF.Copy, scale=SQ128)
            g.op("scalar", evo, reads=(pso_r,), writes=(osb_r,))

        for fc in range(8):
            ps, psr = pool.get()

            def mmo(h, fc=fc, ps=ps):
                ins = None
                for kc in range(8):
                    ins = h.matmul(ps[:, 0:TT],
                                   wob[:, kc * DM + fc * 128: kc * DM + (fc + 1) * 128],
                                   osb[:, kc * TT:(kc + 1) * TT],
                                   start=(kc == 0), stop=(kc == 7))
                return ins
            g.op("tensor", mmo, reads=(osb_r, wob_r), writes=(psr,))
            slot = fc % 2

            def eva(h, fc=fc, ps=ps, slot=slot):
                return h.activation(aos[:, slot * TT:(slot + 1) * TT], ps[:, 0:TT],
                                    AF.Identity, bias=bop[:, fc:fc + 1])
            g.op("scalar", eva, reads=(psr, bop_r), writes=(aos_r[slot],))

            def addx(h, fc=fc, slot=slot):
                h.tensor_tensor(x1s[:, fc * TT:(fc + 1) * TT],
                                xs[:, fc * TT:(fc + 1) * TT],
                                aos[:, slot * TT:(slot + 1) * TT], op=ALU.add)
                return h.tensor_tensor(x1s[:, fc * TT:(fc + 1) * TT],
                                       x1s[:, fc * TT:(fc + 1) * TT],
                                       fs[:, fc * TT:(fc + 1) * TT], op=ALU.add)
            g.op("vector", addx, reads=(xs_r, fs_r, aos_r[slot]), writes=(x1s_r,))

        def stx1(h, t=t):
            return h.dma_start(out=x1r[:, :, t * TT:(t + 1) * TT],
                               in_=x1s[:, :].rearrange("p (kc t) -> p kc t", kc=8))
        g.dma("gpsimd", stx1, reads=(x1s_r,), writes=(x1_or,))

        emit_ln(lp, pool, x1s, x1s_r, xn1s, xn1s_r, on1, on1_r, onb, onb_r,
                sqs, sq_r, sts, sts_r, 8, 0, TT, 0, TT)

        def stxn(h, t=t):
            return h.dma_start(out=xn1r[:, :, t * TT:(t + 1) * TT],
                               in_=xn1s[:, :].rearrange("p (kc t) -> p kc t", kc=8))
        g.dma("gpsimd", stxn, reads=(xn1s_r,), writes=(xn1_or,))

    def fin(h):
        h.nop()
        return None
    g.op("vector", fin, reads=(x1_or, xn1_or))
    return lp


def host_A2a(inputs, cores, prev):
    base = {
        "wo": np.ascontiguousarray(np.asarray(inputs["L0_attn_Wo"]).T),
        "bo": pp(inputs["L0_attn_bo"]),
        "proj": np.ascontiguousarray(np.asarray(inputs["L0_attn_proj_mat"], np.float32)),
    }
    x_fm = fm(np.asarray(inputs["L0_attn_x"]).reshape(16384, DM))
    maps = []
    for c in cores:
        b = c // 4
        pkv4 = np.concatenate([prev["pkv_o"][b * 4 + r] for r in range(4)], axis=0)
        maps.append(dict(base, x=shard(x_fm, c), qb=prev["q_o"][c],
                         ffn0=prev["ffn0_o"][c], pkv4=np.ascontiguousarray(pkv4)))
    return maps


def build_A2b():
    lp = LP()
    g = lp.g
    xn1 = lp.dram_in("xn1", [DM, TPC], BF16)
    wq1 = lp.dram_in("wq1", [DM, DM], F32)
    wk1 = lp.dram_in("wk1", [DM, DM], F32)
    wv1 = lp.dram_in("wv1", [DM, DM], F32)
    fmqw = lp.dram_in("fmqw", [128, 128], F32)
    fmkw = lp.dram_in("fmkw", [128, 128], F32)
    fmqb = lp.dram_in("fmqb", [128, 1], F32)
    fmkb = lp.dram_in("fmkb", [128, 1], F32)
    maskb = lp.dram_in("maskb", [128, 128], BF16)
    q1_o, q1_or = lp.dram_out("q1_o", [DM, TPC], BF16)
    oi_o, oi_or = lp.dram_out("oi_o", [DM, TPC], F32)
    sl_o, sl_or = lp.dram_out("sl_o", [128, 2048], F32)

    xnb, xnb_r = lp.sbuf("xnb", [128, 8 * TPC], BF16)
    wqb, wqb_r = lp.sbuf("wqb", [128, 8 * DM], BF16)
    wkb, wkb_r = lp.sbuf("wkb", [128, 8 * DM], BF16)
    wvb, wvb_r = lp.sbuf("wvb", [128, 8 * DM], BF16)
    fqw, fqw_r = lp.sbuf("fqw", [128, 128], BF16)
    fkw, fkw_r = lp.sbuf("fkw", [128, 128], BF16)
    fqb, fqb_r = lp.sbuf("fqb", [128, 4], F32)
    msk, msk_r = lp.sbuf("msk", [128, 128], BF16)
    q1t, q1t_r = lp.sbuf("q1t", [128, 8 * TT], BF16)
    k1t, k1t_r = lp.sbuf("k1t", [128, 8 * TT], BF16)
    qpf, qpf_r = lp.sbuf("qpf", [128, 8 * TT], BF16)
    qnf, qnf_r = lp.sbuf("qnf", [128, 8 * TT], BF16)
    kpf, kpf_r = lp.sbuf("kpf", [128, 8 * TT], BF16)
    knf, knf_r = lp.sbuf("knf", [128, 8 * TT], BF16)
    atk, atk_r = lp.sbuf("atk", [128, 128], BF16)
    vtk, vtk_r = lp.sbuf("vtk", [128, 128], BF16)
    kpt, kpt_r = lp.sbuf("kpt", [128, 128], BF16)
    knt, knt_r = lp.sbuf("knt", [128, 128], BF16)
    tmpe, tmpe_r = lp.sbuf("tmpe", [128, 128], F32)
    ebk, ebk_r = lp.sbuf("ebk", [128, 256], BF16)
    ebr, ebr_r = lp.sbuf("ebr", [1, 256], F32)
    fkbb, fkbb_r = lp.sbuf("fkbb", [128, 1], BF16)
    idb, idb_r = lp.sbuf("idb", [128, 128], BF16)
    sf, sf_r = lp.sbuf("sf", [128, 2048], F32)
    sfb, sfb_r = lp.sbuf("sfb", [128, 2048], BF16)
    ost, ost_r = lp.sbuf("ost", [128, 1024], F32)

    class Pool:
        def __init__(self):
            self.tiles = [lp.psum(f"pp{i}", [128, TT]) for i in range(6)]
            self.i = 0
        def get(self):
            t = self.tiles[self.i % 6]
            self.i += 1
            return t
    pool = Pool()

    ld_fm(lp, xnb, xnb_r, xn1, 8, eng="sync")
    ld_fm(lp, wqb, wqb_r, wq1, 8)
    ld_fm(lp, wkb, wkb_r, wk1, 8)
    ld_fm(lp, wvb, wvb_r, wv1, 8)
    ld_plain(lp, fqw[:, :], fqw_r, fmqw, eng="gpsimd")
    ld_plain(lp, fkw[:, :], fkw_r, fmkw, eng="gpsimd")
    ld_plain(lp, msk[:, :], msk_r, maskb, eng="sync")
    fqb_t, fqb_tr = lp.sbuf("fqbt", [128, 2], F32)
    ld_plain(lp, fqb_t[:, 0:1], fqb_tr, fmqb, eng="sync")
    ld_plain(lp, fqb_t[:, 1:2], fqb_tr, fmkb, eng="sync")
    LN16 = math.log(16.0)

    def mkb(h):
        h.tensor_scalar_add(fqb[:, 0:1], fqb_t[:, 0:1], -LN16)
        h.tensor_scalar(fqb[:, 1:2], fqb_t[:, 0:1], -1.0, -LN16,
                        op0=ALU.mult, op1=ALU.add)
        h.tensor_copy(fqb[:, 2:3], fqb_t[:, 1:2])
        return h.tensor_scalar_mul(fqb[:, 3:4], fqb_t[:, 1:2], -1.0)
    g.op("vector", mkb, reads=(fqb_tr,), writes=(fqb_r,))

    idnb = lp.dram_in("idnb", [128, 128], BF16)
    ld_plain(lp, idb[:, :], idb_r, idnb, eng="sync")

    def mkfb(h):
        return h.tensor_copy(fkbb[:, :], fqb_t[:, 1:2])
    g.op("vector", mkfb, reads=(fqb_tr,), writes=(fkbb_r,))
    pse, pse_r = pool.get()

    def rowb(h):
        return h.matmul(pse[0:1, 0:128], fkbb[:, :], idb[:, :],
                        start=True, stop=True)
    g.op("tensor", rowb, reads=(fkbb_r, idb_r), writes=(pse_r,))

    def erow(h):
        h.activation(ebr[0:1, 0:128], pse[0:1, 0:128], AF.Exp)
        return h.activation(ebr[0:1, 128:256], pse[0:1, 0:128], AF.Exp, scale=-1.0)
    g.op("scalar", erow, reads=(pse_r,), writes=(ebr_r,))
    onbb, onbb_r = lp.sbuf("onbb", [1, 128], F32)

    def c0b(h):
        return h.memset(onbb[:, :], 1.0)
    g.op("vector", c0b, writes=(onbb_r,))
    pse2, pse2_r = pool.get()

    def bce(h):
        h.matmul(pse2[:, 0:128], onbb[:, :], ebr[0:1, 0:128], start=True, stop=True)
        return h.matmul(pse2[:, 128:256], onbb[:, :], ebr[0:1, 128:256],
                        start=True, stop=True)
    g.op("tensor", bce, reads=(ebr_r, onbb_r), writes=(pse2_r,))

    def cbe(h):
        return h.tensor_copy(ebk[:, :], pse2[:, 0:256])
    g.op("vector", cbe, reads=(pse2_r,), writes=(ebk_r,))

    def z0(h):
        h.memset(sf[:, :], 0.0)
        return h.memset(sfb[:, :], 0.0)
    g.op("vector", z0, writes=(sf_r, sfb_r))

    q1r = q1_o.rearrange("(kc p) t -> p kc t", p=128)
    oir = oi_o.rearrange("(kc p) t -> p kc t", p=128)

    for t in range(TPC // TT):
        for dst, dst_r, wsb, wsb_r in ((q1t, q1t_r, wqb, wqb_r),
                                       (k1t, k1t_r, wkb, wkb_r)):
            for fc in range(8):
                ps, psr = pool.get()

                def mmp(h, fc=fc, t=t, ps=ps, wsb=wsb):
                    ins = None
                    for kc in range(8):
                        ins = h.matmul(ps[:, 0:TT],
                                       wsb[:, kc * DM + fc * 128: kc * DM + (fc + 1) * 128],
                                       xnb[:, kc * TPC + t * TT: kc * TPC + t * TT + TT],
                                       start=(kc == 0), stop=(kc == 7))
                    return ins
                g.op("tensor", mmp, reads=(xnb_r, wsb_r), writes=(psr,))

                def evp(h, fc=fc, ps=ps, dst=dst):
                    return h.activation(dst[:, fc * TT:(fc + 1) * TT], ps[:, 0:TT],
                                        AF.Copy)
                g.op("scalar", evp, reads=(psr,), writes=(dst_r,))

        def stq1(h, t=t):
            return h.dma_start(out=q1r[:, :, t * TT:(t + 1) * TT],
                               in_=q1t[:, :].rearrange("p (kc t) -> p kc t", kc=8))
        g.dma("gpsimd", stq1, reads=(q1t_r,), writes=(q1_or,))

        for hd in range(8):
            for src_t, src_r, w_sb, w_r, bp_c, bn_c, dp, dp_r, dn, dn_r in (
                    (q1t, q1t_r, fqw, fqw_r, 0, 1, qpf, qpf_r, qnf, qnf_r),
                    (k1t, k1t_r, fkw, fkw_r, 2, 3, kpf, kpf_r, knf, knf_r)):
                ps, psr = pool.get()

                def mmy(h, hd=hd, ps=ps, w_sb=w_sb, src_t=src_t):
                    return h.matmul(ps[:, 0:TT], w_sb[:, :],
                                    src_t[:, hd * TT:(hd + 1) * TT],
                                    start=True, stop=True)
                g.op("tensor", mmy, reads=(src_r, w_r), writes=(psr,))

                def evyp(h, hd=hd, ps=ps, dp=dp, bp_c=bp_c):
                    return h.activation(dp[:, hd * TT:(hd + 1) * TT], ps[:, 0:TT],
                                        AF.Exp, bias=fqb[:, bp_c:bp_c + 1])
                g.op("scalar", evyp, reads=(psr, fqb_r), writes=(dp_r,))

                def evyn(h, hd=hd, ps=ps, dn=dn, bn_c=bn_c):
                    return h.activation(dn[:, hd * TT:(hd + 1) * TT], ps[:, 0:TT],
                                        AF.Exp, scale=-1.0,
                                        bias=fqb[:, bn_c:bn_c + 1])
                g.op("scalar", evyn, reads=(psr, fqb_r), writes=(dn_r,))

        for s in range(4):
            c0 = s * 128
            for hd in range(8):
                psA, psA_r = pool.get()

                def mmA(h, hd=hd, c0=c0, psA=psA):
                    h.matmul(psA[:, 0:128], kpf[:, hd * TT + c0: hd * TT + c0 + 128],
                             qpf[:, hd * TT + c0: hd * TT + c0 + 128],
                             start=True, stop=False)
                    return h.matmul(psA[:, 0:128],
                                    knf[:, hd * TT + c0: hd * TT + c0 + 128],
                                    qnf[:, hd * TT + c0: hd * TT + c0 + 128],
                                    start=False, stop=True)
                g.op("tensor", mmA, reads=(kpf_r, knf_r, qpf_r, qnf_r), writes=(psA_r,))

                def mskA(h, psA=psA):
                    return h.tensor_tensor(atk[:, :], psA[:, 0:128], msk[:, :],
                                           op=ALU.mult)
                g.op("vector", mskA, reads=(psA_r, msk_r), writes=(atk_r,))

                psv, psv_r = pool.get()

                def vmm(h, hd=hd, c0=c0, t=t, psv=psv):
                    ins = None
                    for kc in range(8):
                        ins = h.matmul(
                            psv[:, 0:128],
                            xnb[:, kc * TPC + t * TT + c0: kc * TPC + t * TT + c0 + 128],
                            wvb[:, kc * DM + hd * 128: kc * DM + (hd + 1) * 128],
                            start=(kc == 0), stop=(kc == 7))
                    return ins
                g.op("tensor", vmm, reads=(xnb_r, wvb_r), writes=(psv_r,))

                def vev(h, psv=psv):
                    return h.tensor_copy(vtk[:, :], psv[:, 0:128])
                g.op("vector", vev, reads=(psv_r,), writes=(vtk_r,))

                psy2, psy2_r = pool.get()

                def ymm(h, hd=hd, c0=c0, psy2=psy2):
                    return h.matmul(psy2[:, 0:128],
                                    k1t[:, hd * TT + c0: hd * TT + c0 + 128],
                                    fkw[:, :], start=True, stop=True)
                g.op("tensor", ymm, reads=(k1t_r, fkw_r), writes=(psy2_r,))

                def ekp(h, psy2=psy2):
                    return h.activation(tmpe[:, :], psy2[:, 0:128], AF.Exp)
                g.op("scalar", ekp, reads=(psy2_r,), writes=(tmpe_r,))

                def mkp(h):
                    return h.tensor_tensor(kpt[:, :], tmpe[:, :], ebk[:, 0:128],
                                           op=ALU.mult)
                g.op("vector", mkp, reads=(tmpe_r, ebk_r), writes=(kpt_r,))

                def ekn(h, psy2=psy2):
                    return h.activation(tmpe[:, :], psy2[:, 0:128], AF.Exp,
                                        scale=-1.0)
                g.op("scalar", ekn, reads=(psy2_r,), writes=(tmpe_r,))

                def mkn(h):
                    return h.tensor_tensor(knt[:, :], tmpe[:, :], ebk[:, 128:256],
                                           op=ALU.mult)
                g.op("vector", mkn, reads=(tmpe_r, ebk_r), writes=(knt_r,))

                psO, psO_r = pool.get()

                def mmO(h, hd=hd, c0=c0, psO=psO):
                    h.matmul(psO[:, 0:128], vtk[:, :], atk[:, :],
                             start=True, stop=False)
                    h.matmul(psO[:, 0:128], sfb[:, hd * 256: hd * 256 + 128],
                             qpf[:, hd * TT + c0: hd * TT + c0 + 128],
                             start=False, stop=False)
                    return h.matmul(psO[:, 0:128],
                                    sfb[:, hd * 256 + 128: hd * 256 + 256],
                                    qnf[:, hd * TT + c0: hd * TT + c0 + 128],
                                    start=False, stop=True)
                g.op("tensor", mmO, reads=(vtk_r, atk_r, sfb_r, qpf_r, qnf_r),
                     writes=(psO_r,))

                def evO(h, hd=hd, psO=psO):
                    return h.activation(ost[:, hd * 128:(hd + 1) * 128],
                                        psO[:, 0:128], AF.Copy)
                g.op("scalar", evO, reads=(psO_r,), writes=(ost_r,))

                psD, psD_r = pool.get()

                def mmD(h, hd=hd, c0=c0, psD=psD):
                    h.matmul(psD[:, 0:128], kpt[:, :], vtk[:, :],
                             start=True, stop=True)
                    return h.matmul(psD[:, 128:256], knt[:, :], vtk[:, :],
                                    start=False, stop=True)
                g.op("tensor", mmD, reads=(kpt_r, knt_r, vtk_r), writes=(psD_r,))

                def upS(h, hd=hd, psD=psD):
                    h.tensor_tensor(sf[:, hd * 256:(hd + 1) * 256],
                                    sf[:, hd * 256:(hd + 1) * 256],
                                    psD[:, 0:256], op=ALU.add)
                    return h.tensor_copy(sfb[:, hd * 256:(hd + 1) * 256],
                                         sf[:, hd * 256:(hd + 1) * 256])
                g.op("vector", upS, reads=(psD_r,), writes=(sf_r, sfb_r))

            def sto(h, t=t, s=s):
                return h.dma_start(
                    out=oir[:, :, t * TT + s * 128: t * TT + s * 128 + 128],
                    in_=ost[:, :].rearrange("p (kc t) -> p kc t", kc=8))
            g.dma("gpsimd", sto, reads=(ost_r,), writes=(oi_or,))

    def stS(h):
        return h.dma_start(out=sl_o[:, :], in_=sf[:, :])
    g.dma("gpsimd", stS, reads=(sf_r,), writes=(sl_or,))

    def fin(h):
        h.nop()
        return None
    g.op("vector", fin, reads=(q1_or, oi_or, sl_or))
    return lp


def host_A2b(inputs, cores, prev):
    idn = np.eye(128)
    tril_T = (np.arange(128)[:, None] <= np.arange(128)[None, :])
    import ml_dtypes
    base = {
        "wq1": np.ascontiguousarray(np.asarray(inputs["L1_attn_Wq"]).T),
        "wk1": np.ascontiguousarray(np.asarray(inputs["L1_attn_Wk"]).T),
        "wv1": np.ascontiguousarray(np.asarray(inputs["L1_attn_Wv"]).T),
        "fmqw": np.ascontiguousarray(np.asarray(inputs["L1_attn_fmq_w"]).T),
        "fmkw": np.ascontiguousarray(np.asarray(inputs["L1_attn_fmk_w"]).T),
        "fmqb": np.ascontiguousarray(np.asarray(inputs["L1_attn_fmq_b"],
                                                np.float32).reshape(128, 1)),
        "fmkb": np.ascontiguousarray(np.asarray(inputs["L1_attn_fmk_b"],
                                                np.float32).reshape(128, 1)),
        "maskb": tril_T.astype(ml_dtypes.bfloat16),
        "idnb": idn.astype(ml_dtypes.bfloat16),
    }
    return [dict(base, xn1=prev["xn1_o"][c]) for c in cores]


def build_A3():
    lp = LP()
    g = lp.g
    q1 = lp.dram_in("q1", [DM, TPC], BF16)
    oi = lp.dram_in("oi", [DM, TPC], F32)
    x1 = lp.dram_in("x1", [DM, TPC], F32)
    sl4 = lp.dram_in("sl4", [512, 2048], F32)
    mskp = lp.dram_in("mskp", [128, 4], F32)
    rmsw = lp.dram_in("rmsw", [128, 1], F32)
    wo1 = lp.dram_in("wo1", [DM, DM], F32)
    fmqw = lp.dram_in("fmqw", [128, 128], F32)
    fmqb = lp.dram_in("fmqb", [128, 1], F32)
    x2a_o, x2a_or = lp.dram_out("x2a_o", [DM, TPC], F32)

    q1t, q1t_r = lp.sbuf("q1t", [128, 8 * TT], BF16)
    oit, oit_r = lp.sbuf("oit", [128, 8 * TT], F32)
    x1t, x1t_r = lp.sbuf("x1t", [128, 8 * TT], F32)
    slt, slt_r = lp.sbuf("slt", [128, 2 * 2048], F32)
    spf, spf_r = lp.sbuf("spf", [128, 2048], F32)
    spb, spb_r = lp.sbuf("spb", [128, 2048], BF16)
    mkt, mkt_r = lp.sbuf("mkt", [128, 4], F32)
    rwt, rwt_r = lp.sbuf("rwt", [128, 1], F32)
    wob, wob_r = lp.sbuf("wob", [128, 8 * DM], BF16)
    fqw, fqw_r = lp.sbuf("fqw", [128, 128], BF16)
    fqb, fqb_r = lp.sbuf("fqb", [128, 2], F32)
    fqb_t, fqb_tr = lp.sbuf("fqbt", [128, 1], F32)
    qps, qps_r = lp.sbuf("qps", [128, TT], BF16)
    qns, qns_r = lp.sbuf("qns", [128, TT], BF16)
    onf, onf_r = lp.sbuf("onf", [128, 8 * TT], BF16)
    sqt, sqt_r = lp.sbuf("sqt", [128, TT], F32)
    stt_, stt_r = lp.sbuf("stt", [1, 2 * TT], F32)
    on1, on1_r = lp.sbuf("on1", [128, 1], F32)
    onb, onb_r = lp.sbuf("onb", [1, 128], F32)
    wrs, wrs_r = lp.sbuf("wrs", [128, 64], F32)
    brs, brs_r = lp.sbuf("brs", [1, 8], F32)
    brb, brb_r = lp.sbuf("brb", [128, 8], F32)
    lgr, lgr_r = lp.sbuf("lgr", [128, 128], F32)
    e8t, e8t_r = lp.sbuf("e8t", [128, 128], F32)
    mx8, mx8_r = lp.sbuf("mx8", [128, 16], F32)
    nm8, nm8_r = lp.sbuf("nm8", [128, 16], F32)
    sm8, sm8_r = lp.sbuf("sm8", [128, 16], F32)
    rc8, rc8_r = lp.sbuf("rc8", [128, 16], F32)
    wmt, wmt_r = lp.sbuf("wmt", [128, 128], F32)
    aos, _ = lp.sbuf("aos", [128, 2 * TT], F32)
    aos_r = [Res("aos0"), Res("aos1")]
    x2s, x2s_r = lp.sbuf("x2s", [128, 8 * TT], F32)
    eps_t, eps_r = lp.constp(1e-5)

    class Pool:
        def __init__(self):
            self.tiles = [lp.psum(f"pp{i}", [128, TT]) for i in range(6)]
            self.i = 0
        def get(self):
            t = self.tiles[self.i % 6]
            self.i += 1
            return t
    pool = Pool()

    def c0(h):
        h.memset(on1[:, :], 1.0)
        return h.memset(onb[:, :], 1.0)
    g.op("vector", c0, writes=(on1_r, onb_r))
    ld_plain(lp, mkt[:, :], mkt_r, mskp, eng="sync")
    ld_plain(lp, rwt[:, :], rwt_r, rmsw, eng="sync")
    ld_plain(lp, fqw[:, :], fqw_r, fmqw, eng="gpsimd")
    ld_plain(lp, fqb_t[:, :], fqb_tr, fmqb, eng="sync")
    ld_fm(lp, wob, wob_r, wo1, 8)
    LN16 = math.log(16.0)

    def mkb(h):
        h.tensor_scalar_add(fqb[:, 0:1], fqb_t[:, 0:1], -LN16)
        return h.tensor_scalar(fqb[:, 1:2], fqb_t[:, 0:1], -1.0, -LN16,
                               op0=ALU.mult, op1=ALU.add)
    g.op("vector", mkb, reads=(fqb_tr,), writes=(fqb_r,))

    def zs(h):
        return h.memset(spf[:, :], 0.0)
    g.op("vector", zs, writes=(spf_r,))
    for r4 in range(4):
        slot = r4 % 2

        def ldp(h, r4=r4, slot=slot):
            return h.dma_start(out=slt[:, slot * 2048:(slot + 1) * 2048],
                               in_=sl4[r4 * 128:(r4 + 1) * 128, :])
        g.dma("sync", ldp, writes=(slt_r,))

        def acc(h, r4=r4, slot=slot):
            return h.scalar_tensor_tensor(spf[:, :],
                                          slt[:, slot * 2048:(slot + 1) * 2048],
                                          mkt[:, r4:r4 + 1], spf[:, :],
                                          op0=ALU.mult, op1=ALU.add)
        g.op("vector", acc, reads=(slt_r, mkt_r), writes=(spf_r,))

    def cvS(h):
        return h.tensor_copy(spb[:, :], spf[:, :])
    g.op("vector", cvS, reads=(spf_r,), writes=(spb_r,))

    q1r = q1.rearrange("(kc p) t -> p kc t", p=128)
    oir = oi.rearrange("(kc p) t -> p kc t", p=128)
    x1r = x1.rearrange("(kc p) t -> p kc t", p=128)
    x2r = x2a_o.rearrange("(kc p) t -> p kc t", p=128)

    for t in range(TPC // TT):
        for buf, br, src in ((q1t, q1t_r, q1r), (oit, oit_r, oir), (x1t, x1t_r, x1r)):
            def ldt(h, buf=buf, src=src, t=t):
                return h.dma_start(
                    out=buf[:, :].rearrange("p (kc t) -> p kc t", kc=8),
                    in_=src[:, :, t * TT:(t + 1) * TT])
            g.dma("sync", ldt, writes=(br,))

        for hd in range(8):
            psy, psy_r = pool.get()

            def mmy(h, hd=hd, psy=psy):
                return h.matmul(psy[:, 0:TT], fqw[:, :],
                                q1t[:, hd * TT:(hd + 1) * TT], start=True, stop=True)
            g.op("tensor", mmy, reads=(q1t_r, fqw_r), writes=(psy_r,))

            def evp(h, psy=psy):
                return h.activation(qps[:, :], psy[:, 0:TT], AF.Exp,
                                    bias=fqb[:, 0:1])
            g.op("scalar", evp, reads=(psy_r, fqb_r), writes=(qps_r,))

            def evn(h, psy=psy):
                return h.activation(qns[:, :], psy[:, 0:TT], AF.Exp, scale=-1.0,
                                    bias=fqb[:, 1:2])
            g.op("scalar", evn, reads=(psy_r, fqb_r), writes=(qns_r,))

            psc, psc_r = pool.get()

            def mmc(h, hd=hd, psc=psc):
                h.matmul(psc[:, 0:TT], spb[:, hd * 256: hd * 256 + 128],
                         qps[:, :], start=True, stop=False)
                return h.matmul(psc[:, 0:TT], spb[:, hd * 256 + 128: hd * 256 + 256],
                                qns[:, :], start=False, stop=True)
            g.op("tensor", mmc, reads=(spb_r, qps_r, qns_r), writes=(psc_r,))

            def addo(h, hd=hd, psc=psc):
                return h.tensor_tensor(oit[:, hd * TT:(hd + 1) * TT],
                                       oit[:, hd * TT:(hd + 1) * TT],
                                       psc[:, 0:TT], op=ALU.add)
            g.op("vector", addo, reads=(psc_r,), writes=(oit_r,))

            def sq(h, hd=hd):
                return h.activation(sqt[:, :], oit[:, hd * TT:(hd + 1) * TT],
                                    AF.Square)
            g.op("scalar", sq, reads=(oit_r,), writes=(sqt_r,))
            pss, pss_r = pool.get()

            def mms(h, pss=pss):
                return h.matmul(pss[0:1, 0:TT], on1[:, :], sqt[:, :],
                                start=True, stop=True)
            g.op("tensor", mms, reads=(sqt_r, on1_r), writes=(pss_r,))

            def evs(h, pss=pss):
                return h.activation(stt_[0:1, 0:TT], pss[0:1, 0:TT], AF.Sqrt,
                                    scale=1.0 / 128.0, bias=eps_t[0:1, :])
            g.op("scalar", evs, reads=(pss_r, eps_r), writes=(stt_r,))

            def rcp(h):
                return h.reciprocal(stt_[0:1, TT:2 * TT], stt_[0:1, 0:TT])
            g.op("vector", rcp, reads=(stt_r,), writes=(stt_r,), fence=True)

            psb, psb_r = pool.get()

            def bc(h, psb=psb):
                return h.matmul(psb[:, 0:TT], onb[:, :], stt_[0:1, TT:2 * TT],
                                start=True, stop=True)
            g.op("tensor", bc, reads=(stt_r, onb_r), writes=(psb_r,))

            def nrm(h, hd=hd, psb=psb):
                h.tensor_tensor(sqt[:, :], oit[:, hd * TT:(hd + 1) * TT],
                                psb[:, 0:TT], op=ALU.mult)
                return h.tensor_scalar(onf[:, hd * TT:(hd + 1) * TT], sqt[:, :],
                                       rwt[:, 0:1], None, op0=ALU.mult)
            g.op("vector", nrm, reads=(oit_r, psb_r, rwt_r), writes=(onf_r, sqt_r))

        for fc in range(8):
            ps, psr = pool.get()

            def mmo(h, fc=fc, ps=ps):
                ins = None
                for kc in range(8):
                    ins = h.matmul(ps[:, 0:TT],
                                   wob[:, kc * DM + fc * 128: kc * DM + (fc + 1) * 128],
                                   onf[:, kc * TT:(kc + 1) * TT],
                                   start=(kc == 0), stop=(kc == 7))
                return ins
            g.op("tensor", mmo, reads=(onf_r, wob_r), writes=(psr,))

            def addx(h, fc=fc, ps=ps):
                return h.tensor_tensor(x2s[:, fc * TT:(fc + 1) * TT],
                                       x1t[:, fc * TT:(fc + 1) * TT],
                                       ps[:, 0:TT], op=ALU.add)
            g.op("vector", addx, reads=(x1t_r, psr), writes=(x2s_r,))

        def stx2(h, t=t):
            return h.dma_start(out=x2r[:, :, t * TT:(t + 1) * TT],
                               in_=x2s[:, :].rearrange("p (kc t) -> p kc t", kc=8))
        g.dma("gpsimd", stx2, reads=(x2s_r,), writes=(x2a_or,))

    def fin(h):
        h.nop()
        return None
    g.op("vector", fin, reads=(x2a_or,))
    return lp


def host_A3(inputs, cores, prev):
    base = {
        "rmsw": np.ascontiguousarray(np.asarray(inputs["L1_attn_rms_w"],
                                                np.float32).reshape(128, 1)),
        "wo1": np.ascontiguousarray(np.asarray(inputs["L1_attn_Wo"]).T),
        "fmqw": np.ascontiguousarray(np.asarray(inputs["L1_attn_fmq_w"]).T),
        "fmqb": np.ascontiguousarray(np.asarray(inputs["L1_attn_fmq_b"],
                                                np.float32).reshape(128, 1)),
    }
    maps = []
    for c in cores:
        b, qt = c // 4, c % 4
        sl4 = np.concatenate([prev["sl_o"][b * 4 + r] for r in range(4)], axis=0)
        msk = np.zeros((128, 4), np.float32)
        msk[:, :qt] = 1.0
        maps.append(dict(base, q1=prev["q1_o"][c], oi=prev["oi_o"][c],
                         x1=prev["x1_o"][c], sl4=np.ascontiguousarray(sl4),
                         mskp=msk))
    return maps


TIN = 2432
HALO = 384


def build_AFa():
    lp = LP()
    g = lp.g
    yin = lp.dram_in("yin", [DM, TIN], F32)
    wqkv = lp.dram_in("wqkv", [DM, 3072], F32)
    wo = lp.dram_in("wo", [DM, DM], F32)
    qs_i = lp.dram_in("qs", [128, 1], F32)
    ks_i = lp.dram_in("ks", [128, 1], F32)
    csq = lp.dram_in("csq", [128, TIN], BF16)
    ssq = lp.dram_in("ssq", [128, TIN], BF16)
    csk = lp.dram_in("csk", [128, TIN], BF16)
    ssk = lp.dram_in("ssk", [128, TIN], BF16)
    e64_i = lp.dram_in("e64i", [128, 128], BF16)
    b64_i = lp.dram_in("b64i", [16, 1024], BF16)
    r64_i = lp.dram_in("r64i", [128, 128], BF16)
    band_i = lp.dram_in("bandi", [128, 384], F32)
    kvrow = lp.dram_in("kvrow", [1, TIN], F32)
    idf_i = lp.dram_in("idfi", [128, 128], BF16)
    bandT_i = lp.dram_in("bandTi", [128, 384], F32)
    kvcol_i = lp.dram_in("kvcoli", [128, 18], F32)
    x2_o, x2_or = lp.dram_out("x2_o", [DM, TPC], F32)

    xsb, xsb_r = lp.sbuf("xsb", [128, 8 * TIN], BF16)
    wkv, _ = lp.sbuf("wkv", [128, 8 * 2048], BF16)
    wkv_r = [Res("wkvk"), Res("wkvv")]
    krot, krot_r = lp.sbuf("krot", [128, 8 * 2304], BF16)
    vtok, vtok_r = lp.sbuf("vtok", [128, 18 * 1024], BF16)
    ctab, _ = lp.sbuf("ctab", [128, 2 * TIN], BF16)
    ctab_r = [Res("ctabc"), Res("ctabs")]
    e64, e64_r = lp.sbuf("e64", [128, 128], BF16)
    b64, b64_r = lp.sbuf("b64", [16, 1024], BF16)
    r64, r64_r = lp.sbuf("r64", [128, 128], BF16)
    band, band_r = lp.sbuf("band", [128, 384], F32)
    kvb, kvb_r = lp.sbuf("kvb", [128, TIN], BF16)
    qsp, qsp_r = lp.sbuf("qsp", [128, 2], F32)
    idf, idf_r = lp.sbuf("idf", [128, 128], BF16)
    raw, raw_r = lp.sbuf("raw", [128, 8 * TT], BF16)
    sq5, sq5_r = lp.sbuf("sq5", [128, TT], F32)
    sqb, sqb_r = lp.sbuf("sqb", [128, TT], BF16)
    nst, nst_r = lp.sbuf("nst", [16, TT], F32)
    nsb, nsb_r = lp.sbuf("nsb", [16, TT], BF16)
    tmpa, tmpa_r = lp.sbuf("tmpa", [128, TT], F32)
    tmpb, tmpb_r = lp.sbuf("tmpb", [128, TT], BF16)
    qrot, qrot_r = lp.sbuf("qrot", [128, 8 * 128], BF16)
    scb_l = [lp.sbuf(f"scb{i}", [128, 1], F32) for i in range(2)]
    mqb_l = [lp.sbuf(f"mqb{i}", [128, 1], BF16) for i in range(2)]
    mrw_l = [lp.sbuf("mrw0", [1, 128], F32)] * 2
    ebuf_l = [lp.sbuf(f"ebuf{i}", [128, 384], BF16) for i in range(2)]
    srw_l = [lp.sbuf(f"srw{i}", [1, 128], F32) for i in range(2)]
    tmph_l = [lp.sbuf(f"tmph{i}", [128, 384], F32) for i in range(2)]
    on1b, on1b_r = lp.sbuf("on1b", [128, 1], BF16)
    tmpo, tmpo_r = lp.sbuf("tmpo", [128, 128], F32)
    bandT, bandT_r = lp.sbuf("bandT", [128, 384], F32)
    kvcol, kvcol_r = lp.sbuf("kvcol", [128, 18], F32)
    osb, osb_r = lp.sbuf("osb", [128, 8 * 128], BF16)
    rsd, rsd_r = lp.sbuf("rsd", [128, 8 * 128], F32)
    x2s, x2s_r = lp.sbuf("x2s", [128, 8 * 128], F32)

    class Pool:
        def __init__(self):
            self.tiles = [lp.psum(f"pp{i}", [128, TT]) for i in range(6)]
            self.i = 0
        def get(self):
            t = self.tiles[self.i % 6]
            self.i += 1
            return t
    pool = Pool()
    psO_l = [lp.psum("psOd0", [128, 128])]
    psS_l = [lp.psum("psSd0", [128, 128])]

    def ldxs(h):
        return h.dma_start(out=xsb[:, :].rearrange("p (kc t) -> p kc t", kc=8),
                           in_=yin.rearrange("(kc p) t -> p kc t", p=128))
    g.dma("gpsimd", ldxs, writes=(xsb_r,))
    ld_plain(lp, e64[:, :], e64_r, e64_i, eng="sync")
    ld_plain(lp, b64[:, :], b64_r, b64_i, eng="sync")
    ld_plain(lp, r64[:, :], r64_r, r64_i, eng="sync")
    ld_plain(lp, band[:, :], band_r, band_i, eng="sync")
    ld_plain(lp, qsp[:, 0:1], qsp_r, qs_i, eng="sync")
    ld_plain(lp, qsp[:, 1:2], qsp_r, ks_i, eng="sync")
    ld_plain(lp, idf[:, :], idf_r, idf_i, eng="sync")
    ld_plain(lp, bandT[:, :], bandT_r, bandT_i, eng="sync")
    ld_plain(lp, kvcol[:, :], kvcol_r, kvcol_i, eng="sync")

    wr = wqkv.rearrange("(kc p) f -> p kc f", p=128)
    for half, rr in ((0, wkv_r[0]), (1, wkv_r[1])):
        def ldw(h, half=half):
            return h.dma_start(
                out=wkv[:, half * 8192:(half + 1) * 8192].rearrange(
                    "p (kc f) -> p kc f", kc=8),
                in_=wr[:, :, 1024 + half * 1024: 1024 + (half + 1) * 1024])
        g.dma("gpsimd", ldw, writes=(rr,))
    def ldc1(h):
        return h.dma_start(out=ctab[:, 0:TIN], in_=csk)
    g.dma("sync", ldc1, writes=(ctab_r[0],))

    def ldc2(h):
        return h.dma_start(out=ctab[:, TIN:2 * TIN], in_=ssk)
    g.dma("sync", ldc2, writes=(ctab_r[1],))

    kvr_s, kvr_sr = lp.sbuf("kvrs", [1, TIN], F32)
    ld_plain(lp, kvr_s[:, :], kvr_sr, kvrow, eng="sync")
    onb, onb_r = lp.sbuf("onb", [1, 128], F32)

    def c0(h):
        h.memset(on1b[:, :], 1.0)
        return h.memset(onb[:, :], 1.0)
    g.op("vector", c0, writes=(onb_r, on1b_r))
    for j in range(5):
        w = TIN - j * 512 if j == 4 else 512
        w = min(512, TIN - j * 512)
        ps, psr = pool.get()

        def bkv(h, j=j, w=w, ps=ps):
            return h.matmul(ps[:, 0:w], onb[:, :], kvr_s[:, j * 512: j * 512 + w],
                            start=True, stop=True)
        g.op("tensor", bkv, reads=(kvr_sr, onb_r), writes=(psr,))

        def ckv(h, j=j, w=w, ps=ps):
            return h.tensor_copy(kvb[:, j * 512: j * 512 + w], ps[:, 0:w])
        g.op("vector", ckv, reads=(psr,), writes=(kvb_r,))

    def shifted(kc, c0, w):
        off = -1 if kc >= 4 else 0
        return xsb[:, kc * TIN + c0 + off: kc * TIN + c0 + off + w]

    import os as _os
    _skips = _os.environ.get("AF_SKIP", "")
    S1 = [(128, 512), (640, 512), (1152, 512), (1664, 512), (2176, 256)]
    if "s1" in _skips:
        S1 = []
    for (c0, w) in S1:
        for which, wres, whalf in (("k", wkv_r[0], 0), ("v", wkv_r[1], 1)):
            for fc in range(8):
                ps, psr = pool.get()

                def mmkv(h, fc=fc, c0=c0, w=w, ps=ps, whalf=whalf):
                    ins = None
                    for kc in range(8):
                        ins = h.matmul(
                            ps[:, 0:w],
                            wkv[:, whalf * 8192 + kc * 1024 + fc * 128:
                                whalf * 8192 + kc * 1024 + (fc + 1) * 128],
                            shifted(kc, c0, w),
                            start=(kc == 0), stop=(kc == 7))
                    return ins
                g.op("tensor", mmkv, reads=(xsb_r, wres), writes=(psr,))

                def evkv(h, fc=fc, w=w, ps=ps):
                    return h.activation(raw[:, fc * TT: fc * TT + w], ps[:, 0:w],
                                        AF.Copy)
                g.op("scalar", evkv, reads=(psr,), writes=(raw_r,))

            if which == "k" and "krot" in _skips:
                continue
            if which == "k":
                pss, pss_r = pool.get()
                for fc in range(8):
                    def sqf(h, fc=fc, w=w):
                        return h.activation(sqb[:, 0:w], raw[:, fc * TT: fc * TT + w],
                                            AF.Square)
                    g.op("scalar", sqf, reads=(raw_r,), writes=(sqb_r,))

                    def mme(h, fc=fc, w=w, pss=pss):
                        return h.matmul(pss[0:16, 0:w], e64[:, fc * 16:(fc + 1) * 16],
                                        sqb[:, 0:w], start=(fc == 0), stop=(fc == 7))
                    g.op("tensor", mme, reads=(sqb_r, e64_r), writes=(pss_r,))

                def evn(h, w=w, pss=pss):
                    return h.activation(nst[:, 0:w], pss[0:16, 0:w], AF.Sqrt)
                g.op("scalar", evn, reads=(pss_r,), writes=(nst_r,))

                def rcna(h, w=w):
                    h.tensor_scalar_max(nst[:, 0:w], nst[:, 0:w], 1e-12)
                    return h.reciprocal(nst[:, 0:w], nst[:, 0:w])
                g.op("vector", rcna, reads=(nst_r,), writes=(nst_r,), fence=True)

                def rcn(h, w=w):
                    return h.tensor_copy(nsb[:, 0:w], nst[:, 0:w])
                g.op("vector", rcn, reads=(nst_r,), writes=(nsb_r,))

                for fc in range(8):
                    psb, psb_r = pool.get()

                    def bcn(h, fc=fc, w=w, psb=psb):
                        return h.matmul(psb[:, 0:w], b64[:, fc * 128:(fc + 1) * 128],
                                        nsb[:, 0:w], start=True, stop=True)
                    g.op("tensor", bcn, reads=(nsb_r, b64_r), writes=(psb_r,))

                    def nm1(h, fc=fc, w=w, psb=psb):
                        h.tensor_tensor(tmpa[:, 0:w], raw[:, fc * TT: fc * TT + w],
                                        psb[:, 0:w], op=ALU.mult)
                        return h.tensor_scalar(tmpb[:, 0:w], tmpa[:, 0:w],
                                               qsp[:, 1:2], None, op0=ALU.mult)
                    g.op("vector", nm1, reads=(raw_r, psb_r, qsp_r),
                         writes=(tmpa_r, tmpb_r))

                    psr2, psr2_r = pool.get()

                    def rmm(h, w=w, psr2=psr2):
                        return h.matmul(psr2[:, 0:w], r64[:, :], tmpb[:, 0:w],
                                        start=True, stop=True)
                    g.op("tensor", rmm, reads=(tmpb_r, r64_r), writes=(psr2_r,))

                    def ro1(h, fc=fc, c0=c0, w=w, psr2=psr2):
                        h.tensor_tensor(tmpa[:, 0:w], tmpb[:, 0:w],
                                        ctab[:, c0: c0 + w], op=ALU.mult)
                        h.tensor_tensor(sq5[:, 0:w], psr2[:, 0:w],
                                        ctab[:, TIN + c0: TIN + c0 + w], op=ALU.mult)
                        return h.tensor_tensor(
                            krot[:, fc * 2304 + c0 - 128: fc * 2304 + c0 - 128 + w],
                            tmpa[:, 0:w], sq5[:, 0:w], op=ALU.add)
                    g.op("vector", ro1,
                         reads=(tmpb_r, psr2_r, ctab_r[0], ctab_r[1]),
                         writes=(krot_r, tmpa_r, sq5_r))
            elif "vtr" not in _skips:
                nch = w // 128
                for ch in range(nch):
                    ch_abs = (c0 - 128) // 128 + ch
                    for vh in range(2):
                        psv, psv_r = pool.get()

                        def vmm(h, c0=c0, ch=ch, vh=vh, psv=psv):
                            ins = None
                            for kc in range(8):
                                ins = h.matmul(
                                    psv[:, 0:TT],
                                    shifted(kc, c0 + ch * 128, 128),
                                    wkv[:, 8192 + kc * 1024 + vh * TT:
                                        8192 + kc * 1024 + (vh + 1) * TT],
                                    start=(kc == 0), stop=(kc == 7))
                            return ins
                        g.op("tensor", vmm, reads=(xsb_r, wkv_r[1]), writes=(psv_r,))

                        def vev(h, ch_abs=ch_abs, vh=vh, psv=psv):
                            return h.activation(
                                vtok[:, ch_abs * 1024 + vh * TT:
                                     ch_abs * 1024 + (vh + 1) * TT],
                                psv[:, 0:TT], AF.Copy)
                        g.op("scalar", vev, reads=(psv_r,), writes=(vtok_r,))

    def ldwq(h):
        return h.dma_start(
            out=wkv[:, 0:8192].rearrange("p (kc f) -> p kc f", kc=8),
            in_=wr[:, :, 0:1024])
    g.dma("gpsimd", ldwq, writes=(wkv_r[0],))
    wor = wo.rearrange("(kc p) f -> p kc f", p=128)

    def ldwo(h):
        return h.dma_start(
            out=wkv[:, 8192:16384].rearrange("p (kc f) -> p kc f", kc=8),
            in_=wor[:, :, :])
    g.dma("gpsimd", ldwo, writes=(wkv_r[1],))

    def ldc3(h):
        return h.dma_start(out=ctab[:, 0:TIN], in_=csq)
    g.dma("sync", ldc3, writes=(ctab_r[0],))

    def ldc4(h):
        return h.dma_start(out=ctab[:, TIN:2 * TIN], in_=ssq)
    g.dma("sync", ldc4, writes=(ctab_r[1],))

    yr3 = yin.rearrange("(kc p) t -> p kc t", p=128)
    x2r = x2_o.rearrange("(kc p) t -> p kc t", p=128)

    for qt in ([] if "s2" in _skips else range(16)):
        q0 = HALO + qt * 128
        def ldr(h, q0=q0):
            return h.dma_start(out=rsd[:, :].rearrange("p (kc t) -> p kc t", kc=8),
                               in_=yr3[:, :, q0:q0 + 128])
        g.dma("sync", ldr, writes=(rsd_r,))

        for fc in range(8):
            ps, psr = pool.get()

            def mmq(h, fc=fc, q0=q0, ps=ps):
                ins = None
                for kc in range(8):
                    ins = h.matmul(
                        ps[:, 0:128],
                        wkv[:, kc * 1024 + fc * 128: kc * 1024 + (fc + 1) * 128],
                        shifted(kc, q0, 128),
                        start=(kc == 0), stop=(kc == 7))
                return ins
            g.op("tensor", mmq, reads=(xsb_r, wkv_r[0]), writes=(psr,))

            def evq(h, fc=fc, ps=ps):
                return h.activation(raw[:, fc * TT: fc * TT + 128], ps[:, 0:128],
                                    AF.Copy)
            g.op("scalar", evq, reads=(psr,), writes=(raw_r,))

        pss, pss_r = pool.get()
        for fc in range(8):
            def sqf(h, fc=fc):
                return h.activation(sqb[:, 0:128], raw[:, fc * TT: fc * TT + 128],
                                    AF.Square)
            g.op("scalar", sqf, reads=(raw_r,), writes=(sqb_r,))

            def mme(h, fc=fc, pss=pss):
                return h.matmul(pss[0:16, 0:128], e64[:, fc * 16:(fc + 1) * 16],
                                sqb[:, 0:128], start=(fc == 0), stop=(fc == 7))
            g.op("tensor", mme, reads=(sqb_r, e64_r), writes=(pss_r,))

        def evn(h, pss=pss):
            return h.activation(nst[:, 0:128], pss[0:16, 0:128], AF.Sqrt)
        g.op("scalar", evn, reads=(pss_r,), writes=(nst_r,))

        def rcna(h):
            h.tensor_scalar_max(nst[:, 0:128], nst[:, 0:128], 1e-12)
            return h.reciprocal(nst[:, 0:128], nst[:, 0:128])
        g.op("vector", rcna, reads=(nst_r,), writes=(nst_r,), fence=True)

        def rcn(h):
            return h.tensor_copy(nsb[:, 0:128], nst[:, 0:128])
        g.op("vector", rcn, reads=(nst_r,), writes=(nsb_r,))

        for fc in range(8):
            psb, psb_r = pool.get()

            def bcn(h, fc=fc, psb=psb):
                return h.matmul(psb[:, 0:128], b64[:, fc * 128:(fc + 1) * 128],
                                nsb[:, 0:128], start=True, stop=True)
            g.op("tensor", bcn, reads=(nsb_r, b64_r), writes=(psb_r,))

            def nm1(h, fc=fc, psb=psb):
                h.tensor_tensor(tmpa[:, 0:128], raw[:, fc * TT: fc * TT + 128],
                                psb[:, 0:128], op=ALU.mult)
                return h.tensor_scalar(tmpb[:, 0:128], tmpa[:, 0:128],
                                       qsp[:, 0:1], None, op0=ALU.mult)
            g.op("vector", nm1, reads=(raw_r, psb_r, qsp_r),
                 writes=(tmpa_r, tmpb_r))

            psr2, psr2_r = pool.get()

            def rmm(h, psr2=psr2):
                return h.matmul(psr2[:, 0:128], r64[:, :], tmpb[:, 0:128],
                                start=True, stop=True)
            g.op("tensor", rmm, reads=(tmpb_r, r64_r), writes=(psr2_r,))

            def ro1(h, fc=fc, q0=q0, psr2=psr2):
                h.tensor_tensor(tmpa[:, 0:128], tmpb[:, 0:128],
                                ctab[:, q0: q0 + 128], op=ALU.mult)
                h.tensor_tensor(sq5[:, 0:128], psr2[:, 0:128],
                                ctab[:, TIN + q0: TIN + q0 + 128], op=ALU.mult)
                return h.tensor_tensor(qrot[:, fc * 128:(fc + 1) * 128],
                                       tmpa[:, 0:128], sq5[:, 0:128], op=ALU.add)
            g.op("vector", ro1, reads=(tmpb_r, psr2_r, ctab_r[0], ctab_r[1]),
                 writes=(qrot_r, tmpa_r, sq5_r))

        ch0 = (q0 - 256) // 128 - 1
        k0 = q0 - 256 - 128

        kvband, kvband_r = sq5, sq5_r
        for j2 in range(3):
            def mkkb(h, j2=j2, ch0=ch0):
                return h.tensor_scalar(kvband[:, j2 * 128:(j2 + 1) * 128],
                                       bandT[:, j2 * 128:(j2 + 1) * 128],
                                       kvcol[:, ch0 + j2: ch0 + j2 + 1], None,
                                       op0=ALU.add)
            g.op("vector", mkkb, reads=(bandT_r, kvcol_r), writes=(kvband_r,),
                 fence=(j2 == 2))

        def ph1(hd):
            fc, par = hd // 2, hd % 2
            rrot = hd % 2
            (scb, scb_r) = scb_l[rrot]
            (mqb, mqb_r) = mqb_l[rrot]
            (mrw, mrw_r) = mrw_l[rrot]
            (ebuf, ebuf_r) = ebuf_l[rrot]
            psQ, psQ_r = pool.get()

            def mms(h, fc=fc, par=par, k0=k0, psQ=psQ):
                return h.matmul(
                    psQ[:, 0:384],
                    qrot[par * 64: par * 64 + 64, fc * 128:(fc + 1) * 128],
                    krot[par * 64: par * 64 + 64,
                         fc * 2304 + k0: fc * 2304 + k0 + 384],
                    start=True, stop=True)
            g.op("tensor", mms, reads=(qrot_r, krot_r), writes=(psQ_r,))

            def msks(h, q0=q0, psQ=psQ, ebuf=ebuf):
                h.tensor_tensor(ebuf[:, :], psQ[:, 0:384], band[:, :],
                                op=ALU.add)
                return h.tensor_tensor(ebuf[:, :], ebuf[:, :],
                                       kvb[:, q0 - 256: q0 + 128], op=ALU.add)
            g.op("vector", msks, reads=(psQ_r, band_r, kvb_r),
                 writes=(ebuf_r,))
            return {"mrw": (mrw, mrw_r), "scb": (scb, scb_r),
                    "mqb": (mqb, mqb_r), "ebuf": (ebuf, ebuf_r)}

        def ph1b(hd, st):
            (scb, scb_r) = st["scb"]
            (mqb, mqb_r) = st["mqb"]
            (mrw, mrw_r) = st["mrw"]
            (ebuf, ebuf_r) = st["ebuf"]

            def msk2(h, ebuf=ebuf, scb=scb):
                return h.tensor_reduce(scb[:, 0:1], ebuf[:, :],
                                       axis=mybir.AxisListType.X, op=ALU.max)
            g.op("vector", msk2, reads=(ebuf_r,), writes=(scb_r,))

            def msk3(h, mqb=mqb, scb=scb):
                return h.activation(mqb[:, :], scb[:, 0:1], AF.Copy)
            g.op("scalar", msk3, reads=(scb_r,), writes=(mqb_r,))

            psM, psM_r = pool.get()

            def mrow(h, psM=psM, mqb=mqb):
                return h.matmul(psM[0:1, 0:128], mqb[:, :], idf[:, :],
                                start=True, stop=True)
            g.op("tensor", mrow, reads=(mqb_r, idf_r), writes=(psM_r,))

            def erow(h, psM=psM, mrw=mrw):
                return h.activation(mrw[0:1, :], psM[0:1, 0:128], AF.Copy)
            g.op("scalar", erow, reads=(psM_r,), writes=(mrw_r,))
            return st

        def ph2(hd, st):
            fc, par = hd // 2, hd % 2
            rrot = hd % 2
            (mrw, mrw_r) = st["mrw"]
            (ebuf, ebuf_r) = ebuf_l[rrot]
            (tmpah, tmpah_r) = tmph_l[rrot]
            psB, psB_r = pool.get()

            def mbc(h, psB=psB, mrw=mrw):
                ins = None
                for j2 in range(3):
                    ins = h.matmul(psB[:, j2 * 128:(j2 + 1) * 128], onb[:, :],
                                   mrw[0:1, :], start=True, stop=True)
                return ins
            g.op("tensor", mbc, reads=(mrw_r, onb_r), writes=(psB_r,))

            psT, psT_r = pool.get()

            def mmt(h, fc=fc, par=par, k0=k0, psT=psT):
                ins = None
                for j2 in range(3):
                    ins = h.matmul(
                        psT[:, j2 * 128:(j2 + 1) * 128],
                        krot[par * 64: par * 64 + 64,
                             fc * 2304 + k0 + j2 * 128:
                             fc * 2304 + k0 + (j2 + 1) * 128],
                        qrot[par * 64: par * 64 + 64, fc * 128:(fc + 1) * 128],
                        start=True, stop=True)
                return ins
            g.op("tensor", mmt, reads=(qrot_r, krot_r), writes=(psT_r,))

            def earg(h, psT=psT, tmpah=tmpah):
                return h.tensor_tensor(tmpah[:, :], psT[:, 0:384],
                                       kvband[:, 0:384], op=ALU.add)
            g.op("vector", earg, reads=(psT_r, kvband_r), writes=(tmpah_r,))

            def earg2(h, psB=psB, tmpah=tmpah):
                return h.tensor_tensor(tmpah[:, :], tmpah[:, :],
                                       psB[:, 0:384], op=ALU.subtract)
            g.op("vector", earg2, reads=(psB_r, tmpah_r), writes=(tmpah_r,))

            def eexp(h, tmpah=tmpah, ebuf=ebuf):
                return h.activation(ebuf[:, :], tmpah[:, :], AF.Exp, scale=8.0)
            g.op("scalar", eexp, reads=(tmpah_r,), writes=(ebuf_r,))
            return {}

        def ph3(hd):
            fc, par = hd // 2, hd % 2
            rrot = hd % 2
            (ebuf, ebuf_r) = ebuf_l[rrot]
            (srw, srw_r) = srw_l[rrot]
            pso, pso_r = psO_l[0]
            psS, psS_r = psS_l[0]
            srow = (hd % 2) * 32

            def summall(h, srow=srow, psS=psS, ebuf=ebuf):
                h.matmul(psS[srow:srow + 1, 0:128], on1b[:, :],
                         ebuf[:, 0:128], start=True, stop=False)
                h.matmul(psS[srow:srow + 1, 0:128], on1b[:, :],
                         ebuf[:, 128:256], start=False, stop=False)
                return h.matmul(psS[srow:srow + 1, 0:128], on1b[:, :],
                                ebuf[:, 256:384], start=False, stop=True)
            g.op("tensor", summall, reads=(ebuf_r, on1b_r), writes=(psS_r,))

            def esum(h, srow=srow, psS=psS, srw=srw):
                return h.activation(srw[0:1, :], psS[srow:srow + 1, 0:128],
                                    AF.Copy)
            g.op("scalar", esum, reads=(psS_r,), writes=(srw_r,))

            def rsum(h, srw=srw):
                return h.reciprocal(srw[0:1, :], srw[0:1, :])
            g.op("vector", rsum, reads=(srw_r,), writes=(srw_r,))

            psR, psR_r = pool.get()

            def rbc(h, psR=psR, srw=srw):
                return h.matmul(psR[:, 0:128], onb[:, :], srw[0:1, :],
                                start=True, stop=True)
            g.op("tensor", rbc, reads=(srw_r, onb_r), writes=(psR_r,))

            def mmo(h, hd=hd, ch0=ch0, par=par, pso=pso, ebuf=ebuf):
                ins = None
                for j2 in range(3):
                    ins = h.matmul(
                        pso[par * 64: par * 64 + 64, 0:128],
                        vtok[:, (ch0 + j2) * 1024 + hd * 64:
                             (ch0 + j2) * 1024 + (hd + 1) * 64],
                        ebuf[:, j2 * 128:(j2 + 1) * 128],
                        start=(j2 == 0), stop=(j2 == 2))
                return ins
            g.op("tensor", mmo, reads=(vtok_r, ebuf_r), writes=(pso_r,))

            def evo(h, fc=fc, par=par, pso=pso):
                return h.activation(tmpo[par * 64:par * 64 + 64, :],
                                    pso[par * 64:par * 64 + 64, 0:128], AF.Copy)
            g.op("scalar", evo, reads=(pso_r,), writes=(tmpo_r,))

            def nrm2(h, fc=fc, par=par, psR=psR):
                return h.tensor_tensor(osb[par * 64:par * 64 + 64,
                                           fc * 128:(fc + 1) * 128],
                                       tmpo[par * 64:par * 64 + 64, :],
                                       psR[par * 64:par * 64 + 64, 0:128],
                                       op=ALU.mult)
            g.op("vector", nrm2, reads=(tmpo_r, psR_r), writes=(osb_r,))

        for hp in range(0, 16, 2):
            st0 = ph1(hp)
            st1 = ph1(hp + 1)
            ph1b(hp, st0)
            ph1b(hp + 1, st1)
            ph2(hp, st0)
            ph2(hp + 1, st1)
            ph3(hp)
            ph3(hp + 1)

        for fc in range(8):
            ps, psr = pool.get()

            def mmw(h, fc=fc, ps=ps):
                ins = None
                for kc in range(8):
                    ins = h.matmul(
                        ps[:, 0:128],
                        wkv[:, 8192 + kc * 1024 + fc * 128:
                            8192 + kc * 1024 + (fc + 1) * 128],
                        osb[:, kc * 128:(kc + 1) * 128],
                        start=(kc == 0), stop=(kc == 7))
                return ins
            g.op("tensor", mmw, reads=(osb_r, wkv_r[1]), writes=(psr,))

            def adr(h, fc=fc, ps=ps):
                return h.tensor_tensor(x2s[:, fc * 128:(fc + 1) * 128],
                                       rsd[:, fc * 128:(fc + 1) * 128],
                                       ps[:, 0:128], op=ALU.add)
            g.op("vector", adr, reads=(rsd_r, psr), writes=(x2s_r,))

        def stx(h, qt=qt):
            return h.dma_start(out=x2r[:, :, qt * 128:(qt + 1) * 128],
                               in_=x2s[:, :].rearrange("p (kc t) -> p kc t", kc=8))
        g.dma("gpsimd", stx, reads=(x2s_r,), writes=(x2_or,))

    def fin(h):
        h.nop()
        return None
    g.op("vector", fin, reads=(x2_or,))
    return lp



def _rotary_tables(pos):
    half = np.arange(0, 64, 2, dtype=np.float32)
    inv_freq = (1.0 / 10000.0 ** (half / 64.0)).astype(np.float32)
    t = pos.astype(np.float32)
    fr = t[:, None] * inv_freq[None, :]
    fr = np.concatenate([fr, fr], -1)
    base_scale = ((half + 0.4 * 64) / (1.4 * 64)).astype(np.float32)
    power = (t - 8192 // 2) / (256 // 2)
    sc = base_scale[None, :] ** power[:, None]
    sc = np.concatenate([sc, sc], -1).astype(np.float32)
    return np.cos(fr), np.sin(fr), sc


def host_AFa(inputs, cores, prev, d, y_full):
    import ml_dtypes
    bf = ml_dtypes.bfloat16
    e64 = np.zeros((128, 128), np.float32)
    for fc in range(8):
        for hh in range(16):
            if 2 * fc <= hh <= 2 * fc + 1:
                e64[(hh - 2 * fc) * 64:(hh - 2 * fc) * 64 + 64, fc * 16 + hh] = 1
    b64 = np.zeros((16, 1024), np.float32)
    for fc in range(8):
        for hh in range(16):
            if 2 * fc <= hh <= 2 * fc + 1:
                b64[hh, fc * 128 + (hh - 2 * fc) * 64: fc * 128 + (hh - 2 * fc) * 64 + 64] = 1
    r64 = np.zeros((128, 128), np.float32)
    for blk in range(2):
        for m in range(64):
            if m < 32:
                r64[blk * 64 + m + 32, blk * 64 + m] = -1.0
            else:
                r64[blk * 64 + m - 32, blk * 64 + m] = 1.0
    pr = np.arange(128)[:, None]
    rr = np.arange(384)[None, :]
    band = np.where((pr <= rr) & (rr <= pr + 256), 0.0, -1e30).astype(np.float32)
    idf = np.eye(128).astype(bf)
    bandT = np.ascontiguousarray(
        np.concatenate([band[:, j * 128:(j + 1) * 128].T for j in range(3)], 1))
    qs = np.tile(np.asarray(inputs["L1_ffn_q_scale"])[d], 2).reshape(128, 1)
    ks = np.tile(np.asarray(inputs["L1_ffn_k_scale"])[d], 2).reshape(128, 1)
    base = {
        "wqkv": np.ascontiguousarray(np.asarray(inputs["L1_ffn_Wqkv"])[d]),
        "wo": np.ascontiguousarray(np.asarray(inputs["L1_ffn_Wo"])[d]),
        "qs": np.ascontiguousarray(qs.astype(np.float32)),
        "ks": np.ascontiguousarray(ks.astype(np.float32)),
        "e64i": e64.astype(bf), "b64i": b64.astype(bf), "r64i": r64.astype(bf),
        "bandi": band, "bandTi": bandT, "idfi": idf,
    }
    yfm = fm(y_full)
    maps = []
    for c in cores:
        b, qt = c // 4, c % 4
        ypad = np.concatenate(
            [np.zeros((DM, HALO), np.float32), yfm[:, b * 8192:(b + 1) * 8192]], 1)
        yin = np.ascontiguousarray(ypad[:, qt * 2048: qt * 2048 + TIN])
        pos = np.clip(np.arange(qt * 2048 - HALO, qt * 2048 - HALO + TIN), 0, 8191)
        cs, sn, sc = _rotary_tables(pos)
        rows = np.arange(128) % 64
        csq = np.ascontiguousarray((cs * sc)[:, rows].T).astype(bf)
        ssq = np.ascontiguousarray((sn * sc)[:, rows].T).astype(bf)
        csk = np.ascontiguousarray((cs / sc)[:, rows].T).astype(bf)
        ssk = np.ascontiguousarray((sn / sc)[:, rows].T).astype(bf)
        kvr = np.zeros((1, TIN), np.float32)
        if qt == 0:
            kvr[0, :HALO] = -1e30
        kvc = np.ascontiguousarray(kvr[0, 128:2432].reshape(18, 128).T)
        maps.append(dict(base, yin=yin, csq=csq, ssq=ssq, csk=csk, ssk=ssk,
                         kvrow=kvr, kvcoli=kvc))
    return maps


def build_AFf():
    lp = LP()
    g = lp.g
    x2h = lp.dram_in("x2h", [DM, 2049], F32)
    w1p = lp.dram_in("w1p", [DM, 5632], F32)
    w2p = lp.dram_in("w2p", [2816, DM], F32)
    b1p = lp.dram_in("b1p", [128, 44], F32)
    b2p = lp.dram_in("b2p", [128, 8], F32)
    lng = lp.dram_in("lng", [128, 8], F32)
    lnb = lp.dram_in("lnb", [128, 8], F32)
    y_o, y_or = lp.dram_out("y_o", [DM, TPC], F32)

    xft, xft_r = lp.sbuf("xft", [128, 8 * 513], F32)
    xnt, xnt_r = lp.sbuf("xnt", [128, 8 * TT], BF16)
    w2b, w2b_r = lp.sbuf("w2b", [128, 22 * DM], BF16)
    wst, _ = lp.sbuf("wst", [128, 4 * 2048], BF16)
    wst_r = [Res(f"wst{i}") for i in range(4)]
    b1s, b1s_r = lp.sbuf("b1s", [128, 44], F32)
    b2s, b2s_r = lp.sbuf("b2s", [128, 8], F32)
    lgs, lgs_r = lp.sbuf("lgs", [128, 8], F32)
    lbs, lbs_r = lp.sbuf("lbs", [128, 8], F32)
    tta_l = [lp.sbuf(f"tta{i}", [128, TT], F32) for i in range(2)]
    ttg_l = [lp.sbuf(f"ttg{i}", [128, TT], F32) for i in range(2)]
    h2b, h2b_r = lp.sbuf("h2b", [128, 22 * TT], BF16)
    yst, _ = lp.sbuf("yst", [128, 2 * TT], F32)
    yst_r = [Res("yst0"), Res("yst1")]
    sqs, _ = lp.sbuf("sqs", [128, 2 * TT], F32)
    sq_r = (Res("sq0"), Res("sq1"))
    sts, sts_r = lp.sbuf("sts", [1, 4 * TT], F32)
    on1, on1_r = lp.sbuf("on1", [128, 1], F32)
    onb, onb_r = lp.sbuf("onb", [1, 128], F32)
    wrs, wrs_r = lp.sbuf("wrs", [128, 64], F32)
    brs, brs_r = lp.sbuf("brs", [1, 8], F32)
    brb, brb_r = lp.sbuf("brb", [128, 8], F32)
    lgr, lgr_r = lp.sbuf("lgr", [128, 128], F32)
    e8t, e8t_r = lp.sbuf("e8t", [128, 128], F32)
    mx8, mx8_r = lp.sbuf("mx8", [128, 16], F32)
    nm8, nm8_r = lp.sbuf("nm8", [128, 16], F32)
    sm8, sm8_r = lp.sbuf("sm8", [128, 16], F32)
    rc8, rc8_r = lp.sbuf("rc8", [128, 16], F32)
    wmt, wmt_r = lp.sbuf("wmt", [128, 128], F32)

    class Pool:
        def __init__(self):
            self.tiles = [lp.psum(f"pp{i}", [128, TT]) for i in range(6)]
            self.i = 0
        def get(self):
            t = self.tiles[self.i % 6]
            self.i += 1
            return t
    pool = Pool()

    def c0(h):
        h.memset(on1[:, :], 1.0)
        return h.memset(onb[:, :], 1.0)
    g.op("vector", c0, writes=(on1_r, onb_r))
    ld_plain(lp, b1s[:, :], b1s_r, b1p, eng="sync")
    ld_plain(lp, b2s[:, :], b2s_r, b2p, eng="sync")
    ld_plain(lp, lgs[:, :], lgs_r, lng, eng="sync")
    ld_plain(lp, lbs[:, :], lbs_r, lnb, eng="sync")
    ld_fm(lp, w2b, w2b_r, w2p, 22)

    w1r = w1p.rearrange("(kc p) f -> p kc f", p=128)
    xr = x2h.rearrange("(kc p) t -> p kc t", p=128)
    yr = y_o.rearrange("(kc p) t -> p kc t", p=128)

    for t in range(TPC // TT):
        def ldx1(h, t=t):
            return h.dma_start(
                out=xft[:, :].rearrange("p (kc t) -> p kc t", kc=8)[:, 0:4, 0:512],
                in_=xr[:, 0:4, 1 + t * TT: 1 + (t + 1) * TT])
        g.dma("sync", ldx1, writes=(xft_r,))

        def ldx2(h, t=t):
            return h.dma_start(
                out=xft[:, :].rearrange("p (kc t) -> p kc t", kc=8)[:, 4:8, 0:513],
                in_=xr[:, 4:8, t * TT: t * TT + 513])
        g.dma("sync", ldx2, writes=(xft_r,))

        emit_ln(lp, pool, xft, xft_r, xnt, xnt_r, on1, on1_r, onb, onb_r,
                sqs, sq_r, sts, sts_r, 8, 0, 513, 0, TT,
                gam=lgs, gam_r=lgs_r, bet=lbs, bet_r=lbs_r)

        for fc in range(22):
            sa = (fc // 2) % 2
            sg = 2 + (fc // 2) % 2
            fo = (fc % 2) * 128

            if fc % 2 == 0:
                nw = min(2, 22 - fc)

                def ldwa(h, fc=fc, sa=sa, nw=nw):
                    return h.dma_start(
                        out=wst[:, sa * 2048: sa * 2048 + nw * 1024].rearrange(
                            "p (kc f) -> p kc f", kc=8),
                        in_=w1r[:, :, fc * 128:(fc + nw) * 128])
                g.dma("gpsimd", ldwa, writes=(wst_r[sa],))

                def ldwg(h, fc=fc, sg=sg, nw=nw):
                    return h.dma_start(
                        out=wst[:, sg * 2048: sg * 2048 + nw * 1024].rearrange(
                            "p (kc f) -> p kc f", kc=8),
                        in_=w1r[:, :, 2816 + fc * 128: 2816 + (fc + nw) * 128])
                g.dma("gpsimd", ldwg, writes=(wst_r[sg],))

            psa, psa_r = pool.get()
            psg, psg_r = pool.get()

            def mma(h, sa=sa, fo=fo, psa=psa):
                ins = None
                for kc in range(8):
                    ins = h.matmul(psa[:, 0:TT],
                                   wst[:, sa * 2048 + kc * 256 + fo:
                                       sa * 2048 + kc * 256 + fo + 128],
                                   xnt[:, kc * TT:(kc + 1) * TT],
                                   start=(kc == 0), stop=(kc == 7))
                return ins
            g.op("tensor", mma, reads=(xnt_r, wst_r[sa]), writes=(psa_r,))

            def mmg(h, sg=sg, fo=fo, psg=psg):
                ins = None
                for kc in range(8):
                    ins = h.matmul(psg[:, 0:TT],
                                   wst[:, sg * 2048 + kc * 256 + fo:
                                       sg * 2048 + kc * 256 + fo + 128],
                                   xnt[:, kc * TT:(kc + 1) * TT],
                                   start=(kc == 0), stop=(kc == 7))
                return ins
            g.op("tensor", mmg, reads=(xnt_r, wst_r[sg]), writes=(psg_r,))

            tta, tta_r = tta_l[fc % 2]
            ttg, ttg_r = ttg_l[fc % 2]

            def eva(h, fc=fc, psa=psa, tta=tta):
                return h.activation(tta[:, :], psa[:, 0:TT], AF.Identity,
                                    bias=b1s[:, fc:fc + 1])
            g.op("scalar", eva, reads=(psa_r, b1s_r), writes=(tta_r,))

            def evg(h, fc=fc, psg=psg, ttg=ttg):
                return h.activation(ttg[:, :], psg[:, 0:TT], AF.Gelu,
                                    bias=b1s[:, 22 + fc:22 + fc + 1])
            g.op("scalar", evg, reads=(psg_r, b1s_r), writes=(ttg_r,))

            def mul(h, fc=fc, tta=tta, ttg=ttg):
                return h.tensor_tensor(h2b[:, fc * TT:(fc + 1) * TT],
                                       tta[:, :], ttg[:, :], op=ALU.mult)
            g.op("vector", mul, reads=(tta_r, ttg_r), writes=(h2b_r,))

        for dc in range(8):
            ps, psr = pool.get()

            def mmd(h, dc=dc, ps=ps):
                ins = None
                for kc in range(22):
                    ins = h.matmul(ps[:, 0:TT],
                                   w2b[:, kc * DM + dc * 128: kc * DM + (dc + 1) * 128],
                                   h2b[:, kc * TT:(kc + 1) * TT],
                                   start=(kc == 0), stop=(kc == 21))
                return ins
            g.op("tensor", mmd, reads=(h2b_r, w2b_r), writes=(psr,))
            slot = dc % 2

            def evd(h, dc=dc, ps=ps, slot=slot):
                return h.activation(yst[:, slot * TT:(slot + 1) * TT], ps[:, 0:TT],
                                    AF.Identity, bias=b2s[:, dc:dc + 1])
            g.op("scalar", evd, reads=(psr, b2s_r), writes=(yst_r[slot],))

            def adr(h, dc=dc, slot=slot):
                off = 0 if dc < 4 else 1
                return h.tensor_tensor(yst[:, slot * TT:(slot + 1) * TT],
                                       yst[:, slot * TT:(slot + 1) * TT],
                                       xft[:, dc * 513 + off: dc * 513 + off + TT],
                                       op=ALU.add)
            g.op("vector", adr, reads=(xft_r, yst_r[slot]), writes=(yst_r[slot],))

            def std(h, t=t, dc=dc, slot=slot):
                return h.dma_start(out=yr[:, dc, t * TT:(t + 1) * TT],
                                   in_=yst[:, slot * TT:(slot + 1) * TT])
            g.dma("gpsimd", std, reads=(yst_r[slot],), writes=(y_or,))

    def fin(h):
        h.nop()
        return None
    g.op("vector", fin, reads=(y_or,))
    return lp


def host_AFf(inputs, cores, prev, d, x2_full):
    W1 = np.asarray(inputs["L1_ffn_W1"])[d]
    W2 = np.asarray(inputs["L1_ffn_W2"])[d]
    b1 = np.asarray(inputs["L1_ffn_b1"])[d]
    b2 = np.asarray(inputs["L1_ffn_b2"])[d]
    w1pad = np.zeros((DM, 5632), np.float32)
    w1pad[:, 0:2730] = W1[:, 0:2730]
    w1pad[:, 2816:2816 + 2730] = W1[:, 2730:5460]
    w2pad = np.zeros((2816, DM), np.float32)
    w2pad[0:2730] = W2
    b1pad = np.zeros(5632, np.float32)
    b1pad[0:2730] = b1[0:2730]
    b1pad[2816:2816 + 2730] = b1[2730:]
    base = {
        "w1p": w1pad, "w2p": w2pad, "b1p": pp(b1pad), "b2p": pp(b2),
        "lng": pp(np.asarray(inputs["L1_ffn_ln_g"])[d]),
        "lnb": pp(np.asarray(inputs["L1_ffn_ln_b"])[d]),
    }
    xfm = fm(x2_full)
    maps = []
    for c in cores:
        b, qt = c // 4, c % 4
        if qt == 0:
            x2h = np.concatenate([np.zeros((DM, 1), np.float32),
                                  xfm[:, b * 8192: b * 8192 + 2048]], 1)
        else:
            x2h = xfm[:, c * TPC - 1: (c + 1) * TPC]
        maps.append(dict(base, x2h=np.ascontiguousarray(x2h)))
    return maps


def build_A9a():
    lp = LP()
    g = lp.g
    x2a = lp.dram_in("x2a", [DM, TPC], F32)
    y6 = lp.dram_in("y6", [DM, TPC], F32)
    win = lp.dram_in("win", [DM, 4232], F32)
    binr = lp.dram_in("binr", [1, 4232], F32)
    lng = lp.dram_in("lng", [128, 8], F32)
    lnb = lp.dram_in("lnb", [128, 8], F32)
    binp = lp.dram_in("binp", [128, 33], F32)
    alog = lp.dram_in("alog", [1, 8], F32)
    dtb = lp.dram_in("dtb", [1, 8], F32)
    dski = lp.dram_in("dski", [1, 8], F32)
    triU_i = lp.dram_in("triUi", [128, 128], BF16)
    triS_i = lp.dram_in("triSi", [128, 128], BF16)
    idf_i = lp.dram_in("idfi", [128, 128], BF16)
    idf32_i = lp.dram_in("idf32i", [128, 128], F32)
    maskT_i = lp.dram_in("maskTi", [128, 128], F32)
    x2_o, x2_or = lp.dram_out("x2_o", [DM, TPC], F32)
    xn2_o, xn2_or = lp.dram_out("xn2_o", [DM, TPC], F32)
    xn2b_o, xn2b_or = lp.dram_out("xn2b_o", [DM, TPC], BF16)
    sz_o, sz_or = lp.dram_out("sz_o", [2048, TPC], BF16)
    yid_o, yid_or = lp.dram_out("yid_o", [2048, TPC], BF16)
    ce_o, ce_or = lp.dram_out("ce_o", [128, 16384], BF16)
    hloc_o, hloc_or = lp.dram_out("hloc_o", [128, 1024], F32)
    dcor_o, dcor_or = lp.dram_out("dcor_o", [1, 8], F32)
    csum_o, csum_or = lp.dram_out("csum_o", [1, 128], F32)

    xat, xat_r = lp.sbuf("xat", [128, 8 * TT], F32)
    x2t, x2t_r = lp.sbuf("x2t", [128, 8 * TT], F32)
    xnt, xnt_r = xat, xat_r
    xnbt, xnbt_r = lp.sbuf("xnbt", [128, 8 * TT], BF16)
    ub, ub_r = lp.sbuf("ub", [128, 8 * TPC], BF16)
    wib, wib_r = lp.sbuf("wib", [128, 8 * 4232], BF16)
    bis, bis_r = lp.sbuf("bis", [128, 33], F32)
    lgs, lgs_r = lp.sbuf("lgs", [128, 8], F32)
    lbs, lbs_r = lp.sbuf("lbs", [128, 8], F32)
    triU, triU_r = lp.sbuf("triU", [128, 128], BF16)
    triS, triS_r = lp.sbuf("triS", [128, 128], BF16)
    idf, idf_r = lp.sbuf("idf", [128, 128], BF16)
    idf32, idf32_r = lp.sbuf("idf32", [128, 128], F32)
    maskT, maskT_r = lp.sbuf("maskT", [128, 128], F32)
    abc, abc_r = lp.sbuf("abc", [128, 8], F32)
    dtbc, dtbc_r = lp.sbuf("dtbc", [128, 8], F32)
    dsk, dsk_r = lp.sbuf("dsk", [128, 8], F32)
    bcrow, bcrow_r = lp.sbuf("bcrow", [128, 128], F32)
    dtrow, dtrow_r = lp.sbuf("dtrow", [128, 8], F32)
    ar_s, ar_sr = lp.sbuf("ars", [1, 24], F32)
    birs, birs_r = lp.sbuf("birs", [1, 2184], F32)
    dtt, dtt_r = lp.sbuf("dtt", [128, 8], F32)
    dts, dts_r = lp.sbuf("dts", [128, 8], F32)
    att, att_r = lp.sbuf("att", [128, 8], F32)
    atb, atb_r = lp.sbuf("atb", [128, 8], BF16)
    cumc, cumc_r = lp.sbuf("cumc", [128, 8], F32)
    cumb, cumb_r = lp.sbuf("cumb", [128, 8], BF16)
    cumf, cumf_r = lp.sbuf("cumf", [128, 8], F32)
    cst, cst_r = lp.sbuf("cst", [1, 128], F32)
    sufc, sufc_r = lp.sbuf("sufc", [128, 8], F32)
    ecc, ecc_r = lp.sbuf("ecc", [128, 8], BF16)
    etot, etot_r = lp.sbuf("etot", [1, 8], F32)
    eclb, eclb_r = lp.sbuf("eclb", [128, 8], F32)
    drun, drun_r = lp.sbuf("drun", [1, 8], F32)
    bct, bct_r = lp.sbuf("bct", [128, 128], BF16)
    csb, csb_r = lp.sbuf("csb", [128, TT], BF16)
    bcf, bcf_r = lp.sbuf("bcf", [128, TT], BF16)
    xsf, xsf_r = lp.sbuf("xsf", [128, 16 * TT], BF16)
    xst, xst_r = lp.sbuf("xst", [128, 2048], BF16)
    xbb, xbb_r = lp.sbuf("xbb", [128, 2048], BF16)
    dxt, dxt_r = lp.sbuf("dxt", [128, 256], BF16)
    wxt, wxt_r = lp.sbuf("wxt", [128, 256], BF16)
    ltb, ltb_r = lp.sbuf("ltb", [128, 128], BF16)
    mtb, mtb_r = lp.sbuf("mtb", [128, 128], BF16)
    cef, cef_r = lp.sbuf("cef", [128, 128], BF16)
    hstf, hstf_r = lp.sbuf("hstf", [128, 1024], F32)
    hstb, hstb_r = lp.sbuf("hstb", [128, 1024], BF16)
    yis, yis_r = lp.sbuf("yis", [128, 2048], BF16)
    tsc, tsc_r = lp.sbuf("tsc", [128, TT], F32)
    tsd, tsd_r = lp.sbuf("tsd", [128, TT], F32)
    sqs, _ = lp.sbuf("sqs", [128, 2 * TT], F32)
    sq_r = (Res("sq0"), Res("sq1"))
    sts, sts_r = lp.sbuf("sts", [1, 4 * TT], F32)
    on1, on1_r = lp.sbuf("on1", [128, 1], F32)
    onb, onb_r = lp.sbuf("onb", [1, 128], F32)
    wrs, wrs_r = lp.sbuf("wrs", [128, 64], F32)
    brs, brs_r = lp.sbuf("brs", [1, 8], F32)
    brb, brb_r = lp.sbuf("brb", [128, 8], F32)
    lgr, lgr_r = lp.sbuf("lgr", [128, 128], F32)
    e8t, e8t_r = lp.sbuf("e8t", [128, 128], F32)
    mx8, mx8_r = lp.sbuf("mx8", [128, 16], F32)
    nm8, nm8_r = lp.sbuf("nm8", [128, 16], F32)
    sm8, sm8_r = lp.sbuf("sm8", [128, 16], F32)
    rc8, rc8_r = lp.sbuf("rc8", [128, 16], F32)
    wmt, wmt_r = lp.sbuf("wmt", [128, 128], F32)

    class Pool:
        def __init__(self):
            self.tiles = [lp.psum(f"pp{i}", [128, TT]) for i in range(6)]
            self.i = 0
        def get(self):
            t = self.tiles[self.i % 6]
            self.i += 1
            return t
    pool = Pool()
    psI, psI_r = lp.psum("psI", [128, 256])

    def c0(h):
        h.memset(on1[:, :], 1.0)
        return h.memset(onb[:, :], 1.0)
    g.op("vector", c0, writes=(on1_r, onb_r))
    for dst, dst_r, src_ in ((lgs, lgs_r, lng), (lbs, lbs_r, lnb),
                             (triU, triU_r, triU_i), (triS, triS_r, triS_i),
                             (idf, idf_r, idf_i), (idf32, idf32_r, idf32_i),
                             (maskT, maskT_r, maskT_i),
                             (bis, bis_r, binp)):
        ld_plain(lp, dst[:, :], dst_r, src_, eng="sync")
    ld_fm(lp, wib, wib_r, win, 8)
    ld_plain(lp, ar_s[:, 0:8], ar_sr, alog, eng="sync")
    ld_plain(lp, ar_s[:, 8:16], ar_sr, dtb, eng="sync")
    ld_plain(lp, ar_s[:, 16:24], ar_sr, dski, eng="sync")
    ld_plain(lp, birs[:, :], birs_r, binr[0:1, 2048:4232], eng="sync")

    psb0, psb0_r = pool.get()

    def bca(h, psb0=psb0):
        h.matmul(psb0[:, 0:24], onb[:, :], ar_s[:, :], start=True, stop=True)
        h.matmul(psb0[:, 24:32], onb[:, :], birs[0:1, 2176:2184],
                 start=True, stop=True)
        return h.matmul(psb0[:, 32:160], onb[:, :], birs[0:1, 2048:2176],
                        start=True, stop=True)
    g.op("tensor", bca, reads=(ar_sr, onb_r, birs_r), writes=(psb0_r,))

    def mka(h, psb0=psb0):
        h.activation(abc[:, :], psb0[:, 0:8], AF.Exp)
        h.activation(dsk[:, :], psb0[:, 16:24], AF.Copy)
        h.activation(dtbc[:, :], psb0[:, 8:16], AF.Copy)
        return h.activation(bcrow[:, :], psb0[:, 32:160], AF.Copy)
    g.op("scalar", mka, reads=(psb0_r,), writes=(abc_r, dsk_r, bcrow_r, dtbc_r))

    def mka2(h, psb0=psb0):
        h.tensor_scalar_mul(abc[:, :], abc[:, :], -1.0)
        return h.tensor_tensor(dtbc[:, :], dtbc[:, :], psb0[:, 24:32],
                               op=ALU.add)
    g.op("vector", mka2, reads=(psb0_r, abc_r, dtbc_r), writes=(abc_r, dtbc_r))

    for q4 in range(4):
        psb1, psb1_r = pool.get()

        def bxb(h, q4=q4, psb1=psb1):
            return h.matmul(psb1[:, 0:TT], onb[:, :],
                            birs[0:1, q4 * TT:(q4 + 1) * TT],
                            start=True, stop=True)
        g.op("tensor", bxb, reads=(onb_r, birs_r), writes=(psb1_r,))

        def cxb(h, q4=q4, psb1=psb1):
            return h.tensor_copy(xbb[:, q4 * TT:(q4 + 1) * TT], psb1[:, 0:TT])
        g.op("vector", cxb, reads=(psb1_r,), writes=(xbb_r,))

    def z0(h):
        h.memset(hstf[:, :], 0.0)
        h.memset(hstb[:, :], 0.0)
        return h.memset(drun[:, :], 1.0)
    g.op("vector", z0, writes=(hstf_r, hstb_r, drun_r))

    xar = x2a.rearrange("(kc p) t -> p kc t", p=128)
    y6r = y6.rearrange("(kc p) t -> p kc t", p=128)
    x2r = x2_o.rearrange("(kc p) t -> p kc t", p=128)
    xn2r = xn2_o.rearrange("(kc p) t -> p kc t", p=128)
    xn2br = xn2b_o.rearrange("(kc p) t -> p kc t", p=128)
    szr = sz_o.rearrange("(kc p) t -> p kc t", p=128)
    yidr = yid_o.rearrange("(kc p) t -> p kc t", p=128)

    for t in range(TPC // TT):
        def lda(h, t=t):
            return h.dma_start(out=xat[:, :].rearrange("p (kc t) -> p kc t", kc=8),
                               in_=xar[:, :, t * TT:(t + 1) * TT])
        g.dma("sync", lda, writes=(xat_r,))

        def ldy(h, t=t):
            return h.dma_start(out=x2t[:, :].rearrange("p (kc t) -> p kc t", kc=8),
                               in_=y6r[:, :, t * TT:(t + 1) * TT])
        g.dma("sync", ldy, writes=(x2t_r,))

        def addx(h):
            ins = None
            for fc in range(8):
                ins = h.tensor_tensor(x2t[:, fc * TT:(fc + 1) * TT],
                                      xat[:, fc * TT:(fc + 1) * TT],
                                      x2t[:, fc * TT:(fc + 1) * TT], op=ALU.add)
            return ins
        g.op("vector", addx, reads=(xat_r,), writes=(x2t_r,))

        def stx2(h, t=t):
            return h.dma_start(out=x2r[:, :, t * TT:(t + 1) * TT],
                               in_=x2t[:, :].rearrange("p (kc t) -> p kc t", kc=8))
        g.dma("gpsimd", stx2, reads=(x2t_r,), writes=(x2_or,))

        emit_ln(lp, pool, x2t, x2t_r, xnt, xnt_r, on1, on1_r, onb, onb_r,
                sqs, sq_r, sts, sts_r, 8, 0, TT, 0, TT)

        def stxn(h, t=t):
            return h.dma_start(out=xn2r[:, :, t * TT:(t + 1) * TT],
                               in_=xnt[:, :].rearrange("p (kc t) -> p kc t", kc=8))
        g.dma("gpsimd", stxn, reads=(xnt_r,), writes=(xn2_or,))

        def cvb(h):
            ins = None
            for fc in range(8):
                ins = h.tensor_copy(xnbt[:, fc * TT:(fc + 1) * TT],
                                    xnt[:, fc * TT:(fc + 1) * TT])
            return ins
        g.op("vector", cvb, reads=(xnt_r,), writes=(xnbt_r,))

        def stxb(h, t=t):
            return h.dma_start(out=xn2br[:, :, t * TT:(t + 1) * TT],
                               in_=xnbt[:, :].rearrange("p (kc t) -> p kc t", kc=8))
        g.dma("gpsimd", stxb, reads=(xnbt_r,), writes=(xn2b_or,))

        emit_ln(lp, pool, xnt, xnt_r, ub, ub_r, on1, on1_r, onb, onb_r,
                sqs, sq_r, sts, sts_r, 8, 0, TT, t * TT, TPC,
                gam=lgs, gam_r=lgs_r, bet=lbs, bet_r=lbs_r)

    for t in range(TPC // TT):
        for fc in range(16):
            ps, psr = pool.get()

            def mmz(h, fc=fc, t=t, ps=ps):
                ins = None
                for kc in range(8):
                    ins = h.matmul(ps[:, 0:TT],
                                   wib[:, kc * 4232 + fc * 128: kc * 4232 + (fc + 1) * 128],
                                   ub[:, kc * TPC + t * TT: kc * TPC + t * TT + TT],
                                   start=(kc == 0), stop=(kc == 7))
                return ins
            g.op("tensor", mmz, reads=(ub_r, wib_r), writes=(psr,))

            def evz(h, fc=fc, ps=ps):
                return h.activation(tsc[:, :], ps[:, 0:TT], AF.Silu,
                                    bias=bis[:, fc:fc + 1])
            g.op("scalar", evz, reads=(psr, bis_r), writes=(tsc_r,))

            def stz(h, fc=fc, t=t):
                return h.dma_start(out=szr[:, fc, t * TT:(t + 1) * TT],
                                   in_=tsc[:, :])
            g.dma("gpsimd", stz, reads=(tsc_r,), writes=(sz_or,))

    NCH = TPC // 128
    for ch in range(NCH):
        t = ch // 4
        c0_ = (ch % 4) * 128
        if ch % 4 == 0:
            for fc in range(16):
                ps, psr = pool.get()

                def mmxf(h, fc=fc, t=t, ps=ps):
                    ins = None
                    for kc in range(8):
                        ins = h.matmul(
                            ps[:, 0:TT],
                            wib[:, kc * 4232 + 2048 + fc * 128:
                                kc * 4232 + 2048 + (fc + 1) * 128],
                            ub[:, kc * TPC + t * TT: kc * TPC + t * TT + TT],
                            start=(kc == 0), stop=(kc == 7))
                    return ins
                g.op("tensor", mmxf, reads=(ub_r, wib_r), writes=(psr,))

                def evxf(h, fc=fc, ps=ps):
                    return h.activation(xsf[:, fc * TT:(fc + 1) * TT], ps[:, 0:TT],
                                        AF.Identity, bias=bis[:, 16 + fc:17 + fc])
                g.op("scalar", evxf, reads=(psr, bis_r), writes=(xsf_r,))

            psbc, psbc_r = pool.get()

            def mmbc(h, t=t, psbc=psbc):
                ins = None
                for kc in range(8):
                    ins = h.matmul(psbc[:, 0:TT],
                                   wib[:, kc * 4232 + 4096: kc * 4232 + 4224],
                                   ub[:, kc * TPC + t * TT: kc * TPC + t * TT + TT],
                                   start=(kc == 0), stop=(kc == 7))
                return ins
            g.op("tensor", mmbc, reads=(ub_r, wib_r), writes=(psbc_r,))

            def evbc(h, psbc=psbc):
                return h.activation(bcf[:, :], psbc[:, 0:TT], AF.Identity,
                                    bias=bis[:, 32:33])
            g.op("scalar", evbc, reads=(psbc_r, bis_r), writes=(bcf_r,))

            def cvcs(h):
                h.tensor_copy(csb[0:64, :], bcf[64:128, :])
                return h.tensor_copy(csb[64:128, :], bcf[64:128, :])
            g.op("vector", cvcs, reads=(bcf_r,), writes=(csb_r,))

        psd, psd_r = pool.get()

        def mmdt(h, t=t, c0_=c0_, psd=psd):
            ins = None
            for kc in range(8):
                ins = h.matmul(
                    psd[:, 0:8],
                    ub[:, kc * TPC + t * TT + c0_: kc * TPC + t * TT + c0_ + 128],
                    wib[:, kc * 4232 + 4224: kc * 4232 + 4232],
                    start=(kc == 0), stop=(kc == 7))
            return ins
        g.op("tensor", mmdt, reads=(ub_r, wib_r), writes=(psd_r,))

        def spa(h, psd=psd):
            return h.tensor_tensor(dtt[:, :], psd[:, 0:8], dtbc[:, :], op=ALU.add)
        g.op("vector", spa, reads=(psd_r, dtbc_r), writes=(dtt_r,))

        def spb(h):
            h.activation(tsd[:, 0:8], dtt[:, :], AF.Abs)
            return h.activation(tsc[:, 0:8], dtt[:, :], AF.Relu)
        g.op("scalar", spb, reads=(dtt_r,), writes=(tsd_r, tsc_r))

        def spc(h):
            h.activation(tsd[:, 8:16], tsd[:, 0:8], AF.Exp, scale=-1.0)
            return h.activation(tsd[:, 16:24], tsd[:, 8:16], AF.Ln, bias=1.0)
        g.op("scalar", spc, reads=(tsd_r,), writes=(tsd_r,))

        def spd1(h):
            return h.tensor_tensor(dtt[:, :], tsc[:, 0:8], tsd[:, 16:24],
                                   op=ALU.add)
        g.op("vector", spd1, reads=(tsc_r, tsd_r), writes=(dtt_r,), fence=True)

        def spd2(h):
            return h.tensor_tensor(att[:, :], dtt[:, :], abc[:, :], op=ALU.mult)
        g.op("vector", spd2, reads=(dtt_r, abc_r), writes=(att_r,), fence=True)

        def spd3(h):
            return h.activation(atb[:, :], att[:, :], AF.Copy)
        g.op("scalar", spd3, reads=(att_r,), writes=(atb_r,))

        psc, psc_r = pool.get()

        def mmcu(h, psc=psc):
            h.matmul(psc[:, 0:8], triU[:, :], atb[:, :], start=True, stop=True)
            return h.matmul(psc[:, 128:136], triS[:, :], atb[:, :],
                            start=True, stop=True)
        g.op("tensor", mmcu, reads=(atb_r, triU_r, triS_r), writes=(psc_r,))

        def evcu(h, psc=psc):
            h.activation(cumc[:, :], psc[:, 0:8], AF.Copy)
            h.activation(ecc[:, :], psc[:, 0:8], AF.Exp)
            return h.activation(sufc[:, :], psc[:, 128:136], AF.Exp)
        g.op("scalar", evcu, reads=(psc_r,), writes=(cumc_r, ecc_r, sufc_r))

        def mkds(h):
            return h.tensor_tensor(dts[:, :], dtt[:, :], sufc[:, :], op=ALU.mult)
        g.op("vector", mkds, reads=(dtt_r, sufc_r), writes=(dts_r,))

        pst_, pst_r2 = pool.get()

        def mmtt(h, pst_=pst_):
            return h.matmul(pst_[0:1, 0:8], on1[:, :], att[:, :],
                            start=True, stop=True)
        g.op("tensor", mmtt, reads=(att_r, on1_r), writes=(pst_r2,))

        def evtt(h, ch=ch, pst_=pst_):
            h.activation(etot[:, :], pst_[0:1, 0:8], AF.Exp)
            return h.activation(cst[0:1, ch * 8:(ch + 1) * 8], pst_[0:1, 0:8],
                                AF.Copy)
        g.op("scalar", evtt, reads=(pst_r2,), writes=(etot_r, cst_r))

        pse_, pse_r = pool.get()

        def mmeb2(h, pse_=pse_):
            return h.matmul(pse_[:, 0:8], onb[:, :], etot[0:1, :],
                            start=True, stop=True)
        g.op("tensor", mmeb2, reads=(etot_r, onb_r), writes=(pse_r,))

        def evel(h, pse_=pse_):
            return h.activation(eclb[:, :], pse_[:, 0:8], AF.Copy)
        g.op("scalar", evel, reads=(pse_r,), writes=(eclb_r,))

        def updr(h):
            return h.tensor_tensor(drun[:, :], drun[:, :], etot[:, :], op=ALU.mult)
        g.op("vector", updr, reads=(etot_r, drun_r), writes=(drun_r,))

        for q4 in range(4):
            psx, psx_r = pool.get()

            def mmxs(h, t=t, c0_=c0_, q4=q4, psx=psx):
                ins = None
                for kc in range(8):
                    ins = h.matmul(
                        psx[:, 0:TT],
                        ub[:, kc * TPC + t * TT + c0_: kc * TPC + t * TT + c0_ + 128],
                        wib[:, kc * 4232 + 2048 + q4 * TT:
                            kc * 4232 + 2048 + (q4 + 1) * TT],
                        start=(kc == 0), stop=(kc == 7))
                return ins
            g.op("tensor", mmxs, reads=(ub_r, wib_r), writes=(psx_r,))

            def evxs(h, q4=q4, psx=psx):
                return h.tensor_tensor(xst[:, q4 * TT:(q4 + 1) * TT], psx[:, 0:TT],
                                       xbb[:, q4 * TT:(q4 + 1) * TT], op=ALU.add)
            g.op("vector", evxs, reads=(psx_r, xbb_r), writes=(xst_r,))

        psb2, psb2_r = pool.get()

        def mmbt(h, t=t, c0_=c0_, psb2=psb2):
            ins = None
            for kc in range(8):
                ins = h.matmul(
                    psb2[:, 0:128],
                    ub[:, kc * TPC + t * TT + c0_: kc * TPC + t * TT + c0_ + 128],
                    wib[:, kc * 4232 + 4096: kc * 4232 + 4224],
                    start=(kc == 0), stop=(kc == 7))
            return ins
        g.op("tensor", mmbt, reads=(ub_r, wib_r), writes=(psb2_r,))

        def evbt(h, psb2=psb2):
            return h.tensor_tensor(bct[:, :], psb2[:, 0:128], bcrow[:, :],
                                   op=ALU.add)
        g.op("vector", evbt, reads=(psb2_r, bcrow_r), writes=(bct_r,))

        for hd in range(8):
            psm, psm_r = pool.get()

            def mmcb(h, c0_=c0_, psm=psm):
                return h.matmul(psm[:, 0:128], bcf[0:64, c0_:c0_ + 128],
                                csb[0:64, c0_:c0_ + 128], start=True, stop=True)
            g.op("tensor", mmcb, reads=(bcf_r, csb_r), writes=(psm_r,))

            psl, psl_r = pool.get()

            def mmli(h, hd=hd, psl=psl):
                return h.matmul(psl[0:1, 0:128], cumc[:, hd:hd + 1], idf32[:, :],
                                start=True, stop=True)
            g.op("tensor", mmli, reads=(cumc_r, idf32_r), writes=(psl_r,))

            def evli(h, psl=psl):
                return h.activation(sts[0:1, 3 * TT: 3 * TT + 128],
                                    psl[0:1, 0:128], AF.Copy)
            g.op("scalar", evli, reads=(psl_r,), writes=(sts_r,))

            psl2, psl2_r = pool.get()

            def mmlb(h, psl2=psl2):
                return h.matmul(psl2[:, 0:128], onb[:, :],
                                sts[0:1, 3 * TT: 3 * TT + 128],
                                start=True, stop=True)
            g.op("tensor", mmlb, reads=(sts_r, onb_r), writes=(psl2_r,))

            def evlt(h, hd=hd, psl2=psl2):
                return h.scalar_tensor_tensor(tsd[:, 0:128], psl2[:, 0:128],
                                              cumc[:, hd:hd + 1], maskT[:, :],
                                              op0=ALU.subtract, op1=ALU.add)
            g.op("vector", evlt, reads=(psl2_r, cumc_r, maskT_r), writes=(tsd_r,))

            def exlt(h):
                return h.activation(ltb[:, :], tsd[:, 0:128], AF.Exp)
            g.op("scalar", exlt, reads=(tsd_r,), writes=(ltb_r,))

            def mmt2(h, psm=psm):
                return h.tensor_tensor(mtb[:, :], psm[:, 0:128], ltb[:, :],
                                       op=ALU.mult)
            g.op("vector", mmt2, reads=(psm_r, ltb_r), writes=(mtb_r,))

            def mkdx(h, hd=hd):
                h.tensor_scalar(dxt[:, :], xst[:, hd * 256:(hd + 1) * 256],
                                dtt[:, hd:hd + 1], None, op0=ALU.mult)
                return h.tensor_scalar(wxt[:, :], xst[:, hd * 256:(hd + 1) * 256],
                                       dts[:, hd:hd + 1], None, op0=ALU.mult)
            g.op("vector", mkdx, reads=(xst_r, dtt_r, dts_r),
                 writes=(dxt_r, wxt_r))

            psc2, psc2_r = pool.get()

            def mmec(h, hd=hd, psc2=psc2):
                return h.matmul(psc2[0:1, 0:128], ecc[:, hd:hd + 1], idf[:, :],
                                start=True, stop=True)
            g.op("tensor", mmec, reads=(ecc_r, idf_r), writes=(psc2_r,))

            def evec(h, psc2=psc2):
                return h.activation(sts[0:1, 3 * TT + 128: 3 * TT + 256],
                                    psc2[0:1, 0:128], AF.Copy)
            g.op("scalar", evec, reads=(psc2_r,), writes=(sts_r,))

            psc3, psc3_r = pool.get()

            def mmeb(h, psc3=psc3):
                return h.matmul(psc3[:, 0:128], onb[:, :],
                                sts[0:1, 3 * TT + 128: 3 * TT + 256],
                                start=True, stop=True)
            g.op("tensor", mmeb, reads=(sts_r, onb_r), writes=(psc3_r,))

            def evcef(h, c0_=c0_, psc3=psc3):
                return h.tensor_tensor(cef[:, :], csb[:, c0_:c0_ + 128],
                                       psc3[:, 0:128], op=ALU.mult)
            g.op("vector", evcef, reads=(csb_r, psc3_r), writes=(cef_r,))

            def stce(h, ch=ch, hd=hd):
                return h.dma_start(out=ce_o[:, (ch * 8 + hd) * 128:
                                            (ch * 8 + hd) * 128 + 128],
                                   in_=cef[:, :])
            g.dma("gpsimd", stce, reads=(cef_r,), writes=(ce_or,))

            pr = (hd % 2) * 64
            co = (hd // 2) * 256

            def mmin(h, hd=hd, pr=pr, co=co):
                ins = None
                for ph in range(2):
                    h.matmul(psI[:, ph * 128:(ph + 1) * 128],
                             dxt[:, ph * 128:(ph + 1) * 128],
                             mtb[:, :], start=True, stop=False)
                    ins = h.matmul(psI[:, ph * 128:(ph + 1) * 128],
                                   hstb[pr:pr + 64, co + ph * 128: co + (ph + 1) * 128],
                                   cef[pr:pr + 64, :], start=False, stop=True)
                return ins
            g.op("tensor", mmin, reads=(dxt_r, mtb_r, hstb_r, cef_r),
                 writes=(psI_r,))

            def evyi(h, hd=hd, c0_=c0_):
                h.scalar_tensor_tensor(yis[:, hd * 256: hd * 256 + 128],
                                       xsf[:, hd * 2 * TT + c0_:
                                           hd * 2 * TT + c0_ + 128],
                                       dsk[:, hd:hd + 1], psI[:, 0:128],
                                       op0=ALU.mult, op1=ALU.add)
                return h.scalar_tensor_tensor(
                    yis[:, hd * 256 + 128: hd * 256 + 256],
                    xsf[:, (hd * 2 + 1) * TT + c0_: (hd * 2 + 1) * TT + c0_ + 128],
                    dsk[:, hd:hd + 1], psI[:, 128:256],
                    op0=ALU.mult, op1=ALU.add)
            g.op("vector", evyi, reads=(psI_r, xsf_r, dsk_r), writes=(yis_r,))

            psD2, psD2_r = pool.get()

            def mmds(h, hd=hd, psD2=psD2):
                return h.matmul(psD2[0:64, 0:256], bct[:, 0:64],
                                wxt[:, :], start=True, stop=True)
            g.op("tensor", mmds, reads=(bct_r, wxt_r), writes=(psD2_r,))

            def upst(h, hd=hd, pr=pr, co=co, psD2=psD2):
                h.scalar_tensor_tensor(hstb[pr:pr + 64, co:co + 256],
                                       hstf[pr:pr + 64, co:co + 256],
                                       eclb[pr:pr + 64, hd:hd + 1],
                                       psD2[0:64, 0:256],
                                       op0=ALU.mult, op1=ALU.add)
                return h.scalar_tensor_tensor(hstf[pr:pr + 64, co:co + 256],
                                              hstf[pr:pr + 64, co:co + 256],
                                              eclb[pr:pr + 64, hd:hd + 1],
                                              psD2[0:64, 0:256],
                                              op0=ALU.mult, op1=ALU.add)
            g.op("vector", upst, reads=(psD2_r, eclb_r), writes=(hstf_r, hstb_r))

        def styi(h, ch=ch, t=t, c0_=c0_):
            return h.dma_start(
                out=yidr[:, :, t * TT + c0_: t * TT + c0_ + 128],
                in_=yis[:, :].rearrange("p (kc t) -> p kc t", kc=16))
        g.dma("gpsimd", styi, reads=(yis_r,), writes=(yid_or,))

    def sth(h):
        return h.dma_start(out=hloc_o[:, :], in_=hstf[:, :])
    g.dma("gpsimd", sth, reads=(hstf_r,), writes=(hloc_or,))

    def std(h):
        return h.dma_start(out=dcor_o[:, :], in_=drun[:, :])
    g.dma("gpsimd", std, reads=(drun_r,), writes=(dcor_or,))

    def stcs(h):
        return h.dma_start(out=csum_o[:, :], in_=cst[:, :])
    g.dma("gpsimd", stcs, reads=(cst_r,), writes=(csum_or,))

    def fin(h):
        h.nop()
        return None
    g.op("vector", fin, reads=(x2_or, xn2_or, xn2b_or, sz_or, yid_or, ce_or,
                               hloc_or, dcor_or, csum_or))
    return lp


def host_A9a(inputs, cores, prev):
    import ml_dtypes
    bf = ml_dtypes.bfloat16
    b_in = np.asarray(inputs["L2_attn_b_in"], np.float32)
    binp = pp(b_in[0:4224])
    idx = np.arange(128)
    triU = (idx[:, None] <= idx[None, :]).astype(bf)
    triS = (idx[:, None] > idx[None, :]).astype(bf)
    maskT = np.where(idx[:, None] <= idx[None, :], 0.0, -1e30).astype(np.float32)
    base = {
        "win": np.ascontiguousarray(np.asarray(inputs["L2_attn_W_in"])),
        "binr": np.ascontiguousarray(b_in.reshape(1, 4232)),
        "binp": binp,
        "lng": pp(np.asarray(inputs["L2_attn_ln_gamma"])),
        "lnb": pp(np.asarray(inputs["L2_attn_ln_beta"])),
        "alog": np.ascontiguousarray(np.asarray(inputs["L2_attn_A_log"],
                                                np.float32).reshape(1, 8)),
        "dtb": np.ascontiguousarray(np.asarray(inputs["L2_attn_dt_bias"],
                                               np.float32).reshape(1, 8)),
        "dski": np.ascontiguousarray(np.asarray(inputs["L2_attn_D_skip"],
                                                np.float32).reshape(1, 8)),
        "triUi": triU, "triSi": triS, "maskTi": maskT,
        "idfi": np.eye(128).astype(bf),
        "idf32i": np.eye(128, dtype=np.float32),
    }
    return [dict(base, x2a=prev["x2a_o"][c], y6=prev["y6_o"][c]) for c in cores]


def build_A9b(nexp=34):
    lp = LP()
    g = lp.g
    xn2b_i = lp.dram_in("xn2b", [DM, TPC], BF16)
    xn2f_i = lp.dram_in("xn2f", [DM, TPC], F32)
    wg_i = lp.dram_in("wgw", [DM, 32], F32)
    if nexp > 0:
        w13_i = lp.dram_in("w13", [34 * DM, 1024], BF16)
        w2_i = lp.dram_in("w2e", [34 * 512, 1024], BF16)
    idf32_i = lp.dram_in("idf32i", [128, 128], F32)
    l2f_o, l2f_or = lp.dram_out("l2f_o", [8 * 128, TPC], F32)

    xb, xb_r = lp.sbuf("xb", [128, 8 * TPC], BF16)
    wgs, wgs_r = lp.sbuf("wgs", [128, 8 * 32], F32)
    idf32, idf32_r = lp.sbuf("idf32", [128, 128], F32)
    onb, onb_r = lp.sbuf("onb", [1, 128], F32)
    lg, lg_r = lp.sbuf("lg", [128, 512], F32)
    mxc, mxc_r = lp.sbuf("mxc", [128, 16], F32)
    nmc, nmc_r = lp.sbuf("nmc", [128, 16], F32)
    smc, smc_r = lp.sbuf("smc", [128, 16], F32)
    rcc, rcc_r = lp.sbuf("rcc", [128, 16], F32)
    sc, sc_r = lp.sbuf("sc", [128, 512], F32)
    pk, pk_r = lp.sbuf("pk", [128, 512], F32)
    tmp, tmp_r = lp.sbuf("tmp", [128, 512], F32)
    wm, wm_r = lp.sbuf("wm", [128, 512], F32)
    wmT, wmT_r = lp.sbuf("wmT", [32, TPC], F32)
    grow, grow_r = lp.sbuf("grow", [1, TPC], F32)
    geb, geb_r = lp.sbuf("geb", [128, TPC], BF16)
    w13b = []
    w13b_r = []
    w2b = []
    w2b_r = []
    for s in range(2):
        t, r = lp.sbuf(f"w13b{s}", [128, 8 * 1024], BF16)
        w13b.append(t)
        w13b_r.append(r)
        t, r = lp.sbuf(f"w2b{s}", [128, 4 * 1024], BF16)
        w2b.append(t)
        w2b_r.append(r)
    gb, gb_r = lp.sbuf("gb", [128, 2048], BF16)
    ub, ub_r = lp.sbuf("ub", [128, 2048], BF16)
    hb, hb_r = lp.sbuf("hb", [128, 2048], BF16)
    yacc, yacc_r = lp.sbuf("yacc", [128, 8 * TPC], F32)

    class Pool:
        def __init__(self):
            self.tiles = [lp.psum(f"pp{i}", [128, TT]) for i in range(6)]
            self.i = 0

        def get(self):
            t = self.tiles[self.i % 6]
            self.i += 1
            return t
    pool = Pool()

    def c0(h):
        return h.memset(onb[:, :], 1.0)
    g.op("vector", c0, writes=(onb_r,))

    ld_fm(lp, yacc, yacc_r, xn2f_i, 8, eng="sync")
    ld_fm(lp, xb, xb_r, xn2b_i, 8, eng="sync")
    ld_fm(lp, wgs, wgs_r, wg_i, 8, eng="sync")
    ld_plain(lp, idf32[:, :], idf32_r, idf32_i, eng="sync")

    for tc in range(16):
        ps, psr = pool.get()

        def mmg(h, tc=tc, ps=ps):
            ins = None
            for kc in range(8):
                ins = h.matmul(ps[:, 0:32],
                               yacc[:, kc * TPC + tc * 128: kc * TPC + tc * 128 + 128],
                               wgs[:, kc * 32:(kc + 1) * 32],
                               start=(kc == 0), stop=(kc == 7))
            return ins
        g.op("tensor", mmg, reads=(yacc_r, wgs_r), writes=(psr,))

        def evg(h, tc=tc, ps=ps):
            return h.activation(lg[:, tc * 32:(tc + 1) * 32], ps[:, 0:32],
                                AF.Copy)
        g.op("scalar", evg, reads=(psr,), writes=(lg_r,))

    for tc in range(16):
        def rmx(h, tc=tc):
            return h.tensor_reduce(mxc[:, tc:tc + 1], lg[:, tc * 32:(tc + 1) * 32],
                                   axis=mybir.AxisListType.X, op=ALU.max)
        g.op("vector", rmx, reads=(lg_r,), writes=(mxc_r,))
    for tc in range(16):
        def ngx(h, tc=tc):
            return h.activation(nmc[:, tc:tc + 1], mxc[:, tc:tc + 1], AF.Copy,
                                scale=-1.0)
        g.op("scalar", ngx, reads=(mxc_r,), writes=(nmc_r,))
    for tc in range(16):
        def exg(h, tc=tc):
            return h.activation(sc[:, tc * 32:(tc + 1) * 32],
                                lg[:, tc * 32:(tc + 1) * 32], AF.Exp,
                                bias=nmc[:, tc:tc + 1])
        g.op("scalar", exg, reads=(lg_r, nmc_r), writes=(sc_r,))
    for tc in range(16):
        def rsm(h, tc=tc):
            return h.tensor_reduce(smc[:, tc:tc + 1], sc[:, tc * 32:(tc + 1) * 32],
                                   axis=mybir.AxisListType.X, op=ALU.add)
        g.op("vector", rsm, reads=(sc_r,), writes=(smc_r,), fence=(tc == 15))

    def rcp(h):
        return h.reciprocal(rcc[:, :], smc[:, :])
    g.op("vector", rcp, reads=(smc_r,), writes=(rcc_r,), fence=True)

    for tc in range(16):
        def nrm(h, tc=tc):
            return h.tensor_scalar(sc[:, tc * 32:(tc + 1) * 32],
                                   sc[:, tc * 32:(tc + 1) * 32],
                                   rcc[:, tc:tc + 1], None, op0=ALU.mult)
        g.op("vector", nrm, reads=(sc_r, rcc_r), writes=(sc_r,))

    def zwm(h):
        return h.memset(wm[:, :], 0.0)
    g.op("vector", zwm, writes=(wm_r,))
    for rnd in range(4):
        for tc in range(16):
            def rm2(h, tc=tc):
                return h.tensor_reduce(mxc[:, tc:tc + 1],
                                       sc[:, tc * 32:(tc + 1) * 32],
                                       axis=mybir.AxisListType.X, op=ALU.max)
            g.op("vector", rm2, reads=(sc_r,), writes=(mxc_r,))
        for tc in range(16):
            def pck(h, tc=tc):
                return h.tensor_scalar(pk[:, tc * 32:(tc + 1) * 32],
                                       sc[:, tc * 32:(tc + 1) * 32],
                                       mxc[:, tc:tc + 1], None, op0=ALU.is_ge)
            g.op("vector", pck, reads=(sc_r, mxc_r), writes=(pk_r,))

        def acc(h):
            h.tensor_tensor(tmp[:, :], pk[:, :], sc[:, :], op=ALU.mult)
            h.tensor_tensor(wm[:, :], wm[:, :], tmp[:, :], op=ALU.add)
            h.tensor_scalar(tmp[:, :], pk[:, :], -2.0, None, op0=ALU.mult)
            return h.tensor_tensor(sc[:, :], sc[:, :], tmp[:, :], op=ALU.add)
        g.op("vector", acc, reads=(pk_r, sc_r, wm_r, tmp_r),
             writes=(tmp_r, wm_r, sc_r), fence=True)

    for q4 in range(4):
        ps, psr = pool.get()

        def mmt(h, q4=q4, ps=ps):
            ins = None
            for j in range(4):
                tc = q4 * 4 + j
                ins = h.matmul(ps[0:32, j * 128:(j + 1) * 128],
                               wm[:, tc * 32:(tc + 1) * 32], idf32[:, :],
                               start=True, stop=True)
            return ins
        g.op("tensor", mmt, reads=(wm_r, idf32_r), writes=(psr,))

        def evt(h, q4=q4, ps=ps):
            return h.activation(wmT[0:32, q4 * 512:(q4 + 1) * 512],
                                ps[0:32, 0:512], AF.Copy)
        g.op("scalar", evt, reads=(psr,), writes=(wmT_r,))

    def zy(h):
        ins = None
        for q in range(4):
            ins = h.memset(yacc[:, q * 4096:(q + 1) * 4096], 0.0)
        return ins
    g.op("vector", zy, reads=(yacc_r,), writes=(yacc_r,))

    if nexp > 0:
        w13r = w13_i.rearrange("(e kc p) f -> e p kc f", e=34, p=128)
        w2r = w2_i.rearrange("(e fc p) d -> e p fc d", e=34, p=128)
    for e in range(nexp):
        sl = e % 2
        gated = e < 32

        def ldw13(h, e=e, sl=sl):
            return h.dma_start(
                out=w13b[sl][:, :].rearrange("p (kc f) -> p kc f", kc=8),
                in_=w13r[e])
        g.dma("sync", ldw13, writes=(w13b_r[sl],))

        def ldw2(h, e=e, sl=sl):
            return h.dma_start(
                out=w2b[sl][:, :].rearrange("p (fc d) -> p fc d", fc=4),
                in_=w2r[e])
        g.dma("scalar", ldw2, writes=(w2b_r[sl],))

        if gated:
            for q4 in range(4):
                ps, psr = pool.get()

                def mmsel(h, e=e, q4=q4, ps=ps):
                    return h.matmul(ps[0:1, 0:512], idf32[0:32, e:e + 1],
                                    wmT[0:32, q4 * 512:(q4 + 1) * 512],
                                    start=True, stop=True)
                g.op("tensor", mmsel, reads=(idf32_r, wmT_r), writes=(psr,))

                def evsel(h, q4=q4, ps=ps):
                    return h.activation(grow[0:1, q4 * 512:(q4 + 1) * 512],
                                        ps[0:1, 0:512], AF.Copy)
                g.op("scalar", evsel, reads=(psr,), writes=(grow_r,))
            for q4 in range(4):
                ps, psr = pool.get()

                def mmbr(h, q4=q4, ps=ps):
                    return h.matmul(ps[:, 0:512], onb[:, :],
                                    grow[0:1, q4 * 512:(q4 + 1) * 512],
                                    start=True, stop=True)
                g.op("tensor", mmbr, reads=(grow_r, onb_r), writes=(psr,))

                def evbr(h, q4=q4, ps=ps):
                    return h.activation(geb[:, q4 * 512:(q4 + 1) * 512],
                                        ps[:, 0:512], AF.Copy)
                g.op("scalar", evbr, reads=(psr,), writes=(geb_r,))

        for tt in range(4):
            for fbi in range(8):
                ps, psr = pool.get()

                def mm1(h, sl=sl, tt=tt, fbi=fbi, ps=ps):
                    ins = None
                    for kc in range(8):
                        ins = h.matmul(
                            ps[:, 0:TT],
                            w13b[sl][:, kc * 1024 + fbi * 128:
                                     kc * 1024 + (fbi + 1) * 128],
                            xb[:, kc * TPC + tt * TT: kc * TPC + (tt + 1) * TT],
                            start=(kc == 0), stop=(kc == 7))
                    return ins
                g.op("tensor", mm1, reads=(w13b_r[sl], xb_r), writes=(psr,))

                if fbi < 4:
                    def evg1(h, fbi=fbi, ps=ps):
                        return h.activation(gb[:, fbi * TT:(fbi + 1) * TT],
                                            ps[:, 0:TT], AF.Silu)
                    g.op("scalar", evg1, reads=(psr,), writes=(gb_r,))
                elif gated:
                    def evu1(h, tt=tt, fbi=fbi, ps=ps):
                        return h.tensor_tensor(ub[:, (fbi - 4) * TT:(fbi - 3) * TT],
                                               ps[:, 0:TT],
                                               geb[:, tt * TT:(tt + 1) * TT],
                                               op=ALU.mult)
                    g.op("vector", evu1, reads=(psr, geb_r), writes=(ub_r,))
                else:
                    def evu2(h, fbi=fbi, ps=ps):
                        return h.activation(ub[:, (fbi - 4) * TT:(fbi - 3) * TT],
                                            ps[:, 0:TT], AF.Copy)
                    g.op("scalar", evu2, reads=(psr,), writes=(ub_r,))

            def mkh(h):
                return h.tensor_tensor(hb[:, :], gb[:, :], ub[:, :], op=ALU.mult)
            g.op("vector", mkh, reads=(gb_r, ub_r), writes=(hb_r,))

            for dc in range(8):
                ps, psr = pool.get()

                def mm2(h, sl=sl, dc=dc, ps=ps):
                    ins = None
                    for fb in range(4):
                        ins = h.matmul(
                            ps[:, 0:TT],
                            w2b[sl][:, fb * 1024 + dc * 128:
                                    fb * 1024 + (dc + 1) * 128],
                            hb[:, fb * TT:(fb + 1) * TT],
                            start=(fb == 0), stop=(fb == 3))
                    return ins
                g.op("tensor", mm2, reads=(w2b_r[sl], hb_r), writes=(psr,))

                def evy(h, dc=dc, tt=tt, ps=ps):
                    return h.tensor_tensor(
                        yacc[:, dc * TPC + tt * TT: dc * TPC + (tt + 1) * TT],
                        ps[:, 0:TT],
                        yacc[:, dc * TPC + tt * TT: dc * TPC + (tt + 1) * TT],
                        op=ALU.add)
                g.op("vector", evy, reads=(psr, yacc_r), writes=(yacc_r,))

    st_fm(lp, l2f_o, l2f_or, yacc, 8, eng="gpsimd", reads=(yacc_r,))

    def fin(h):
        h.nop()
        return None
    g.op("vector", fin, reads=(l2f_or,))
    return lp


def host_A9b(inputs, cores, prev, nexp=34):
    import ml_dtypes
    bf = ml_dtypes.bfloat16
    W1 = np.asarray(inputs["L2_ffn_W1"], np.float32)
    W2 = np.asarray(inputs["L2_ffn_W2"], np.float32)
    W3 = np.asarray(inputs["L2_ffn_W3"], np.float32)
    Ws1 = np.asarray(inputs["L2_ffn_Ws1"], np.float32)
    Ws2 = np.asarray(inputs["L2_ffn_Ws2"], np.float32)
    Ws3 = np.asarray(inputs["L2_ffn_Ws3"], np.float32)
    w13 = np.zeros((34, 1024, 1024), bf)
    w2e = np.zeros((34, 512, 1024), bf)
    for e in range(32):
        w13[e, :, 0:512] = W1[e]
        w13[e, :, 512:1024] = W3[e]
        w2e[e] = W2[e]
    w13[32, :, 0:512] = Ws1[:, 0:512]
    w13[32, :, 512:1024] = Ws3[:, 0:512]
    w13[33, :, 0:512] = Ws1[:, 512:1024]
    w13[33, :, 512:1024] = Ws3[:, 512:1024]
    w2e[32] = Ws2[0:512]
    w2e[33] = Ws2[512:1024]
    base = {
        "wgw": np.ascontiguousarray(np.asarray(inputs["L2_ffn_Wg"], np.float32)),
        "idf32i": np.eye(128, dtype=np.float32),
    }
    if nexp > 0:
        base["w13"] = np.ascontiguousarray(w13.reshape(34 * 1024, 1024))
        base["w2e"] = np.ascontiguousarray(w2e.reshape(34 * 512, 1024))
    return [dict(base, xn2b=prev["xn2b_o"][c], xn2f=prev["xn2_o"][c])
            for c in cores]


def build_A10():
    lp = LP()
    g = lp.g
    yid_i = lp.dram_in("yid", [2048, TPC], BF16)
    sz_i = lp.dram_in("szd", [2048, TPC], BF16)
    ce_i = lp.dram_in("ced", [128, 16384], BF16)
    hstk_i = lp.dram_in("hstk", [3 * 128, 1024], F32)
    dstk_i = lp.dram_in("dstk", [3 * 128, 4], F32)
    cs_i = lp.dram_in("cs2", [1, 128], F32)
    wout_i = lp.dram_in("woutw", [2048, DM], BF16)
    bout_i = lp.dram_in("boutp", [128, 8], F32)
    x2_i = lp.dram_in("x2d", [DM, TPC], F32)
    xn2_i = lp.dram_in("xn2d", [DM, TPC], F32)
    l2f_i = lp.dram_in("l2fd", [8 * 128, TPC], F32)
    wr_i = lp.dram_in("wrw", [DM, 8], F32)
    br_i = lp.dram_in("brw", [1, 8], F32)
    x3_o, x3_or = lp.dram_out("x3_o", [DM, TPC], F32)
    xn3_o, xn3_or = lp.dram_out("xn3_o", [DM, TPC], F32)
    xn3b_o, xn3b_or = lp.dram_out("xn3b_o", [DM, TPC], BF16)
    wm_o, wm_or = lp.dram_out("wm_o", [128, 128], F32)

    ceb, ceb_r = lp.sbuf("ceb", [128, 16384], BF16)
    hp, hp_r = lp.sbuf("hp", [128, 1024], F32)
    hpb, hpb_r = lp.sbuf("hpb", [128, 1024], BF16)
    hs_s, hs_r = lp.sbuf("hss", [128, 3 * 1024], F32)
    ds_s, ds_r = lp.sbuf("dss", [128, 3 * 4], F32)
    csr, csr_r = lp.sbuf("csr", [1, 128], F32)
    ecr, ecr_r = lp.sbuf("ecr", [1, 128], F32)
    ecsb, ecsb_r = lp.sbuf("ecsb", [128, 64], F32)
    wob, wob_r = lp.sbuf("wob", [128, 16 * 1024], BF16)
    bb, bb_r = lp.sbuf("bb", [128, 8], F32)
    ydt, ydt_r = lp.sbuf("ydt", [128, 16 * TT], BF16)
    szt, szt_r = lp.sbuf("szt", [128, 16 * TT], BF16)
    tmpv, tmpv_r = lp.sbuf("tmpv", [128, 2048], BF16)
    xxt, xxt_r = lp.sbuf("xxt", [128, 8 * TT], F32)
    ldb, ldb_r = lp.sbuf("ldb", [128, 8 * TT], F32)
    x3t, x3t_r = lp.sbuf("x3t", [128, 8 * TT], F32)
    xnt, xnt_r = lp.sbuf("xnt", [128, 8 * TT], F32)
    xnbt, xnbt_r = lp.sbuf("xnbt", [128, 8 * TT], BF16)
    on1, on1_r = lp.sbuf("on1", [128, 1], F32)
    onb, onb_r = lp.sbuf("onb", [1, 128], F32)
    wrs, wrs_r = lp.sbuf("wrs", [128, 64], F32)
    brs, brs_r = lp.sbuf("brs", [1, 8], F32)
    brb, brb_r = lp.sbuf("brb", [128, 8], F32)
    lgr, lgr_r = lp.sbuf("lgr", [128, 128], F32)
    e8t, e8t_r = lp.sbuf("e8t", [128, 128], F32)
    mx8, mx8_r = lp.sbuf("mx8", [128, 16], F32)
    nm8, nm8_r = lp.sbuf("nm8", [128, 16], F32)
    sm8, sm8_r = lp.sbuf("sm8", [128, 16], F32)
    rc8, rc8_r = lp.sbuf("rc8", [128, 16], F32)
    wmt, wmt_r = lp.sbuf("wmt", [128, 128], F32)
    sqs, _ = lp.sbuf("sqs", [128, 2 * TT], F32)
    sq_r = (Res("sq0"), Res("sq1"))
    sts, sts_r = lp.sbuf("sts", [1, 4 * TT], F32)

    class Pool:
        def __init__(self):
            self.tiles = [lp.psum(f"pp{i}", [128, TT]) for i in range(6)]
            self.i = 0

        def get(self):
            t = self.tiles[self.i % 6]
            self.i += 1
            return t
    pool = Pool()

    def c0(h):
        h.memset(on1[:, :], 1.0)
        h.memset(onb[:, :], 1.0)
        return h.memset(hp[:, :], 0.0)
    g.op("vector", c0, writes=(on1_r, onb_r, hp_r))

    def ldce(h):
        return h.dma_start(out=ceb[:, :], in_=ce_i)
    g.dma("sync", ldce, writes=(ceb_r,))

    def ldhs(h):
        return h.dma_start(
            out=hs_s[:, :].rearrange("p (s d) -> p s d", s=3),
            in_=hstk_i.rearrange("(s p) d -> p s d", p=128))
    g.dma("sync", ldhs, writes=(hs_r,))

    def ldds(h):
        return h.dma_start(
            out=ds_s[:, :].rearrange("p (s j) -> p s j", s=3),
            in_=dstk_i.rearrange("(s p) j -> p s j", p=128))
    g.dma("sync", ldds, writes=(ds_r,))

    ld_plain(lp, csr[:, :], csr_r, cs_i, eng="sync")
    ld_fm(lp, wrs, wrs_r, wr_i, 8, eng="sync")
    ld_plain(lp, brs[:, :], brs_r, br_i, eng="sync")
    ld_fm(lp, wob, wob_r, wout_i, 16, eng="sync")
    ld_plain(lp, bb[:, :], bb_r, bout_i, eng="sync")

    for s in range(3):
        for j in range(4):
            def pfx(h, s=s, j=j):
                return h.scalar_tensor_tensor(
                    hp[:, j * 256:(j + 1) * 256], hp[:, j * 256:(j + 1) * 256],
                    ds_s[:, s * 4 + j: s * 4 + j + 1],
                    hs_s[:, s * 1024 + j * 256: s * 1024 + (j + 1) * 256],
                    op0=ALU.mult, op1=ALU.add)
            g.op("vector", pfx, reads=(hp_r, ds_r, hs_r), writes=(hp_r,),
                 fence=(s == 2 and j == 3))

    def cphpb(h):
        return h.activation(hpb[:, :], hp[:, :], AF.Copy)
    g.op("scalar", cphpb, reads=(hp_r,), writes=(hpb_r,))

    def excs(h):
        return h.activation(ecr[:, :], csr[:, :], AF.Exp)
    g.op("scalar", excs, reads=(csr_r,), writes=(ecr_r,))

    psE, psE_r = lp.psum("psE", [128, 64])

    def mmec(h):
        h.matmul(psE[0:64, 0:64], onb[0:1, 0:64], ecr[0:1, 0:64],
                 start=True, stop=True)
        return h.matmul(psE[64:128, 0:64], onb[0:1, 0:64], ecr[0:1, 64:128],
                        start=True, stop=True)
    g.op("tensor", mmec, reads=(onb_r, ecr_r), writes=(psE_r,))

    def evec(h):
        return h.activation(ecsb[:, :], psE[:, 0:64], AF.Copy)
    g.op("scalar", evec, reads=(psE_r,), writes=(ecsb_r,))

    psBr, psBr_r = lp.psum("psBr", [128, 8])

    def mmbr(h):
        return h.matmul(psBr[:, 0:8], onb[0:1, :], brs[0:1, :],
                        start=True, stop=True)
    g.op("tensor", mmbr, reads=(onb_r, brs_r), writes=(psBr_r,))

    def evbr(h):
        return h.activation(brb[:, :], psBr[:, 0:8], AF.Copy)
    g.op("scalar", evbr, reads=(psBr_r,), writes=(brb_r,))

    yidr = yid_i.rearrange("(kc p) t -> p kc t", p=128)
    szr = sz_i.rearrange("(kc p) t -> p kc t", p=128)
    x2r = x2_i.rearrange("(kc p) t -> p kc t", p=128)
    xn2r = xn2_i.rearrange("(kc p) t -> p kc t", p=128)
    l2fr = l2f_i.rearrange("(kc p) t -> p kc t", p=128)
    x3r = x3_o.rearrange("(kc p) t -> p kc t", p=128)
    xn3r = xn3_o.rearrange("(kc p) t -> p kc t", p=128)
    xn3br = xn3b_o.rearrange("(kc p) t -> p kc t", p=128)

    for tt in range(4):
        def ldy(h, tt=tt):
            return h.dma_start(
                out=ydt[:, :].rearrange("p (kc t) -> p kc t", kc=16),
                in_=yidr[:, :, tt * TT:(tt + 1) * TT])
        g.dma("gpsimd", ldy, writes=(ydt_r,))

        def ldsz(h, tt=tt):
            return h.dma_start(
                out=szt[:, :].rearrange("p (kc t) -> p kc t", kc=16),
                in_=szr[:, :, tt * TT:(tt + 1) * TT])
        g.dma("gpsimd", ldsz, writes=(szt_r,))

        def ldx2(h, tt=tt):
            return h.dma_start(
                out=xxt[:, :].rearrange("p (kc t) -> p kc t", kc=8),
                in_=x2r[:, :, tt * TT:(tt + 1) * TT])
        g.dma("sync", ldx2, writes=(xxt_r,))

        def ldxn(h, tt=tt):
            return h.dma_start(
                out=ldb[:, :].rearrange("p (kc t) -> p kc t", kc=8),
                in_=xn2r[:, :, tt * TT:(tt + 1) * TT])
        g.dma("sync", ldxn, writes=(ldb_r,))

        def addxx(h):
            ins = None
            for q in range(2):
                ins = h.tensor_tensor(xxt[:, q * 2048:(q + 1) * 2048],
                                      xxt[:, q * 2048:(q + 1) * 2048],
                                      ldb[:, q * 2048:(q + 1) * 2048],
                                      op=ALU.add)
            return ins
        g.op("vector", addxx, reads=(xxt_r, ldb_r), writes=(xxt_r,))

        def ldl2(h, tt=tt):
            return h.dma_start(
                out=ldb[:, :].rearrange("p (kc t) -> p kc t", kc=8),
                in_=l2fr[:, :, tt * TT:(tt + 1) * TT])
        g.dma("sync", ldl2, reads=(ldb_r,), writes=(ldb_r,))

        for c4 in range(4):
            ch = tt * 4 + c4
            c0_ = c4 * 128
            pscs = []
            for hd in range(8):
                psc, psc_r = pool.get()
                pscs.append((psc, psc_r))

                def mmc(h, ch=ch, hd=hd, psc=psc):
                    ins = None
                    for ph in range(2):
                        pr = (hd % 2) * 64
                        co = (hd // 2) * 256
                        ins = h.matmul(
                            psc[:, ph * 128:(ph + 1) * 128],
                            hpb[pr:pr + 64, co + ph * 128: co + (ph + 1) * 128],
                            ceb[pr:pr + 64, (ch * 8 + hd) * 128:
                                (ch * 8 + hd) * 128 + 128],
                            start=True, stop=True)
                    return ins
                g.op("tensor", mmc, reads=(hpb_r, ceb_r), writes=(psc_r,))

            for hd in range(8):
                psc, psc_r = pscs[hd]

                def eva(h, hd=hd, c0_=c0_, psc=psc):
                    ins = None
                    for ph in range(2):
                        kc = 2 * hd + ph
                        ins = h.tensor_tensor(
                            tmpv[:, kc * 128:(kc + 1) * 128],
                            psc[:, ph * 128:(ph + 1) * 128],
                            ydt[:, kc * TT + c0_: kc * TT + c0_ + 128],
                            op=ALU.add)
                    return ins
                g.op("vector", eva, reads=(psc_r, ydt_r), writes=(tmpv_r,))

            for hd in range(8):
                def evm(h, hd=hd, c0_=c0_):
                    ins = None
                    for ph in range(2):
                        kc = 2 * hd + ph
                        ins = h.tensor_tensor(
                            ydt[:, kc * TT + c0_: kc * TT + c0_ + 128],
                            tmpv[:, kc * 128:(kc + 1) * 128],
                            szt[:, kc * TT + c0_: kc * TT + c0_ + 128],
                            op=ALU.mult)
                    return ins
                g.op("vector", evm, reads=(tmpv_r, szt_r), writes=(ydt_r,))

            for j in range(4):
                def dec(h, ch=ch, j=j):
                    return h.tensor_scalar(
                        hp[:, j * 256:(j + 1) * 256],
                        hp[:, j * 256:(j + 1) * 256],
                        ecsb[:, ch * 4 + j: ch * 4 + j + 1], None, op0=ALU.mult)
                g.op("vector", dec, reads=(hp_r, ecsb_r), writes=(hp_r,),
                     fence=(j == 3))

            def cphp2(h):
                return h.activation(hpb[:, :], hp[:, :], AF.Copy)
            g.op("scalar", cphp2, reads=(hp_r,), writes=(hpb_r,))

        for dc in range(8):
            psw, psw_r = pool.get()

            def mmw(h, dc=dc, psw=psw):
                ins = None
                for kc in range(16):
                    ins = h.matmul(
                        psw[:, 0:TT],
                        wob[:, kc * 1024 + dc * 128: kc * 1024 + (dc + 1) * 128],
                        ydt[:, kc * TT:(kc + 1) * TT],
                        start=(kc == 0), stop=(kc == 15))
                return ins
            g.op("tensor", mmw, reads=(wob_r, ydt_r), writes=(psw_r,))

            def ev1(h, dc=dc, psw=psw):
                return h.tensor_tensor(x3t[:, dc * TT:(dc + 1) * TT],
                                       psw[:, 0:TT],
                                       xxt[:, dc * TT:(dc + 1) * TT],
                                       op=ALU.add)
            g.op("vector", ev1, reads=(psw_r, xxt_r), writes=(x3t_r,))

        for dc in range(8):
            def ev2(h, dc=dc):
                return h.scalar_tensor_tensor(
                    x3t[:, dc * TT:(dc + 1) * TT],
                    x3t[:, dc * TT:(dc + 1) * TT],
                    bb[:, dc:dc + 1],
                    ldb[:, dc * TT:(dc + 1) * TT],
                    op0=ALU.add, op1=ALU.add)
            g.op("vector", ev2, reads=(x3t_r, bb_r, ldb_r), writes=(x3t_r,))

        def stx3(h, tt=tt):
            return h.dma_start(out=x3r[:, :, tt * TT:(tt + 1) * TT],
                               in_=x3t[:, :].rearrange("p (kc t) -> p kc t", kc=8))
        g.dma("gpsimd", stx3, reads=(x3t_r,), writes=(x3_or,))

        emit_ln(lp, pool, x3t, x3t_r, xnt, xnt_r, on1, on1_r, onb, onb_r,
                sqs, sq_r, sts, sts_r, 8, 0, TT, 0, TT)

        def stxn(h, tt=tt):
            return h.dma_start(out=xn3r[:, :, tt * TT:(tt + 1) * TT],
                               in_=xnt[:, :].rearrange("p (kc t) -> p kc t", kc=8))
        g.dma("gpsimd", stxn, reads=(xnt_r,), writes=(xn3_or,))

        for c4 in range(4):
            tc = tt * 4 + c4
            psg, psg_r = pool.get()

            def mmg3(h, c4=c4, psg=psg):
                ins = None
                for kc in range(8):
                    ins = h.matmul(psg[:, 0:8],
                                   xnt[:, kc * TT + c4 * 128:
                                       kc * TT + (c4 + 1) * 128],
                                   wrs[:, kc * 8:(kc + 1) * 8],
                                   start=(kc == 0), stop=(kc == 7))
                return ins
            g.op("tensor", mmg3, reads=(xnt_r, wrs_r), writes=(psg_r,))

            def evg3(h, tc=tc, psg=psg):
                return h.tensor_tensor(lgr[:, tc * 8:(tc + 1) * 8],
                                       psg[:, 0:8], brb[:, :], op=ALU.add)
            g.op("vector", evg3, reads=(psg_r, brb_r), writes=(lgr_r,))

        def cvb(h):
            ins = None
            for q in range(2):
                ins = h.tensor_copy(xnbt[:, q * 2048:(q + 1) * 2048],
                                    xnt[:, q * 2048:(q + 1) * 2048])
            return ins
        g.op("vector", cvb, reads=(xnt_r,), writes=(xnbt_r,))

        def stxb(h, tt=tt):
            return h.dma_start(out=xn3br[:, :, tt * TT:(tt + 1) * TT],
                               in_=xnbt[:, :].rearrange("p (kc t) -> p kc t", kc=8))
        g.dma("gpsimd", stxb, reads=(xnbt_r,), writes=(xn3b_or,))

    for tc in range(16):
        def rmx3(h, tc=tc):
            return h.tensor_reduce(mx8[:, tc:tc + 1], lgr[:, tc * 8:(tc + 1) * 8],
                                   axis=mybir.AxisListType.X, op=ALU.max)
        g.op("vector", rmx3, reads=(lgr_r,), writes=(mx8_r,))
    for tc in range(16):
        def ngx3(h, tc=tc):
            return h.activation(nm8[:, tc:tc + 1], mx8[:, tc:tc + 1], AF.Copy,
                                scale=-1.0)
        g.op("scalar", ngx3, reads=(mx8_r,), writes=(nm8_r,))
    for tc in range(16):
        def exg3(h, tc=tc):
            return h.activation(e8t[:, tc * 8:(tc + 1) * 8],
                                lgr[:, tc * 8:(tc + 1) * 8], AF.Exp,
                                bias=nm8[:, tc:tc + 1])
        g.op("scalar", exg3, reads=(lgr_r, nm8_r), writes=(e8t_r,))
    for tc in range(16):
        def rsm3(h, tc=tc):
            return h.tensor_reduce(sm8[:, tc:tc + 1], e8t[:, tc * 8:(tc + 1) * 8],
                                   axis=mybir.AxisListType.X, op=ALU.add)
        g.op("vector", rsm3, reads=(e8t_r,), writes=(sm8_r,), fence=(tc == 15))

    def rcp3(h):
        return h.reciprocal(rc8[:, :], sm8[:, :])
    g.op("vector", rcp3, reads=(sm8_r,), writes=(rc8_r,), fence=True)

    for tc in range(16):
        def pck3(h, tc=tc):
            return h.tensor_scalar(wmt[:, tc * 8:(tc + 1) * 8],
                                   lgr[:, tc * 8:(tc + 1) * 8],
                                   mx8[:, tc:tc + 1], None, op0=ALU.is_ge)
        g.op("vector", pck3, reads=(lgr_r, mx8_r), writes=(wmt_r,))
    for tc in range(16):
        def gat3(h, tc=tc):
            return h.tensor_scalar(wmt[:, tc * 8:(tc + 1) * 8],
                                   wmt[:, tc * 8:(tc + 1) * 8],
                                   rc8[:, tc:tc + 1], None, op0=ALU.mult)
        g.op("vector", gat3, reads=(wmt_r, rc8_r), writes=(wmt_r,))

    def stwm(h):
        return h.dma_start(out=wm_o[:, :], in_=wmt[:, :])
    g.dma("gpsimd", stwm, reads=(wmt_r,), writes=(wm_or,))

    def fin(h):
        h.nop()
        return None
    g.op("vector", fin, reads=(x3_or, xn3_or, xn3b_or, wm_or))
    return lp


def host_A10(inputs, cores, prev):
    import ml_dtypes
    bf = ml_dtypes.bfloat16
    wout = np.ascontiguousarray(
        np.asarray(inputs["L2_attn_W_out"], np.float32).astype(bf))
    boutp = pp(np.asarray(inputs["L2_attn_b_out"], np.float32))
    wrw = np.ascontiguousarray(np.asarray(inputs["L3_ffn_Wr"], np.float32))
    brw = np.ascontiguousarray(
        np.asarray(inputs["L3_ffn_br"], np.float32).reshape(1, 8))
    maps = []
    for c in cores:
        k = c % 4
        b0 = (c // 4) * 4
        hstk = np.zeros((3, 128, 1024), np.float32)
        dstk = np.zeros((3, 128, 4), np.float32)
        for i, pc in enumerate(range(b0, c)):
            s = 3 - k + i
            hstk[s] = prev["hloc_o"][pc]
            dc = np.asarray(prev["dcor_o"][pc], np.float32)[0]
            for j in range(4):
                dstk[s, 0:64, j] = dc[2 * j]
                dstk[s, 64:128, j] = dc[2 * j + 1]
        cs = np.asarray(prev["csum_o"][c], np.float32)[0]
        cs2 = np.zeros((1, 128), np.float32)
        for ch in range(16):
            for j in range(4):
                cs2[0, ch * 8 + ... if False else 0] = 0
        csA = cs.reshape(16, 8)[:, 0::2].reshape(64)
        csB = cs.reshape(16, 8)[:, 1::2].reshape(64)
        cs2 = np.concatenate([csA, csB]).reshape(1, 128).astype(np.float32)
        maps.append({
            "yid": prev["yid_o"][c], "szd": prev["sz_o"][c],
            "ced": prev["ce_o"][c],
            "hstk": np.ascontiguousarray(hstk.reshape(384, 1024)),
            "dstk": np.ascontiguousarray(dstk.reshape(384, 4)),
            "cs2": np.ascontiguousarray(cs2),
            "woutw": wout, "boutp": boutp, "wrw": wrw, "brw": brw,
            "x2d": prev["x2_o"][c], "xn2d": prev["xn2_o"][c],
            "l2fd": prev["l2f_o"][c],
        })
    return maps


def build_A11a():
    lp = LP()
    g = lp.g
    xnb_i = lp.dram_in("xnbh", [DM, 2176], BF16)
    wqp_i = lp.dram_in("wqp", [DM, 1024], BF16)
    wkp_i = lp.dram_in("wkp", [DM, 1024], BF16)
    wkt_i = lp.dram_in("wkt", [DM, 256], BF16)
    wvt_i = lp.dram_in("wvt", [DM, 1024], BF16)
    bqp_i = lp.dram_in("bqp", [128, 8], F32)
    bkp_i = lp.dram_in("bkp", [128, 8], F32)
    bkr_i = lp.dram_in("bkr", [1, 256], F32)
    bvr_i = lp.dram_in("bvr", [1, 1024], F32)
    rt1_i = lp.dram_in("rt1", [128, 256], BF16)
    rt2_i = lp.dram_in("rt2", [128, 256], BF16)
    tri_i = lp.dram_in("trib", [128, 128], BF16)
    bnc_i = lp.dram_in("bandc", [128, 128], F32)
    bnp_i = lp.dram_in("bandp", [128, 128], F32)
    bnp0_i = lp.dram_in("bandp0", [128, 128], F32)
    num_o, num_or = lp.dram_out("num_o", [1024, TPC], F32)
    den_o, den_or = lp.dram_out("den_o", [16, TPC], F32)
    yloc_o, yloc_or = lp.dram_out("yloc_o", [1024, TPC], BF16)
    pqf_o, pqf_or = lp.dram_out("pqf_o", [16 * 384, TPC], BF16)
    S_o, S_or = lp.dram_out("S_o", [128, 3072], F32)
    z_o, z_or = lp.dram_out("z_o", [128, 48], F32)

    xnb, xnb_r = lp.sbuf("xnb", [128, 8 * 2176], BF16)
    wqb, wqb_r = lp.sbuf("wqb", [128, 8 * 1024], BF16)
    wkb, wkb_r = lp.sbuf("wkb", [128, 8 * 1024], BF16)
    wtb, wtb_r = lp.sbuf("wtb", [128, 8 * 256], BF16)
    wvb, wvb_r = lp.sbuf("wvb", [128, 8 * 1024], BF16)
    bqs, bqs_r = lp.sbuf("bqs", [128, 8], F32)
    bks, bks_r = lp.sbuf("bks", [128, 8], F32)
    bkrs, bkrs_r = lp.sbuf("bkrs", [1, 256], F32)
    bvrs, bvrs_r = lp.sbuf("bvrs", [1, 1024], F32)
    bkt, bkt_r = lp.sbuf("bkt", [128, 256], F32)
    bvt, bvt_r = lp.sbuf("bvt", [128, 1024], F32)
    rt1, rt1_r = lp.sbuf("rt1s", [128, 256], BF16)
    rt2, rt2_r = lp.sbuf("rt2s", [128, 256], BF16)
    trib, trib_r = lp.sbuf("tribs", [128, 128], BF16)
    bnc, bnc_r = lp.sbuf("bncs", [128, 128], F32)
    bnp, bnp_r = lp.sbuf("bnps", [128, 128], F32)
    bnp0, bnp0_r = lp.sbuf("bnp0s", [128, 128], F32)
    kf = []
    kf_r = []
    vt = []
    vt_r = []
    for s in range(2):
        t_, r_ = lp.sbuf(f"kf{s}", [128, 8 * 128], BF16)
        kf.append(t_)
        kf_r.append(r_)
        t_, r_ = lp.sbuf(f"vt{s}", [128, 1024], BF16)
        vt.append(t_)
        vt_r.append(r_)
    qf, qf_r = lp.sbuf("qf", [128, 8 * 128], BF16)
    kt, kt_r = lp.sbuf("kt", [128, 256], BF16)
    pq3_l = [lp.sbuf(f"pq3{i}", [128, 384], BF16) for i in range(2)]
    phk_l = [lp.sbuf(f"phk{i}", [128, 384], BF16) for i in range(2)]
    qre_l = [lp.sbuf(f"qre{i}", [128, 128], BF16) for i in range(2)]
    kre_l = [lp.sbuf(f"kre{i}", [128, 256], BF16) for i in range(2)]
    Sr, Sr_r = lp.sbuf("Sr", [128, 3072], F32)
    Srb, Srb_r = lp.sbuf("Srb", [128, 3072], BF16)
    zr, zr_r = lp.sbuf("zr", [128, 48], F32)
    zrb, zrb_r = lp.sbuf("zrb", [128, 48], BF16)
    sct_l = [lp.sbuf(f"sct{i}", [128, 128], F32) for i in range(2)]
    ecb_l = [lp.sbuf(f"ecb{i}", [128, 2 * 128], BF16) for i in range(2)]
    mtv_l = [lp.sbuf(f"mtv{i}", [128, 128], F32) for i in range(2)]
    mtb_l = [lp.sbuf(f"mtb{i}", [128, 128], BF16) for i in range(2)]
    nlo_l = [lp.sbuf(f"nlo{i}", [64, 128], F32) for i in range(2)]
    rdv_l = [lp.sbuf(f"rdv{i}", [1, 128], F32) for i in range(2)]
    numst, numst_r = lp.sbuf("numst", [128, 2048], F32)
    ylst, ylst_r = lp.sbuf("ylst", [64, 2048], BF16)
    on1b, on1b_r = lp.sbuf("on1b", [128, 1], BF16)
    onb, onb_r = lp.sbuf("onb", [1, 128], F32)

    class Pool:
        def __init__(self):
            self.tiles = [lp.psum(f"pp{i}", [128, TT]) for i in range(6)]
            self.i = 0

        def get(self):
            t = self.tiles[self.i % 6]
            self.i += 1
            return t
    pool = Pool()

    def c0(h):
        h.memset(on1b[:, :], 1.0)
        h.memset(onb[:, :], 1.0)
        h.memset(Sr[:, :], 0.0)
        h.memset(Srb[:, :], 0.0)
        h.memset(zr[:, :], 0.0)
        h.memset(zrb[:, :], 0.0)
        for i in range(2):
            h.memset(pq3_l[i][0][:, 0:128], 0.0)
            h.memset(pq3_l[i][0][32:33, 0:128], 1.0)
            h.memset(phk_l[i][0][:, 0:128], 0.0)
            h.memset(phk_l[i][0][:, 32:33], 1.0)
        return h.nop()
    g.op("vector", c0, writes=(on1b_r, onb_r, Sr_r, Srb_r, zr_r, zrb_r,
                               pq3_l[0][1], pq3_l[1][1], phk_l[0][1],
                               phk_l[1][1]))

    ld_fm(lp, xnb, xnb_r, xnb_i, 8, eng="sync")
    ld_fm(lp, wqb, wqb_r, wqp_i, 8, eng="sync")
    ld_fm(lp, wkb, wkb_r, wkp_i, 8, eng="sync")
    ld_fm(lp, wtb, wtb_r, wkt_i, 8, eng="sync")
    ld_fm(lp, wvb, wvb_r, wvt_i, 8, eng="sync")
    for dst, dr, si in ((bqs, bqs_r, bqp_i), (bks, bks_r, bkp_i),
                        (bkrs, bkrs_r, bkr_i), (bvrs, bvrs_r, bvr_i),
                        (rt1, rt1_r, rt1_i), (rt2, rt2_r, rt2_i),
                        (trib, trib_r, tri_i), (bnc, bnc_r, bnc_i),
                        (bnp, bnp_r, bnp_i), (bnp0, bnp0_r, bnp0_i)):
        ld_plain(lp, dst[:, :], dr, si, eng="sync")

    psb, psb_r = pool.get()

    def bcb(h, psb=psb):
        h.matmul(psb[:, 0:256], onb[0:1, :], bkrs[:, :], start=True, stop=True)
        return h.matmul(psb[:, 256:512], onb[0:1, :], bvrs[0:1, 0:256],
                        start=True, stop=True)
    g.op("tensor", bcb, reads=(onb_r, bkrs_r, bvrs_r), writes=(psb_r,))

    def evbc(h, psb=psb):
        h.activation(bkt[:, :], psb[:, 0:256], AF.Copy)
        return h.activation(bvt[:, 0:256], psb[:, 256:512], AF.Copy)
    g.op("scalar", evbc, reads=(psb_r,), writes=(bkt_r, bvt_r))

    for q in range(1, 4):
        psb2, psb2_r = pool.get()

        def bcb2(h, q=q, psb2=psb2):
            return h.matmul(psb2[:, 0:256], onb[0:1, :],
                            bvrs[0:1, q * 256:(q + 1) * 256],
                            start=True, stop=True)
        g.op("tensor", bcb2, reads=(onb_r, bvrs_r), writes=(psb2_r,))

        def evb2(h, q=q, psb2=psb2):
            return h.activation(bvt[:, q * 256:(q + 1) * 256], psb2[:, 0:256],
                                AF.Copy)
        g.op("scalar", evb2, reads=(psb2_r,), writes=(bvt_r,))

    numr4 = num_o.rearrange("(hh p) t -> p hh t", hh=16)
    ylr4 = yloc_o.rearrange("(hh p) t -> p hh t", hh=16)
    pqr = pqf_o.rearrange("(hh X p) t -> hh p X t", hh=16, X=3)

    for u in range(17):
        sl_ = u % 2
        ucol = u * 128

        for gc in range(8):
            psk, psk_r = pool.get()

            def mmk(h, gc=gc, ucol=ucol, psk=psk):
                ins = None
                for kc in range(8):
                    ins = h.matmul(
                        psk[:, 0:128],
                        wkb[:, kc * 1024 + gc * 128: kc * 1024 + (gc + 1) * 128],
                        xnb[:, kc * 2176 + ucol: kc * 2176 + ucol + 128],
                        start=(kc == 0), stop=(kc == 7))
                return ins
            g.op("tensor", mmk, reads=(wkb_r, xnb_r), writes=(psk_r,))

            def evk(h, gc=gc, sl_=sl_, psk=psk):
                return h.tensor_scalar(kf[sl_][:, gc * 128:(gc + 1) * 128],
                                       psk[:, 0:128], bks[:, gc:gc + 1], None,
                                       op0=ALU.add)
            g.op("vector", evk, reads=(psk_r, bks_r), writes=(kf_r[sl_],))

        for vh in range(2):
            psv, psv_r = pool.get()

            def mmv(h, vh=vh, ucol=ucol, psv=psv):
                ins = None
                for kc in range(8):
                    ins = h.matmul(
                        psv[:, 0:TT],
                        xnb[:, kc * 2176 + ucol: kc * 2176 + ucol + 128],
                        wvb[:, kc * 1024 + vh * TT: kc * 1024 + (vh + 1) * TT],
                        start=(kc == 0), stop=(kc == 7))
                return ins
            g.op("tensor", mmv, reads=(wvb_r, xnb_r), writes=(psv_r,))

            def evv(h, vh=vh, sl_=sl_, psv=psv):
                return h.tensor_tensor(vt[sl_][:, vh * TT:(vh + 1) * TT],
                                       psv[:, 0:TT],
                                       bvt[:, vh * TT:(vh + 1) * TT],
                                       op=ALU.add)
            g.op("vector", evv, reads=(psv_r, bvt_r), writes=(vt_r[sl_],))

        if u == 0:
            continue
        ch = u - 1
        cur, prv = u % 2, (u - 1) % 2

        for gc in range(8):
            psq, psq_r = pool.get()

            def mmq(h, gc=gc, ucol=ucol, psq=psq):
                ins = None
                for kc in range(8):
                    ins = h.matmul(
                        psq[:, 0:128],
                        wqb[:, kc * 1024 + gc * 128: kc * 1024 + (gc + 1) * 128],
                        xnb[:, kc * 2176 + ucol: kc * 2176 + ucol + 128],
                        start=(kc == 0), stop=(kc == 7))
                return ins
            g.op("tensor", mmq, reads=(wqb_r, xnb_r), writes=(psq_r,))

            def evq(h, gc=gc, psq=psq):
                return h.tensor_scalar(qf[:, gc * 128:(gc + 1) * 128],
                                       psq[:, 0:128], bqs[:, gc:gc + 1], None,
                                       op0=ALU.add)
            g.op("vector", evq, reads=(psq_r, bqs_r), writes=(qf_r,))

        pst, pst_r = pool.get()

        def mmkt(h, ucol=ucol, pst=pst):
            ins = None
            for kc in range(8):
                ins = h.matmul(
                    pst[:, 0:256],
                    xnb[:, kc * 2176 + ucol: kc * 2176 + ucol + 128],
                    wtb[:, kc * 256:(kc + 1) * 256],
                    start=(kc == 0), stop=(kc == 7))
            return ins
        g.op("tensor", mmkt, reads=(wtb_r, xnb_r), writes=(pst_r,))

        def evkt(h, pst=pst):
            return h.tensor_tensor(kt[:, :], pst[:, 0:256], bkt[:, :],
                                   op=ALU.add)
        g.op("vector", evkt, reads=(pst_r, bkt_r), writes=(kt_r,))

        def phA(hd, cur=cur, prv=prv, ch=ch):
            bh = (hd % 2) * 64
            gcol = (hd // 2) * 128
            rr = hd % 2
            pq3, pq3_r = pq3_l[rr]
            phk, phk_r = phk_l[rr]
            qre, qre_r = qre_l[rr]
            kre, kre_r = kre_l[rr]
            mtv, mtv_r = mtv_l[rr]
            mtb, mtb_r = mtb_l[rr]
            sct, sct_r = sct_l[rr]
            ecb, ecb_r = ecb_l[rr]

            def cpq0(h, bh=bh, gcol=gcol, pq3=pq3):
                return h.tensor_copy(pq3[0:16, 0:128],
                                     qf[bh:bh + 16, gcol:gcol + 128])
            g.op("vector", cpq0, reads=(qf_r,), writes=(pq3_r,))

            for half in range(2):
                psA, psA_r = pool.get()
                psB, psB_r = pool.get()

                def mmqa(h, bh=bh, gcol=gcol, half=half, psA=psA, psB=psB):
                    h.matmul(psA[:, 0:128], rt1[bh:bh + 16,
                                                half * 128:(half + 1) * 128],
                             qf[bh:bh + 16, gcol:gcol + 128],
                             start=True, stop=True)
                    return h.matmul(psB[:, 0:128],
                                    rt2[bh:bh + 16, half * 128:(half + 1) * 128],
                                    qf[bh:bh + 16, gcol:gcol + 128],
                                    start=True, stop=True)
                g.op("tensor", mmqa, reads=(rt1_r, rt2_r, qf_r),
                     writes=(psA_r, psB_r))

                def evqa(h, psA=psA, qre=qre):
                    return h.activation(qre[:, :], psA[:, 0:128], AF.Copy)
                g.op("scalar", evqa, reads=(psA_r,), writes=(qre_r,))

                def mlqa(h, half=half, psB=psB, pq3=pq3, qre=qre):
                    return h.tensor_tensor(pq3[:, 128 + half * 128:
                                               256 + half * 128],
                                           psB[:, 0:128], qre[:, :],
                                           op=ALU.mult)
                g.op("vector", mlqa, reads=(psB_r, qre_r), writes=(pq3_r,))

            def cpk0(h, hd=hd, phk=phk):
                return h.tensor_copy(phk[:, 0:16], kt[:, hd * 16:(hd + 1) * 16])
            g.op("vector", cpk0, reads=(kt_r,), writes=(phk_r,))

            psK1, psK1_r = pool.get()
            psK2, psK2_r = pool.get()

            def mmka(h, bh=bh, gcol=gcol, psK1=psK1, psK2=psK2):
                h.matmul(psK1[:, 0:256], kf[cur][bh:bh + 16, gcol:gcol + 128],
                         rt1[bh:bh + 16, 0:256], start=True, stop=True)
                return h.matmul(psK2[:, 0:256],
                                kf[cur][bh:bh + 16, gcol:gcol + 128],
                                rt2[bh:bh + 16, 0:256], start=True, stop=True)
            g.op("tensor", mmka, reads=(kf_r[cur], rt1_r, rt2_r),
                 writes=(psK1_r, psK2_r))

            def evka(h, psK1=psK1, kre=kre):
                return h.activation(kre[:, :], psK1[:, 0:256], AF.Copy)
            g.op("scalar", evka, reads=(psK1_r,), writes=(kre_r,))

            def mlka(h, psK2=psK2, phk=phk, kre=kre):
                return h.tensor_tensor(phk[:, 128:384], psK2[:, 0:256],
                                       kre[:, :], op=ALU.mult)
            g.op("vector", mlka, reads=(psK2_r, kre_r), writes=(phk_r,))

            psC, psC_r = pool.get()
            psP, psP_r = pool.get()

            def mmsc(h, bh=bh, gcol=gcol, psC=psC, psP=psP):
                h.matmul(psC[:, 0:128], kf[cur][bh:bh + 16, gcol:gcol + 128],
                         qf[bh:bh + 16, gcol:gcol + 128], start=True, stop=True)
                return h.matmul(psP[:, 0:128],
                                kf[prv][bh:bh + 16, gcol:gcol + 128],
                                qf[bh:bh + 16, gcol:gcol + 128],
                                start=True, stop=True)
            g.op("tensor", mmsc, reads=(kf_r[cur], kf_r[prv], qf_r),
                 writes=(psC_r, psP_r))

            cst1 = lp.constp(1.0)
            csth = lp.constp(0.5)

            def mker(h, psC=psC, mtv=mtv):
                h.activation(mtv[:, :], psC[:, 0:128], AF.Square,
                             bias=cst1[0][:, 0:1])
                return h.activation(mtv[:, :], mtv[:, :], AF.Identity,
                                    scale=0.5, bias=csth[0][:, 0:1])
            g.op("scalar", mker, reads=(psC_r, cst1[1], csth[1]),
                 writes=(mtv_r,))

            def mmsk(h, mtv=mtv, mtb=mtb):
                return h.tensor_tensor(mtb[:, :], mtv[:, :], trib[:, :],
                                       op=ALU.mult)
            g.op("vector", mmsk, reads=(mtv_r, trib_r), writes=(mtb_r,))

            def ebc(h, psC=psC, sct=sct):
                return h.tensor_tensor(sct[:, :], psC[:, 0:128], bnc[:, :],
                                       op=ALU.add)
            g.op("vector", ebc, reads=(psC_r, bnc_r), writes=(sct_r,))

            def exc(h, sct=sct, ecb=ecb):
                return h.activation(ecb[:, 0:128], sct[:, :], AF.Exp)
            g.op("scalar", exc, reads=(sct_r,), writes=(ecb_r,))

            bnpu, bnpu_r = (bnp0, bnp0_r) if ch == 0 else (bnp, bnp_r)

            def ebp(h, psP=psP, sct=sct, bnpu=bnpu):
                return h.tensor_tensor(sct[:, :], psP[:, 0:128], bnpu[:, :],
                                       op=ALU.add)
            g.op("vector", ebp, reads=(psP_r, bnpu_r), writes=(sct_r,))

            def exp_(h, sct=sct, ecb=ecb):
                return h.activation(ecb[:, 128:256], sct[:, :], AF.Exp)
            g.op("scalar", exp_, reads=(sct_r,), writes=(ecb_r,))
            return {}

        def phB(hd, cur=cur, prv=prv, ch=ch):
            rr = hd % 2
            pq3, pq3_r = pq3_l[rr]
            phk, phk_r = phk_l[rr]
            mtb, mtb_r = mtb_l[rr]
            ecb, ecb_r = ecb_l[rr]
            nlo, nlo_r = nlo_l[rr]
            rdv, rdv_r = rdv_l[rr]

            def stpq(h, hd=hd, ch=ch, pq3=pq3):
                return h.dma_start(
                    out=pqr[hd, :, :, ch * 128:(ch + 1) * 128],
                    in_=pq3[:, :].rearrange("p (X t) -> p X t", X=3))
            g.dma("scalar", stpq, reads=(pq3_r,), writes=(pqf_or,))

            psN, psN_r = pool.get()

            def mmnu(h, hd=hd, psN=psN, pq3=pq3, mtb=mtb):
                h.matmul(psN[0:64, 0:128], vt[cur][:, hd * 64:(hd + 1) * 64],
                         mtb[:, :], start=True, stop=False)
                ins = None
                for X in range(3):
                    ins = h.matmul(psN[0:64, 0:128],
                                   Srb[:, (hd * 3 + X) * 64:
                                       (hd * 3 + X + 1) * 64],
                                   pq3[:, X * 128:(X + 1) * 128],
                                   start=False, stop=(X == 2))
                return ins
            g.op("tensor", mmnu, reads=(vt_r[cur], mtb_r, Srb_r, pq3_r),
                 writes=(psN_r,))

            def evnu(h, hd=hd, psN=psN):
                return h.activation(numst[0:64, hd * 128:(hd + 1) * 128],
                                    psN[0:64, 0:128], AF.Copy)
            g.op("scalar", evnu, reads=(psN_r,), writes=(numst_r,))

            psD, psD_r = pool.get()

            def mmde(h, hd=hd, psD=psD, pq3=pq3, mtb=mtb):
                h.matmul(psD[0:1, 0:128], on1b[:, :], mtb[:, :],
                         start=True, stop=False)
                ins = None
                for X in range(3):
                    ins = h.matmul(psD[0:1, 0:128],
                                   zrb[:, hd * 3 + X: hd * 3 + X + 1],
                                   pq3[:, X * 128:(X + 1) * 128],
                                   start=False, stop=(X == 2))
                return ins
            g.op("tensor", mmde, reads=(on1b_r, mtb_r, zrb_r, pq3_r),
                 writes=(psD_r,))

            def evde(h, hd=hd, psD=psD):
                return h.activation(numst[64:65, hd * 128:(hd + 1) * 128],
                                    psD[0:1, 0:128], AF.Copy)
            g.op("scalar", evde, reads=(psD_r,), writes=(numst_r,))

            psL, psL_r = pool.get()
            psE, psE_r = pool.get()

            def mmlo(h, hd=hd, psL=psL, psE=psE, ecb=ecb):
                h.matmul(psL[0:64, 0:128], vt[prv][:, hd * 64:(hd + 1) * 64],
                         ecb[:, 128:256], start=True, stop=False)
                h.matmul(psL[0:64, 0:128], vt[cur][:, hd * 64:(hd + 1) * 64],
                         ecb[:, 0:128], start=False, stop=True)
                h.matmul(psE[0:1, 0:128], on1b[:, :], ecb[:, 128:256],
                         start=True, stop=False)
                return h.matmul(psE[0:1, 0:128], on1b[:, :], ecb[:, 0:128],
                                start=False, stop=True)
            g.op("tensor", mmlo, reads=(vt_r[cur], vt_r[prv], ecb_r, on1b_r),
                 writes=(psL_r, psE_r))

            def rcl(h, psE=psE, rdv=rdv):
                return h.reciprocal(rdv[:, :], psE[0:1, 0:128])
            g.op("vector", rcl, reads=(psE_r,), writes=(rdv_r,))

            def evlo(h, psL=psL, nlo=nlo):
                return h.activation(nlo[:, :], psL[0:64, 0:128], AF.Copy)
            g.op("scalar", evlo, reads=(psL_r,), writes=(nlo_r,))
            return {}

        def phC(hd, cur=cur):
            rr = hd % 2
            phk, phk_r = phk_l[rr]
            nlo, nlo_r = nlo_l[rr]
            rdv, rdv_r = rdv_l[rr]
            psR, psR_r = pool.get()

            def mmrb(h, psR=psR, rdv=rdv):
                return h.matmul(psR[0:64, 0:128], onb[0:1, 0:64], rdv[0:1, :],
                                start=True, stop=True)
            g.op("tensor", mmrb, reads=(onb_r, rdv_r), writes=(psR_r,))

            def mlyl(h, hd=hd, psR=psR, nlo=nlo):
                return h.tensor_tensor(ylst[:, hd * 128:(hd + 1) * 128],
                                       nlo[:, :], psR[0:64, 0:128],
                                       op=ALU.mult)
            g.op("vector", mlyl, reads=(nlo_r, psR_r), writes=(ylst_r,))

            for X in range(3):
                psS, psS_r = pool.get()

                def mmst(h, hd=hd, X=X, psS=psS, phk=phk):
                    h.matmul(psS[:, 0:64], phk[:, X * 128:(X + 1) * 128],
                             vt[cur][:, hd * 64:(hd + 1) * 64],
                             start=True, stop=False)
                    return h.matmul(psS[:, 64:65], phk[:, X * 128:(X + 1) * 128],
                                    on1b[:, :], start=True, stop=True)
                g.op("tensor", mmst, reads=(phk_r, vt_r[cur], on1b_r),
                     writes=(psS_r,))

                def upsb(h, hd=hd, X=X, psS=psS):
                    sc_ = (hd * 3 + X) * 64
                    h.tensor_tensor(Srb[:, sc_:sc_ + 64], Sr[:, sc_:sc_ + 64],
                                    psS[:, 0:64], op=ALU.add)
                    h.tensor_tensor(zrb[:, hd * 3 + X: hd * 3 + X + 1],
                                    zr[:, hd * 3 + X: hd * 3 + X + 1],
                                    psS[:, 64:65], op=ALU.add)
                    h.tensor_tensor(Sr[:, sc_:sc_ + 64], Sr[:, sc_:sc_ + 64],
                                    psS[:, 0:64], op=ALU.add)
                    return h.tensor_tensor(zr[:, hd * 3 + X: hd * 3 + X + 1],
                                           zr[:, hd * 3 + X: hd * 3 + X + 1],
                                           psS[:, 64:65], op=ALU.add)
                g.op("vector", upsb, reads=(psS_r, Sr_r, zr_r),
                     writes=(Srb_r, zrb_r, Sr_r, zr_r))

        for hp in range(0, 16, 2):
            phA(hp)
            phA(hp + 1)
            phB(hp)
            phB(hp + 1)
            phC(hp)
            phC(hp + 1)

        def stnu(h, ch=ch):
            return h.dma_start(out=numr4[:, :, ch * 128:(ch + 1) * 128],
                               in_=numst[0:64, :].rearrange(
                                   "p (hh t) -> p hh t", hh=16))
        g.dma("gpsimd", stnu, reads=(numst_r,), writes=(num_or,))

        def stde2(h, ch=ch):
            return h.dma_start(
                out=den_o[:, ch * 128:(ch + 1) * 128].rearrange(
                    "hh (p t) -> p hh t", p=1),
                in_=numst[64:65, :].rearrange("p (hh t) -> p hh t", hh=16))
        g.dma("gpsimd", stde2, reads=(numst_r,), writes=(den_or,))

        def styl(h, ch=ch):
            return h.dma_start(out=ylr4[:, :, ch * 128:(ch + 1) * 128],
                               in_=ylst[:, :].rearrange("p (hh t) -> p hh t",
                                                        hh=16))
        g.dma("gpsimd", styl, reads=(ylst_r,), writes=(yloc_or,))

    def stS(h):
        return h.dma_start(out=S_o[:, :], in_=Sr[:, :])
    g.dma("gpsimd", stS, reads=(Sr_r,), writes=(S_or,))

    def stz(h):
        return h.dma_start(out=z_o[:, :], in_=zr[:, :])
    g.dma("gpsimd", stz, reads=(zr_r,), writes=(z_or,))

    def fin(h):
        h.nop()
        return None
    g.op("vector", fin, reads=(num_or, den_or, yloc_or, pqf_or, S_or, z_or))
    return lp


def host_A11a(inputs, cores, prev):
    import ml_dtypes
    bf = ml_dtypes.bfloat16
    Wq = np.asarray(inputs["L3_attn_Wq"], np.float32)
    Wk = np.asarray(inputs["L3_attn_Wk"], np.float32)
    Wv = np.asarray(inputs["L3_attn_Wv"], np.float32)
    bq = np.asarray(inputs["L3_attn_bq"], np.float32)
    bk = np.asarray(inputs["L3_attn_bk"], np.float32)
    bv = np.asarray(inputs["L3_attn_bv"], np.float32)
    wqp = np.zeros((1024, 1024), np.float32)
    wkp = np.zeros((1024, 1024), np.float32)
    bqp = np.zeros(1024, np.float32)
    bkp = np.zeros(1024, np.float32)
    for hh in range(16):
        wqp[:, hh * 64:hh * 64 + 16] = Wq[hh * 16:(hh + 1) * 16, :].T
        wkp[:, hh * 64:hh * 64 + 16] = Wk[hh * 16:(hh + 1) * 16, :].T
        bqp[hh * 64:hh * 64 + 16] = bq[hh * 16:(hh + 1) * 16]
        bkp[hh * 64:hh * 64 + 16] = bk[hh * 16:(hh + 1) * 16]
    SQ = np.float32(1.0 / np.sqrt(2.0))
    R1 = np.zeros((16, 256), np.float32)
    R2 = np.zeros((16, 256), np.float32)
    for a in range(16):
        for b in range(16):
            R1[b, a * 16 + b] = 1.0
            R2[a, a * 16 + b] = SQ
    rt1 = np.zeros((128, 256), np.float32)
    rt2 = np.zeros((128, 256), np.float32)
    rt1[0:16] = R1
    rt1[64:80] = R1
    rt2[0:16] = R2
    rt2[64:80] = R2
    idx = np.arange(128)
    trib = (idx[:, None] <= idx[None, :]).astype(bf)
    dfc = idx[None, :] - idx[:, None]
    bandc = np.where((dfc >= 0) & (dfc <= 63), 0.0, -2e30).astype(np.float32)
    dfp = idx[None, :] + 128 - idx[:, None]
    bandp = np.where((dfp >= 0) & (dfp <= 63), 0.0, -2e30).astype(np.float32)
    base = {
        "wqp": np.ascontiguousarray(wqp.astype(bf)),
        "wkp": np.ascontiguousarray(wkp.astype(bf)),
        "wkt": np.ascontiguousarray(Wk.T.astype(bf)),
        "wvt": np.ascontiguousarray(Wv.T.astype(bf)),
        "bqp": pp(bqp), "bkp": pp(bkp),
        "bkr": np.ascontiguousarray(bk.reshape(1, 256)),
        "bvr": np.ascontiguousarray(bv.reshape(1, 1024)),
        "rt1": np.ascontiguousarray(rt1.astype(bf)),
        "rt2": np.ascontiguousarray(rt2.astype(bf)),
        "trib": trib, "bandc": bandc, "bandp": bandp,
    }
    allmask = np.full((128, 128), -2e30, np.float32)
    maps = []
    for c in cores:
        xnb = np.asarray(prev["xn3b_o"][c])
        if c % 4 == 0:
            halo = np.zeros((1024, 128), xnb.dtype)
            bp0 = allmask
        else:
            halo = np.asarray(prev["xn3b_o"][c - 1])[:, 1920:2048]
            bp0 = bandp
        maps.append(dict(base, xnbh=np.ascontiguousarray(
            np.concatenate([halo, xnb], axis=1)), bandp0=bp0))
    return maps


def build_A11b():
    lp = LP()
    g = lp.g
    xn3b_i = lp.dram_in("xn3b", [DM, TPC], BF16)
    wm_i = lp.dram_in("wmg", [128, 128], F32)
    w1_i = lp.dram_in("w1e", [8 * DM, 4096], BF16)
    w2_i = lp.dram_in("w2e", [8 * 4096, 1024], BF16)
    b1_i = lp.dram_in("b1p", [8 * 128, 32], F32)
    b2_i = lp.dram_in("b2p", [8 * 128, 8], F32)
    idf32_i = lp.dram_in("idf32i", [128, 128], F32)
    yffn_o, yffn_or = lp.dram_out("yffn_o", [DM, TPC], BF16)

    xb, xb_r = lp.sbuf("xb", [128, 8 * TPC], BF16)
    yacc, yacc_r = lp.sbuf("yacc", [128, 8 * TPC], BF16)
    h1t, h1t_r = lp.sbuf("h1t", [128, 32 * TT], BF16)
    w2b, w2b_r = lp.sbuf("w2b", [128, 32 * 1024], BF16)
    w1c = []
    w1c_r = []
    for s in range(2):
        t_, r_ = lp.sbuf(f"w1c{s}", [128, 8 * 512], BF16)
        w1c.append(t_)
        w1c_r.append(r_)
    wmt, wmt_r = lp.sbuf("wmt", [128, 128], F32)
    wmT, wmT_r = lp.sbuf("wmT", [8, TPC], F32)
    grow, grow_r = lp.sbuf("grow", [1, TPC], F32)
    geb, geb_r = lp.sbuf("geb", [128, TPC], BF16)
    b1s, b1s_r = lp.sbuf("b1s", [128, 8 * 32], F32)
    b2s, b2s_r = lp.sbuf("b2s", [128, 8 * 8], F32)
    idf32, idf32_r = lp.sbuf("idf32", [128, 128], F32)
    onb, onb_r = lp.sbuf("onb", [1, 128], F32)
    tmp5, tmp5_r = lp.sbuf("tmp5", [128, TT], F32)

    class Pool:
        def __init__(self):
            self.tiles = [lp.psum(f"pp{i}", [128, TT]) for i in range(6)]
            self.i = 0

        def get(self):
            t = self.tiles[self.i % 6]
            self.i += 1
            return t
    pool = Pool()

    def c0(h):
        h.memset(onb[:, :], 1.0)
        ins = None
        for q in range(4):
            ins = h.memset(yacc[:, q * 4096:(q + 1) * 4096], 0.0)
        return ins
    g.op("vector", c0, writes=(onb_r, yacc_r))

    ld_fm(lp, xb, xb_r, xn3b_i, 8, eng="sync")
    ld_plain(lp, wmt[:, :], wmt_r, wm_i, eng="sync")
    ld_plain(lp, idf32[:, :], idf32_r, idf32_i, eng="sync")
    b1r = b1_i.rearrange("(e p) f -> p e f", p=128)
    b2r = b2_i.rearrange("(e p) f -> p e f", p=128)

    def ldb1(h):
        return h.dma_start(out=b1s[:, :].rearrange("p (e f) -> p e f", e=8),
                           in_=b1r)
    g.dma("sync", ldb1, writes=(b1s_r,))

    def ldb2(h):
        return h.dma_start(out=b2s[:, :].rearrange("p (e f) -> p e f", e=8),
                           in_=b2r)
    g.dma("sync", ldb2, writes=(b2s_r,))

    for q4 in range(4):
        ps, psr = pool.get()

        def mmt(h, q4=q4, ps=ps):
            ins = None
            for j in range(4):
                tc = q4 * 4 + j
                ins = h.matmul(ps[0:8, j * 128:(j + 1) * 128],
                               wmt[:, tc * 8:(tc + 1) * 8], idf32[:, :],
                               start=True, stop=True)
            return ins
        g.op("tensor", mmt, reads=(wmt_r, idf32_r), writes=(psr,))

        def evt(h, q4=q4, ps=ps):
            return h.activation(wmT[0:8, q4 * 512:(q4 + 1) * 512],
                                ps[0:8, 0:512], AF.Copy)
        g.op("scalar", evt, reads=(psr,), writes=(wmT_r,))

    w1r = w1_i.rearrange("(e kc p) f -> e p kc f", e=8, p=128)
    w2r = w2_i.rearrange("(e fc p) d -> e p fc d", e=8, p=128)
    for e in range(8):
        def ldw2(h, e=e):
            return h.dma_start(
                out=w2b[:, :].rearrange("p (fc d) -> p fc d", fc=32),
                in_=w2r[e])
        g.dma("scalar", ldw2, reads=(w2b_r,), writes=(w2b_r,))

        for q4 in range(4):
            ps, psr = pool.get()

            def mmsel(h, e=e, q4=q4, ps=ps):
                return h.matmul(ps[0:1, 0:512], idf32[0:8, e:e + 1],
                                wmT[0:8, q4 * 512:(q4 + 1) * 512],
                                start=True, stop=True)
            g.op("tensor", mmsel, reads=(idf32_r, wmT_r), writes=(psr,))

            def evsel(h, q4=q4, ps=ps):
                return h.activation(grow[0:1, q4 * 512:(q4 + 1) * 512],
                                    ps[0:1, 0:512], AF.Copy)
            g.op("scalar", evsel, reads=(psr,), writes=(grow_r,))
        for q4 in range(4):
            ps, psr = pool.get()

            def mmbr(h, q4=q4, ps=ps):
                return h.matmul(ps[:, 0:512], onb[:, :],
                                grow[0:1, q4 * 512:(q4 + 1) * 512],
                                start=True, stop=True)
            g.op("tensor", mmbr, reads=(grow_r, onb_r), writes=(psr,))

            def evbr(h, q4=q4, ps=ps):
                return h.activation(geb[:, q4 * 512:(q4 + 1) * 512],
                                    ps[:, 0:512], AF.Copy)
            g.op("scalar", evbr, reads=(psr,), writes=(geb_r,))

        for tt in range(4):
            for fb in range(32):
                sl = (fb // 4) % 2
                if fb % 4 == 0:
                    def ldw1(h, e=e, fb=fb, sl=sl):
                        return h.dma_start(
                            out=w1c[sl][:, :].rearrange("p (kc f) -> p kc f",
                                                        kc=8),
                            in_=w1r[e, :, :, fb * 128:(fb + 4) * 128])
                    g.dma("sync", ldw1, reads=(w1c_r[sl],), writes=(w1c_r[sl],))

                ps, psr = pool.get()

                def mm1(h, tt=tt, fb=fb, sl=sl, ps=ps):
                    ins = None
                    for kc in range(8):
                        ins = h.matmul(
                            ps[:, 0:TT],
                            w1c[sl][:, kc * 512 + (fb % 4) * 128:
                                    kc * 512 + (fb % 4 + 1) * 128],
                            xb[:, kc * TPC + tt * TT: kc * TPC + (tt + 1) * TT],
                            start=(kc == 0), stop=(kc == 7))
                    return ins
                g.op("tensor", mm1, reads=(w1c_r[sl], xb_r), writes=(psr,))

                def ev1(h, e=e, fb=fb, ps=ps):
                    return h.activation(h1t[:, fb * TT:(fb + 1) * TT],
                                        ps[:, 0:TT], AF.Relu,
                                        bias=b1s[:, e * 32 + fb:
                                                 e * 32 + fb + 1])
                g.op("scalar", ev1, reads=(psr, b1s_r), writes=(h1t_r,))

            for dc in range(8):
                ps, psr = pool.get()

                def mm2(h, dc=dc, ps=ps):
                    ins = None
                    for fb in range(32):
                        ins = h.matmul(
                            ps[:, 0:TT],
                            w2b[:, fb * 1024 + dc * 128:
                                fb * 1024 + (dc + 1) * 128],
                            h1t[:, fb * TT:(fb + 1) * TT],
                            start=(fb == 0), stop=(fb == 31))
                    return ins
                g.op("tensor", mm2, reads=(w2b_r, h1t_r), writes=(psr,))

                def ev2(h, e=e, dc=dc, tt=tt, ps=ps):
                    return h.scalar_tensor_tensor(
                        tmp5[:, :], ps[:, 0:TT], b2s[:, e * 8 + dc:
                                                     e * 8 + dc + 1],
                        geb[:, tt * TT:(tt + 1) * TT],
                        op0=ALU.add, op1=ALU.mult)
                g.op("vector", ev2, reads=(psr, b2s_r, geb_r), writes=(tmp5_r,))

                def ev3(h, dc=dc, tt=tt):
                    return h.tensor_tensor(
                        yacc[:, dc * TPC + tt * TT: dc * TPC + (tt + 1) * TT],
                        yacc[:, dc * TPC + tt * TT: dc * TPC + (tt + 1) * TT],
                        tmp5[:, :], op=ALU.add)
                g.op("vector", ev3, reads=(yacc_r, tmp5_r), writes=(yacc_r,))

    st_fm(lp, yffn_o, yffn_or, yacc, 8, eng="gpsimd", reads=(yacc_r,))

    def fin(h):
        h.nop()
        return None
    g.op("vector", fin, reads=(yffn_or,))
    return lp


def host_A11b(inputs, cores, prev):
    import ml_dtypes
    bf = ml_dtypes.bfloat16
    W1 = np.asarray(inputs["L3_ffn_W1"], np.float32)
    W2 = np.asarray(inputs["L3_ffn_W2"], np.float32)
    b1 = np.asarray(inputs["L3_ffn_b1"], np.float32)
    b2 = np.asarray(inputs["L3_ffn_b2"], np.float32)
    b1p = np.stack([pp(b1[e]) for e in range(8)])
    b2p = np.stack([pp(b2[e]) for e in range(8)])
    base = {
        "w1e": np.ascontiguousarray(W1.astype(bf).reshape(8 * 1024, 4096)),
        "w2e": np.ascontiguousarray(W2.astype(bf).reshape(8 * 4096, 1024)),
        "b1p": np.ascontiguousarray(b1p.reshape(8 * 128, 32)),
        "b2p": np.ascontiguousarray(b2p.reshape(8 * 128, 8)),
        "idf32i": np.eye(128, dtype=np.float32),
    }
    return [dict(base, xn3b=prev["xn3b_o"][c], wmg=prev["wm_o"][c])
            for c in cores]


def build_A12():
    lp = LP()
    g = lp.g
    x3_i = lp.dram_in("x3d", [DM, TPC], F32)
    num_i = lp.dram_in("numd", [1024, TPC], F32)
    den_i = lp.dram_in("dend", [16, TPC], F32)
    yl_i = lp.dram_in("ylocd", [1024, TPC], BF16)
    pq_i = lp.dram_in("pqfd", [16 * 384, TPC], BF16)
    yf_i = lp.dram_in("yffnd", [DM, TPC], BF16)
    Ss_i = lp.dram_in("Sstk", [3 * 128, 3072], F32)
    zs_i = lp.dram_in("zstk", [3 * 128, 48], F32)
    out_o, out_or = lp.dram_out("out_o", [DM, TPC], F32)

    Spr, Spr_r = lp.sbuf("Spr", [128, 3072], F32)
    Sprb, Sprb_r = lp.sbuf("Sprb", [128, 3072], BF16)
    ldS, ldS_r = lp.sbuf("ldS", [128, 3072], F32)
    zpr, zpr_r = lp.sbuf("zpr", [128, 48], F32)
    zprb, zprb_r = lp.sbuf("zprb", [128, 48], BF16)
    ldz, ldz_r = lp.sbuf("ldz", [128, 48], F32)
    pqc, pqc_r = lp.sbuf("pqc", [128, 16 * 384], BF16)
    nmc, nmc_r = lp.sbuf("nmc", [128, 8 * 128], F32)
    denc, denc_r = lp.sbuf("denc", [1, 2048], F32)
    ylc, ylc_r = lp.sbuf("ylc", [128, 8 * 128], BF16)
    x3c, x3c_r = lp.sbuf("x3c", [128, 8 * 128], F32)
    yfc, yfc_r = lp.sbuf("yfc", [128, 8 * 128], BF16)
    oc, oc_r = lp.sbuf("oc", [128, 8 * 128], F32)
    ntb, ntb_r = lp.sbuf("ntb", [128, 128], F32)
    yvb, yvb_r = lp.sbuf("yvb", [128, 128], F32)
    rdv, rdv_r = lp.sbuf("rdv", [1, 128], F32)
    dtb, dtb_r = lp.sbuf("dtb", [1, 128], F32)
    onb, onb_r = lp.sbuf("onb", [1, 128], F32)

    class Pool:
        def __init__(self):
            self.tiles = [lp.psum(f"pp{i}", [128, TT]) for i in range(6)]
            self.i = 0

        def get(self):
            t = self.tiles[self.i % 6]
            self.i += 1
            return t
    pool = Pool()

    def c0(h):
        h.memset(onb[:, :], 1.0)
        h.memset(Spr[:, :], 0.0)
        return h.memset(zpr[:, :], 0.0)
    g.op("vector", c0, writes=(onb_r, Spr_r, zpr_r))

    Sr3 = Ss_i.rearrange("(s p) d -> s p d", p=128)
    zr3 = zs_i.rearrange("(s p) d -> s p d", p=128)
    for s in range(3):
        def ldSs(h, s=s):
            return h.dma_start(out=ldS[:, :], in_=Sr3[s])
        g.dma("sync", ldSs, reads=(ldS_r,), writes=(ldS_r,))

        def adS(h):
            ins = None
            for q in range(2):
                ins = h.tensor_tensor(Spr[:, q * 1536:(q + 1) * 1536],
                                      Spr[:, q * 1536:(q + 1) * 1536],
                                      ldS[:, q * 1536:(q + 1) * 1536],
                                      op=ALU.add)
            return ins
        g.op("vector", adS, reads=(Spr_r, ldS_r), writes=(Spr_r,))

        def ldzs(h, s=s):
            return h.dma_start(out=ldz[:, :], in_=zr3[s])
        g.dma("sync", ldzs, reads=(ldz_r,), writes=(ldz_r,))

        def adz(h):
            return h.tensor_tensor(zpr[:, :], zpr[:, :], ldz[:, :], op=ALU.add)
        g.op("vector", adz, reads=(zpr_r, ldz_r), writes=(zpr_r,))

    def cvS(h):
        h.tensor_copy(Sprb[:, :], Spr[:, :])
        return h.tensor_copy(zprb[:, :], zpr[:, :])
    g.op("vector", cvS, reads=(Spr_r, zpr_r), writes=(Sprb_r, zprb_r),
         fence=True)

    pqr2 = pq_i.rearrange("(hh X p) t -> p hh X t", hh=16, X=3)
    numr = num_i.rearrange("(kc p) t -> p kc t", p=128)
    ylr = yl_i.rearrange("(kc p) t -> p kc t", p=128)
    x3r = x3_i.rearrange("(kc p) t -> p kc t", p=128)
    yfr = yf_i.rearrange("(kc p) t -> p kc t", p=128)
    outr = out_o.rearrange("(kc p) t -> p kc t", p=128)

    for ch in range(16):
        def ldpq(h, ch=ch):
            return h.dma_start(
                out=pqc[:, :].rearrange("p (hh X t) -> p hh X t", hh=16, X=3),
                in_=pqr2[:, :, :, ch * 128:(ch + 1) * 128])
        g.dma("sync", ldpq, reads=(pqc_r,), writes=(pqc_r,))

        def ldnm(h, ch=ch):
            return h.dma_start(
                out=nmc[:, :].rearrange("p (kc t) -> p kc t", kc=8),
                in_=numr[:, :, ch * 128:(ch + 1) * 128])
        g.dma("sync", ldnm, reads=(nmc_r,), writes=(nmc_r,))

        def ldde(h, ch=ch):
            return h.dma_start(
                out=denc[:, :].rearrange("p (hh t) -> p hh t", hh=16),
                in_=den_i[:, ch * 128:(ch + 1) * 128].rearrange(
                    "hh (p t) -> p hh t", p=1))
        g.dma("sync", ldde, reads=(denc_r,), writes=(denc_r,))

        def ldyl(h, ch=ch):
            return h.dma_start(
                out=ylc[:, :].rearrange("p (kc t) -> p kc t", kc=8),
                in_=ylr[:, :, ch * 128:(ch + 1) * 128])
        g.dma("gpsimd", ldyl, reads=(ylc_r,), writes=(ylc_r,))

        def ldx3(h, ch=ch):
            return h.dma_start(
                out=x3c[:, :].rearrange("p (kc t) -> p kc t", kc=8),
                in_=x3r[:, :, ch * 128:(ch + 1) * 128])
        g.dma("gpsimd", ldx3, reads=(x3c_r,), writes=(x3c_r,))

        def ldyf(h, ch=ch):
            return h.dma_start(
                out=yfc[:, :].rearrange("p (kc t) -> p kc t", kc=8),
                in_=yfr[:, :, ch * 128:(ch + 1) * 128])
        g.dma("gpsimd", ldyf, reads=(yfc_r,), writes=(yfc_r,))

        for hd in range(16):
            b_ = (hd % 2) * 64
            kcs = (hd // 2) * 128
            psN, psN_r = pool.get()
            psD, psD_r = pool.get()

            def mmc(h, hd=hd, b_=b_, psN=psN, psD=psD):
                for X in range(3):
                    h.matmul(psN[b_:b_ + 64, 0:128],
                             Sprb[:, (hd * 3 + X) * 64:(hd * 3 + X + 1) * 64],
                             pqc[:, hd * 384 + X * 128: hd * 384 + (X + 1) * 128],
                             start=(X == 0), stop=(X == 2))
                ins = None
                for X in range(3):
                    ins = h.matmul(
                        psD[0:1, 0:128],
                        zprb[:, hd * 3 + X: hd * 3 + X + 1],
                        pqc[:, hd * 384 + X * 128: hd * 384 + (X + 1) * 128],
                        start=(X == 0), stop=(X == 2))
                return ins
            g.op("tensor", mmc, reads=(Sprb_r, zprb_r, pqc_r),
                 writes=(psN_r, psD_r))

            def mkdt(h, hd=hd, psD=psD):
                return h.scalar_tensor_tensor(
                    dtb[:, :], psD[0:1, 0:128], 1e-8,
                    denc[0:1, hd * 128:(hd + 1) * 128],
                    op0=ALU.add, op1=ALU.add)
            g.op("vector", mkdt, reads=(psD_r, denc_r), writes=(dtb_r,),
                 fence=True)

            def rcd(h):
                return h.reciprocal(rdv[:, :], dtb[:, :])
            g.op("vector", rcd, reads=(dtb_r,), writes=(rdv_r,), fence=True)

            def mknt(h, b_=b_, kcs=kcs, psN=psN):
                return h.tensor_tensor(
                    ntb[b_:b_ + 64, :], psN[b_:b_ + 64, 0:128],
                    nmc[b_:b_ + 64, kcs:kcs + 128], op=ALU.add)
            g.op("vector", mknt, reads=(psN_r, nmc_r), writes=(ntb_r,))

            psB, psB_r = pool.get()

            def mmrb(h, b_=b_, psB=psB):
                return h.matmul(psB[b_:b_ + 64, 0:128], onb[0:1, 0:64],
                                rdv[0:1, :], start=True, stop=True)
            g.op("tensor", mmrb, reads=(onb_r, rdv_r), writes=(psB_r,))

            def mkyv(h, b_=b_, psB=psB):
                return h.tensor_tensor(yvb[b_:b_ + 64, :], ntb[b_:b_ + 64, :],
                                       psB[b_:b_ + 64, 0:128], op=ALU.mult)
            g.op("vector", mkyv, reads=(ntb_r, psB_r), writes=(yvb_r,))

            def mkoc(h, b_=b_, kcs=kcs):
                return h.tensor_tensor(
                    oc[b_:b_ + 64, kcs:kcs + 128], yvb[b_:b_ + 64, :],
                    ylc[b_:b_ + 64, kcs:kcs + 128], op=ALU.add)
            g.op("vector", mkoc, reads=(yvb_r, ylc_r), writes=(oc_r,))

        def adx3(h):
            return h.tensor_tensor(oc[:, :], oc[:, :], x3c[:, :], op=ALU.add)
        g.op("vector", adx3, reads=(oc_r, x3c_r), writes=(oc_r,))

        def adyf(h):
            return h.tensor_tensor(oc[:, :], oc[:, :], yfc[:, :], op=ALU.add)
        g.op("vector", adyf, reads=(oc_r, yfc_r), writes=(oc_r,))

        def stoc(h, ch=ch):
            return h.dma_start(out=outr[:, :, ch * 128:(ch + 1) * 128],
                               in_=oc[:, :].rearrange("p (kc t) -> p kc t",
                                                      kc=8))
        g.dma("gpsimd", stoc, reads=(oc_r,), writes=(out_or,))

    def fin(h):
        h.nop()
        return None
    g.op("vector", fin, reads=(out_or,))
    return lp


def host_A12(inputs, cores, prev):
    maps = []
    for c in cores:
        b0 = (c // 4) * 4
        k = c % 4
        Sstk = np.zeros((3, 128, 3072), np.float32)
        zstk = np.zeros((3, 128, 48), np.float32)
        for i, pc in enumerate(range(b0, c)):
            Sstk[i] = prev["S_o"][pc]
            zstk[i] = prev["z_o"][pc]
        maps.append({
            "x3d": prev["x3_o"][c], "numd": prev["num_o"][c],
            "dend": prev["den_o"][c], "ylocd": prev["yloc_o"][c],
            "pqfd": prev["pqf_o"][c], "yffnd": prev["yffn_o"][c],
            "Sstk": np.ascontiguousarray(Sstk.reshape(384, 3072)),
            "zstk": np.ascontiguousarray(zstk.reshape(384, 48)),
        })
    return maps


def kernel(**inputs):
    from concourse.bass_utils import run_bass_kernel_spmd
    cores = list(range(8))

    def gather(res):
        return {k: [res.results[c][k] for c in cores] for k in res.results[0]}

    prev = {}
    for build, host in ((build_A1, host_A1), (build_A2a, host_A2a),
                        (build_A2b, host_A2b), (build_A3, host_A3)):
        nc = build().finish()
        prev.update(gather(run_bass_kernel_spmd(nc, host(inputs, cores, prev),
                                                core_ids=cores)))

    y = np.concatenate([np.asarray(prev["xn1_o"][c], np.float32).T
                        for c in cores], 0)
    ncA = build_AFa().finish()
    ncF = build_AFf().finish()
    for d in range(6):
        res = run_bass_kernel_spmd(ncA, host_AFa(inputs, cores, {}, d, y),
                                   core_ids=cores)
        y = np.concatenate([np.asarray(res.results[c]["x2_o"], np.float32).T
                            for c in cores], 0)
        res = run_bass_kernel_spmd(ncF, host_AFf(inputs, cores, {}, d, y),
                                   core_ids=cores)
        y = np.concatenate([np.asarray(res.results[c]["y_o"], np.float32).T
                            for c in cores], 0)
    y6f = fm(y)
    prev["y6_o"] = [shard(y6f, c) for c in cores]

    for build, host in ((build_A9a, host_A9a), (build_A9b, host_A9b),
                        (build_A10, host_A10), (build_A11a, host_A11a),
                        (build_A11b, host_A11b), (build_A12, host_A12)):
        nc = build().finish()
        prev.update(gather(run_bass_kernel_spmd(nc, host(inputs, cores, prev),
                                                core_ids=cores)))

    out = np.concatenate([np.asarray(prev["out_o"][c], np.float32).T
                          for c in cores], 0)
    return np.ascontiguousarray(out.reshape(2, 8192, 1024), dtype=np.float32)
